# version 04
# baseline (speedup 1.0000x reference)
import contextlib
import numpy as np
import concourse.bass as bass
import concourse.mybir as mybir
from concourse.bass_utils import run_bass_kernel_spmd

F32 = mybir.dt.float32
BF16 = mybir.dt.bfloat16
I32 = mybir.dt.int32
AF = mybir.ActivationFunctionType
ALU = mybir.AluOpType

D = 2048
KD = D // 128
NH = 2
HD = 2048
NST = 16
RNK = 128
IMIX = 4096
IFFN = 8192
IFF1 = 5504
KF1 = IFF1 // 128
NHEAD = 16
EPS = 1e-5
ISQ = float(1.0 / np.sqrt(128.0))
NEG = -1.0e30


class Prog:
    def __init__(self, S=4096, TC=512, debug_out=()):
        self.S, self.TC = S, TC
        self.NTC = S // TC
        self.debug_out = set(debug_out)
        self.nc = bass.Bass("TRN2", target_bir_lowering=False, detect_race_conditions=False)
        self.es = contextlib.ExitStack()
        self.es.__enter__()
        self.c = {k: 0 for k in ("LD0", "LD1", "LG0", "LG1", "LGW",
                                 "ST0", "ST1", "ST2", "ST3", "MM", "ES", "EV", "EG", "VS")}
        self.sem = {k: self.es.enter_context(self.nc.semaphore(k)) for k in self.c}
        self.uid = 0

    def nm(self, p):
        self.uid += 1
        return f"{p}{self.uid}"

    def inc(self, instr, name, k=1):
        instr.then_inc(self.sem[name], k)
        self.c[name] += k
        return self.c[name]

    def snap(self):
        return dict(self.c)

    def bwait(self, eng, snap, skip=()):
        for k, v in snap.items():
            if k not in skip and v > 0 and k != "VS":
                eng.wait_ge(self.sem[k], v)

    def vsync(self, eng, instr):
        self.inc(instr, "VS")
        eng.wait_ge(self.sem["VS"], self.c["VS"])

    def dram(self, name, shape, dt):
        kind = "ExternalOutput" if name in self.debug_out else "Internal"
        return self.nc.dram_tensor(name, shape, dt, kind=kind).ap()

    def sb(self, name, shape, dt):
        st = getattr(self, "ph_es", None) or self.es
        return st.enter_context(self.nc.sbuf_tensor(self.nm(name), shape, dt))

    def build(self):
        nc, es, S, TC, NTC = self.nc, self.es, self.S, self.TC, self.NTC
        NT128 = S // 128
        p = self

        EI = lambda n, sh, dt: nc.dram_tensor(n, sh, dt, kind="ExternalInput").ap()
        xin = EI("xin", [KD, 128, S], F32)
        w_inx = EI("w_inx", [D, IMIX], F32)
        w_ing = EI("w_ing", [D, IMIX], F32)
        convw = EI("convw", [128, 32 * 4], F32)
        convb = EI("convb", [128, 32], F32)
        w_xp = EI("w_xp", [128, NH * KD * 160], F32)
        w_dt = EI("w_dt", [128, NH * HD], F32)
        dtbi = EI("dtbi", [128, NH * 16], F32)
        alog = EI("alog", [128, NH * 16 * NST], F32)
        dpar = EI("dpar", [128, NH * 16], F32)
        w_out = EI("w_out", [IMIX, D], F32)
        w_fg = EI("w_fg", [D, IFFN], I32)
        w_fu = EI("w_fu", [D, IFFN], I32)
        w_fd = EI("w_fd", [IFFN, D], I32)
        sc_g = EI("sc_g", [128, 64], F32)
        sc_u = EI("sc_u", [128, 64], F32)
        sc_d = EI("sc_d", [128, 16], F32)
        w_qkv = EI("w_qkv", [D, 3 * D], F32)
        w_pr = EI("w_pr", [D, D], F32)
        lutv = EI("lutv", [1, 16], F32)
        i_g = EI("i_g", [D, IFF1], I32)
        i_u = EI("i_u", [D, IFF1], I32)
        i_d = EI("i_d", [IFF1, D], I32)
        identf = EI("identf", [128, 128], F32)
        self32 = EI("self32", [32, 32 * 128], F32)
        out = nc.dram_tensor("out", [KD, 128, S], F32, kind="ExternalOutput").ap()

        xresA = p.dram("xresA", [KD, 128, S], BF16)
        xresB = p.dram("xresB", [KD, 128, S], BF16)
        lnx = p.dram("lnx", [KD, 128, S], BF16)
        xssm = p.dram("xssm", [32, 128, S], BF16)
        gsil = p.dram("gsil", [32, 128, S], BF16)
        ygat = p.dram("ygat", [32, 128, S], BF16)
        z32 = p.dram("z32", [KD, 128, S], F32)
        h1 = p.dram("h1", [64, 128, S], BF16)
        qf = p.dram("qf", [NHEAD, 128, S], BF16)
        kfd = p.dram("kfd", [NHEAD, 128, S], BF16)
        vhm = p.dram("vhm", [NHEAD, NT128, 128, 128], BF16)
        ao = p.dram("ao", [KD, 128, S], BF16)
        h2 = p.dram("h2", [KF1, 128, S], BF16)

        ident = p.sb("ident", [128, 128], BF16)
        ones1 = p.sb("ones1", [1, 128], BF16)
        onesc = p.sb("onesc", [128, 1], BF16)
        sel32 = p.sb("sel32", [32, 32 * 128], BF16)
        convw_s = p.sb("convw_s", [128, 32 * 4], F32)
        convb_s = p.sb("convb_s", [128, 32], F32)
        dtb_s = p.sb("dtb_s", [128, NH * 16], F32)
        a_s = p.sb("a_s", [128, NH * 16 * NST], F32)
        d_s = p.sb("d_s", [128, NH * 16], F32)
        scg_s = p.sb("scg_s", [128, 64], F32)
        scu_s = p.sb("scu_s", [128, 64], F32)
        scd_s = p.sb("scd_s", [128, 16], F32)
        lut_s = p.sb("lut_s", [1, 16], F32)
        lutbf = p.sb("lutbf", [1, 16], BF16)
        lutb_s = p.sb("lutb_s", [128, 16], F32)
        carry = p.sb("carry", [128, NH * 16 * NST], F32)
        w_xp_s = p.sb("w_xp_s", [128, NH * KD * 160], BF16)
        w_dt_s = p.sb("w_dt_s", [128, NH * 16 * 128], BF16)
        ps = [es.enter_context(nc.psum_tensor(f"ps{i}", [128, 512], F32)) for i in range(8)]

        block = es.enter_context(nc.Block())

        def p_init():
            s0 = p.snap()

            @block.gpsimd
            def _(g):
                p.inc(g.dma_start(out=ident[:], in_=identf), "LGW", 16)
                p.inc(g.dma_start(out=sel32[:], in_=self32), "LGW", 16)
                p.inc(g.dma_start(out=w_xp_s[:], in_=w_xp), "LGW", 16)
                p.inc(g.dma_start(out=w_dt_s[:], in_=w_dt), "LGW", 16)

            @block.sync
            def _(s):
                p.inc(s.dma_start(out=convw_s[:], in_=convw), "LD0", 16)
                p.inc(s.dma_start(out=convb_s[:], in_=convb), "LD0", 16)
                p.inc(s.dma_start(out=dtb_s[:], in_=dtbi), "LD0", 16)
                p.inc(s.dma_start(out=a_s[:], in_=alog), "LD0", 16)
                p.inc(s.dma_start(out=d_s[:], in_=dpar), "LD0", 16)
                p.inc(s.dma_start(out=scg_s[:], in_=sc_g), "LD0", 16)
                p.inc(s.dma_start(out=scu_s[:], in_=sc_u), "LD0", 16)
                p.inc(s.dma_start(out=scd_s[:], in_=sc_d), "LD0", 16)
                p.inc(s.dma_start(out=lut_s[:], in_=lutv), "LD0", 16)

            ld1, lg1 = p.c["LD0"], p.c["LGW"]

            @block.scalar
            def _(sc):
                sc.wait_ge(p.sem["LD0"], ld1)
                sc.wait_ge(p.sem["LGW"], lg1)
                p.inc(sc.activation(out=a_s[:], in_=a_s[:], func=AF.Exp), "ES")

            es1 = p.c["ES"]

            @block.vector
            def _(v):
                v.wait_ge(p.sem["ES"], es1)
                v.wait_ge(p.sem["LD0"], ld1)
                v.tensor_scalar_mul(a_s[:], a_s[:], -1.0)
                v.memset(carry[:], 0.0)
                v.memset(ones1[:], 1.0)
                v.tensor_copy(out=lutbf[:], in_=lut_s[:])
                p.inc(v.memset(onesc[:], 1.0), "EV")

            ev1 = p.c["EV"]

            @block.tensor
            def _(t):
                t.wait_ge(p.sem["EV"], ev1)
                p.inc(t.matmul(ps[3][:, 0:16], ones1[:], lutbf[:], start=True, stop=True), "MM")

            mm1 = p.c["MM"]

            @block.vector
            def _(v):
                v.wait_ge(p.sem["MM"], mm1)
                p.inc(v.tensor_copy(out=lutb_s[:], in_=ps[3][:, 0:16]), "EV")

        def p_lnlike(src, src_f32, mode, xold=None, dst=None, write_xres=None):
            s0 = p.snap()
            KT = KD * TC
            final_f32 = (mode == "post" and dst is out)
            xb = [p.sb("lnxb", [128, KT], BF16) for _ in range(2)]
            xsq = p.sb("lnsq", [128, KT], BF16)
            onx = p.sb("lnonx", [128, KT], F32 if final_f32 else BF16)
            xob = [p.sb("lnxo", [128, KT], BF16) for _ in range(2)] if mode == "post" else None
            stat = p.sb("lnst", [1, 3 * TC], F32)
            coef = p.sb("lnco", [1, 2 * TC], BF16)
            abc = p.sb("lnab", [128, TC], BF16)
            bbc = p.sb("lnbb", [128, TC], BF16)

            mm0, es0, ev0 = s0["MM"], s0["ES"], s0["EV"]
            r_es, r_mm, r_ev = 3, 2, 3
            LQ = "LG" if src_f32 else "LD"
            ld_done = {}
            st_done = {}
            n_st = 2 if (mode == "pre" and write_xres is not None) else 1

            def loader(q, tc):
                if tc >= 2:
                    q.wait_ge(p.sem["EV"], ev0 + (tc - 2) * r_ev + r_ev)
                    pt = tc - 2
                    sm = "ST" + str(pt % 2)
                    q.wait_ge(p.sem[sm], s0[sm] + 16 * n_st * (pt // 2 + 1))
                sem = LQ + str(tc % 2)
                p.inc(q.dma_start(
                    out=xb[tc % 2][:].rearrange("p (k t) -> p k t", k=KD),
                    in_=src[:, :, tc * TC:(tc + 1) * TC].rearrange("k p t -> p k t")),
                    sem, 16)
                if mode == "post":
                    p.inc(q.dma_start(
                        out=xob[tc % 2][:].rearrange("p (k t) -> p k t", k=KD),
                        in_=xold[:, :, tc * TC:(tc + 1) * TC].rearrange("k p t -> p k t")),
                        sem, 16)
                ld_done[tc] = (sem, p.c[sem])

            if src_f32:
                @block.gpsimd
                def _(g):
                    p.bwait(g, s0, skip=("LG0", "LG1"))
                    for tc in range(NTC):
                        loader(g, tc)

            @block.sync
            def _(s):
                p.bwait(s, s0, skip=("LD0", "LD1", "ST0", "ST1"))
                for tc in range(NTC):
                    if not src_f32:
                        loader(s, tc)
                    s.wait_ge(p.sem["EV"], ev0 + tc * r_ev + 3)
                    sem = "ST" + str(tc % 2)
                    if mode == "pre":
                        p.inc(s.dma_start(
                            out=lnx[:, :, tc * TC:(tc + 1) * TC].rearrange("k p t -> p k t"),
                            in_=onx[:].rearrange("p (k t) -> p k t", k=KD)), sem, 16)
                        if write_xres is not None:
                            p.inc(s.dma_start(
                                out=write_xres[:, :, tc * TC:(tc + 1) * TC].rearrange("k p t -> p k t"),
                                in_=xb[tc % 2][:].rearrange("p (k t) -> p k t", k=KD)), sem, 16)
                    else:
                        p.inc(s.dma_start(
                            out=dst[:, :, tc * TC:(tc + 1) * TC].rearrange("k p t -> p k t"),
                            in_=onx[:].rearrange("p (k t) -> p k t", k=KD)), sem, 16)
                    st_done[tc] = (sem, p.c[sem])

            @block.scalar
            def _(sc):
                p.bwait(sc, s0, skip=("ES",))
                for tc in range(NTC):
                    sm, v = ld_done[tc]
                    sc.wait_ge(p.sem[sm], v)
                    p.inc(sc.activation(out=xsq[:], in_=xb[tc % 2][:], func=AF.Square), "ES")
                    sc.wait_ge(p.sem["EV"], ev0 + tc * r_ev + 1)
                    p.inc(sc.activation(out=stat[0:1, 2 * TC:3 * TC], in_=stat[0:1, TC:2 * TC],
                                        func=AF.Ln), "ES")
                    p.inc(sc.activation(out=stat[0:1, TC:2 * TC], in_=stat[0:1, 2 * TC:3 * TC],
                                        func=AF.Exp, scale=-0.5), "ES")

            @block.tensor
            def _(t):
                p.bwait(t, s0, skip=("MM",))
                for tc in range(NTC):
                    t.wait_ge(p.sem["ES"], es0 + tc * r_es + 1)
                    for k in range(KD):
                        t.matmul(ps[6][0:1, 0:TC], onesc[:], xb[tc % 2][:, k * TC:(k + 1) * TC],
                                 start=(k == 0), stop=(k == KD - 1))
                    for k in range(KD):
                        mi = t.matmul(ps[7][0:1, 0:TC], onesc[:], xsq[:, k * TC:(k + 1) * TC],
                                      start=(k == 0), stop=(k == KD - 1))
                    p.inc(mi, "MM")
                    t.wait_ge(p.sem["EV"], ev0 + tc * r_ev + 2)
                    t.matmul(ps[4][:, 0:TC], ones1[:], coef[0:1, 0:TC], start=True, stop=True)
                    mi = t.matmul(ps[5][:, 0:TC], ones1[:], coef[0:1, TC:2 * TC], start=True, stop=True)
                    p.inc(mi, "MM")

            @block.vector
            def _(v):
                p.bwait(v, s0, skip=("EV",))
                for tc in range(NTC):
                    v.wait_ge(p.sem["MM"], mm0 + tc * r_mm + 1)
                    v.tensor_scalar(out=stat[0:1, 0:TC], in0=ps[6][0:1, 0:TC],
                                    scalar1=1.0 / D, scalar2=None, op0=ALU.mult)
                    p.vsync(v, v.tensor_scalar(out=stat[0:1, TC:2 * TC], in0=ps[7][0:1, 0:TC],
                                               scalar1=1.0 / D, scalar2=float(EPS),
                                               op0=ALU.mult, op1=ALU.add))
                    v.tensor_tensor(out=stat[0:1, 2 * TC:3 * TC], in0=stat[0:1, 0:TC],
                                    in1=stat[0:1, 0:TC], op=ALU.mult)
                    p.inc(v.tensor_tensor(out=stat[0:1, TC:2 * TC], in0=stat[0:1, TC:2 * TC],
                                          in1=stat[0:1, 2 * TC:3 * TC], op=ALU.subtract), "EV")
                    v.wait_ge(p.sem["ES"], es0 + tc * r_es + 3)
                    v.tensor_copy(out=coef[0:1, 0:TC], in_=stat[0:1, TC:2 * TC])
                    v.tensor_tensor(out=stat[0:1, 2 * TC:3 * TC], in0=stat[0:1, 0:TC],
                                    in1=stat[0:1, TC:2 * TC], op=ALU.mult)
                    p.inc(v.tensor_scalar_mul(coef[0:1, TC:2 * TC], stat[0:1, 2 * TC:3 * TC], -1.0), "EV")
                    v.wait_ge(p.sem["MM"], mm0 + tc * r_mm + 2)
                    if tc >= 1:
                        _sm, _val = st_done[tc - 1]
                        v.wait_ge(p.sem[_sm], _val)
                    v.tensor_copy(out=abc[:], in_=ps[4][:, 0:TC])
                    p.vsync(v, v.tensor_copy(out=bbc[:], in_=ps[5][:, 0:TC]))
                    xv = xb[tc % 2][:].rearrange("p (k t) -> p k t", k=KD)
                    ov = onx[:].rearrange("p (k t) -> p k t", k=KD)
                    v.tensor_tensor(out=ov, in0=xv,
                                    in1=abc[:].unsqueeze(1).broadcast_to((128, KD, TC)), op=ALU.mult)
                    ii = v.tensor_tensor(out=ov, in0=ov,
                                         in1=bbc[:].unsqueeze(1).broadcast_to((128, KD, TC)), op=ALU.add)
                    if mode == "post":
                        ii = v.tensor_tensor(out=ov, in0=ov,
                                             in1=xob[tc % 2][:].rearrange("p (k t) -> p k t", k=KD),
                                             op=ALU.add)
                    p.inc(ii, "EV")

        def p_gemm(w, wkind, Kfeat, groups, rhs, evac, store_ap, out_f32=False, NPS=4, nrb=2):
            KC = Kfeat // 128
            s0 = p.snap()
            maxw = max(hi - lo for lo, hi in groups)
            wg = p.sb("gmw", [128, KC * maxw * 128], BF16)
            rb = [p.sb("gmr", [128, KC * TC], BF16) for _ in range(nrb)]
            ot = [p.sb("gmo", [128, TC], F32 if out_f32 else BF16) for _ in range(4)]
            mm0, es0, ev0, lgw0 = s0["MM"], s0["ES"], s0["EV"], s0["LGW"]
            NG = len(groups)
            units = []
            for g, (lo, hi) in enumerate(groups):
                for tc in range(NTC):
                    for j in range(hi - lo):
                        units.append((g, tc, j, lo + j))
            mm_of = {}
            last_of_g = {}
            n = mm0
            for g, (lo, hi) in enumerate(groups):
                for tc in range(NTC):
                    n += (hi - lo)
                    mm_of[(g, tc)] = n
                last_of_g[g] = n
            ld_rhs = {}
            st_done = {}

            nslab = {g: (groups[g][1] - groups[g][0]) for g in range(len(groups))}
            ibuf = [p.sb("gmib", [128, KC * 128], I32)] if wkind == "lut" else None
            acf = p.sb("gmac", [128, KC * 128], BF16) if wkind == "lut" else None
            tmf = p.sb("gmtm", [128, KC * 128], BF16) if wkind == "lut" else None
            slab_ld = {}
            nw_ev = {}
            slab_ev = {}
            if wkind == "lut":
                si = 0
                for g, (lo, hi) in enumerate(groups):
                    for j in range(hi - lo):
                        slab_ev[si] = ev0 + si + 1
                        si += 1

            @block.gpsimd
            def _(g_):
                p.bwait(g_, s0, skip=("LGW", "LG0", "LG1"))
                si = 0
                for g, (lo, hi) in enumerate(groups):
                    if g >= 1:
                        g_.wait_ge(p.sem["MM"], last_of_g[g - 1])
                        if wkind == "i32":
                            g_.wait_ge(p.sem["EV"], ev0 + g)
                    if wkind == "lut":
                        for j in range(hi - lo):
                            if si >= 1:
                                g_.wait_ge(p.sem["EV"], slab_ev[si - 1])
                            sem = "LG" + str(si % 2)
                            p.inc(g_.dma_start(
                                out=ibuf[0][:].rearrange("p (k m) -> p k m", k=KC),
                                in_=w.rearrange("(k q) m -> q k m", q=128)[:, :, (lo + j) * 128:(lo + j + 1) * 128]),
                                sem, 16)
                            slab_ld[si] = (sem, p.c[sem])
                            si += 1
                    else:
                        p.inc(g_.dma_start(
                            out=wg[:, 0:KC * (hi - lo) * 128].rearrange("p (k m) -> p k m", k=KC),
                            in_=w.rearrange("(k q) m -> q k m", q=128)[:, :, lo * 128:hi * 128]), "LGW", 16)

            if wkind == "i32":
                @block.vector
                def _(v):
                    p.bwait(v, s0, skip=("EV",))
                    for g, (lo, hi) in enumerate(groups):
                        v.wait_ge(p.sem["LGW"], lgw0 + 16 * (g + 1))
                        if g >= 1:
                            v.wait_ge(p.sem["MM"], last_of_g[g - 1])
                        p.inc(v.tensor_scalar_add(wg[:, 0:KC * (hi - lo) * 128],
                                                  wg[:, 0:KC * (hi - lo) * 128], -128.0), "EV")
                        nw_ev[g] = p.c["EV"]
            elif wkind == "lut":
                @block.vector
                def _(v):
                    p.bwait(v, s0, skip=("EV",))
                    si = 0
                    for g, (lo, hi) in enumerate(groups):
                        if g >= 1:
                            v.wait_ge(p.sem["MM"], last_of_g[g - 1])
                        for j in range(hi - lo):
                            sm, val = slab_ld[si]
                            v.wait_ge(p.sem[sm], val)
                            wsl = KC * 128
                            for vv in range(16):
                                dstv = acf[:] if vv == 0 else tmf[:]
                                v.tensor_scalar(out=dstv, in0=ibuf[0][:], scalar1=float(vv),
                                                scalar2=lutb_s[:, vv:vv + 1], op0=ALU.is_equal, op1=ALU.mult)
                                if vv > 0:
                                    v.tensor_tensor(out=acf[:], in0=acf[:], in1=tmf[:], op=ALU.add)
                            p.inc(v.tensor_copy(
                                out=wg[:, 0:KC * (hi - lo) * 128]
                                .rearrange("p (k m) -> p k m", k=KC)[:, :, j * 128:(j + 1) * 128],
                                in_=acf[:].rearrange("p (k m) -> p k m", k=KC)), "EV")
                            si += 1
                        nw_ev[g] = p.c["EV"]

            @block.sync
            def _(s):
                p.bwait(s, s0, skip=("LD0", "LD1", "ST0", "ST1", "ST2", "ST3"))
                i = 0
                for g, (lo, hi) in enumerate(groups):
                    for tc in range(NTC):
                        u = g * NTC + tc
                        if u >= nrb:
                            pg, ptc = divmod(u - nrb, NTC)
                            s.wait_ge(p.sem["MM"], mm_of[(pg, ptc)])
                        sem = "LD" + str(u % 2)
                        p.inc(s.dma_start(
                            out=rb[u % nrb][:].rearrange("p (k t) -> p k t", k=KC),
                            in_=rhs[:, :, tc * TC:(tc + 1) * TC].rearrange("k p t -> p k t")), sem, 16)
                        ld_rhs[(g, tc)] = (sem, p.c[sem])
                        for j in range(hi - lo):
                            s.wait_ge(p.sem["ES"], es0 + i + 1)
                            ssem = "ST" + str(i % 4)
                            p.inc(s.dma_start(out=store_ap(lo + j, tc), in_=ot[i % 4][:]), ssem, 16)
                            st_done[i] = (ssem, p.c[ssem])
                            i += 1

            @block.tensor
            def _(t):
                p.bwait(t, s0, skip=("MM",))
                i = 0
                for g, (lo, hi) in enumerate(groups):
                    if wkind == "f32":
                        t.wait_ge(p.sem["LGW"], lgw0 + 16 * (g + 1))
                    elif wkind == "i32":
                        t.wait_ge(p.sem["LGW"], lgw0 + 16 * (g + 1))
                        t.wait_ge(p.sem["EV"], nw_ev[g])
                    else:
                        t.wait_ge(p.sem["EV"], nw_ev[g])
                    for tc in range(NTC):
                        u = g * NTC + tc
                        sm, v = ld_rhs[(g, tc)]
                        t.wait_ge(p.sem[sm], v)
                        for j in range(hi - lo):
                            if i >= NPS:
                                t.wait_ge(p.sem["ES"], es0 + i - NPS + 1)
                            for k in range(KC):
                                mi = t.matmul(
                                    ps[i % NPS][:, 0:TC],
                                    wg[:, (k * (hi - lo) + j) * 128:(k * (hi - lo) + j + 1) * 128],
                                    rb[u % nrb][:, k * TC:(k + 1) * TC],
                                    start=(k == 0), stop=(k == KC - 1))
                            p.inc(mi, "MM")
                            i += 1

            @block.scalar
            def _(sc):
                p.bwait(sc, s0, skip=("ES",))
                for i, (g, tc, j, m) in enumerate(units):
                    sc.wait_ge(p.sem["MM"], mm0 + i + 1)
                    if i >= 4:
                        sm, v = st_done[i - 4]
                        sc.wait_ge(p.sem[sm], v)
                    p.inc(evac(m, ps[i % NPS][:, 0:TC], ot[i % 4][:], sc), "ES")

        def p_gemm_dual(wg_dram, wu_dram, wkind, Kfeat, groups, rhs, hdst, scg=None, scu=None):
            KC = Kfeat // 128
            s0 = p.snap()
            maxw = max(hi - lo for lo, hi in groups)
            wga = p.sb("duwa", [128, KC * maxw * 128], BF16)
            wgb = p.sb("duwb", [128, KC * maxw * 128], BF16)
            rb = [p.sb("durb", [128, KC * TC], BF16) for _ in range(2)]
            gt = [p.sb("dugt", [128, TC], BF16) for _ in range(2)]
            ut = p.sb("duut", [128, TC], BF16)
            ot = [p.sb("duot", [128, TC], BF16) for _ in range(4)]
            ibuf = [p.sb("duib", [128, KC * 128], I32) for _ in range(2)] if wkind == "lut" else None
            acf = p.sb("duac", [128, KC * 128], BF16) if wkind == "lut" else None
            tmf = p.sb("dutm", [128, KC * 128], BF16) if wkind == "lut" else None
            mm0, es0, ev0, lgw0 = s0["MM"], s0["ES"], s0["EV"], s0["LGW"]
            NG = len(groups)
            units = []
            for g, (lo, hi) in enumerate(groups):
                for tc in range(NTC):
                    for j in range(hi - lo):
                        units.append((g, tc, j, lo + j))
            mm_of, last_of_g = {}, {}
            n = mm0
            for g, (lo, hi) in enumerate(groups):
                for tc in range(NTC):
                    n += (hi - lo)
                    mm_of[(g, tc)] = n
                last_of_g[g] = n
            nw_ev = {}
            slab_ev = {}
            if wkind == "lut":
                si = 0
                for g, (lo, hi) in enumerate(groups):
                    si += 2 * (hi - lo)
                    nw_ev[g] = ev0 + si
            ev_unit = {}
            n = ev0
            si = 0
            for g, (lo, hi) in enumerate(groups):
                if wkind == "lut":
                    n += 2 * (hi - lo)
                else:
                    n += 2
                nw_ev[g] = n
                for tc in range(NTC):
                    for j in range(hi - lo):
                        n += 1
                        ev_unit[si] = n
                        si += 1
            ld_rhs, st_done, slab_ld = {}, {}, {}
            dq_ev = {}
            if wkind == "lut":
                n = ev0
                si = 0
                for g, (lo, hi) in enumerate(groups):
                    for j in range(2 * (hi - lo)):
                        n += 1
                        dq_ev[si] = n
                        si += 1
                    n += NTC * (hi - lo)

            @block.gpsimd
            def _(g_):
                p.bwait(g_, s0, skip=("LGW", "LG0", "LG1"))
                si = 0
                for g, (lo, hi) in enumerate(groups):
                    if g >= 1:
                        g_.wait_ge(p.sem["MM"], last_of_g[g - 1])
                        g_.wait_ge(p.sem["EV"], nw_ev[g - 1])
                    if wkind == "lut":
                        for wdram, wbuf in ((wg_dram, wga), (wu_dram, wgb)):
                            for j in range(hi - lo):
                                if si >= 2:
                                    g_.wait_ge(p.sem["EV"], dq_ev[si - 2])
                                sem = "LG" + str(si % 2)
                                p.inc(g_.dma_start(
                                    out=ibuf[si % 2][:].rearrange("p (k m) -> p k m", k=KC),
                                    in_=wdram.rearrange("(k q) m -> q k m", q=128)
                                    [:, :, (lo + j) * 128:(lo + j + 1) * 128]), sem, 16)
                                slab_ld[si] = (sem, p.c[sem])
                                si += 1
                    else:
                        p.inc(g_.dma_start(
                            out=wga[:, 0:KC * (hi - lo) * 128].rearrange("p (k m) -> p k m", k=KC),
                            in_=wg_dram.rearrange("(k q) m -> q k m", q=128)[:, :, lo * 128:hi * 128]),
                            "LGW", 16)
                        p.inc(g_.dma_start(
                            out=wgb[:, 0:KC * (hi - lo) * 128].rearrange("p (k m) -> p k m", k=KC),
                            in_=wu_dram.rearrange("(k q) m -> q k m", q=128)[:, :, lo * 128:hi * 128]),
                            "LGW", 16)

            @block.sync
            def _(s):
                p.bwait(s, s0, skip=("LD0", "LD1", "ST0", "ST1", "ST2", "ST3"))
                i = 0
                for g, (lo, hi) in enumerate(groups):
                    for tc in range(NTC):
                        u = g * NTC + tc
                        if u >= 2:
                            pg, ptc = divmod(u - 2, NTC)
                            s.wait_ge(p.sem["MM"], mm_of[(pg, ptc)])
                        sem = "LD" + str(u % 2)
                        p.inc(s.dma_start(
                            out=rb[u % 2][:].rearrange("p (k t) -> p k t", k=KC),
                            in_=rhs[:, :, tc * TC:(tc + 1) * TC].rearrange("k p t -> p k t")), sem, 16)
                        ld_rhs[(g, tc)] = (sem, p.c[sem])
                        for j in range(hi - lo):
                            s.wait_ge(p.sem["EV"], ev_unit[i])
                            ssem = "ST" + str(i % 4)
                            p.inc(s.dma_start(out=hdst[lo + j, :, tc * TC:(tc + 1) * TC],
                                              in_=ot[i % 4][:]), ssem, 16)
                            st_done[i] = (ssem, p.c[ssem])
                            i += 1

            @block.vector
            def _(v):
                p.bwait(v, s0, skip=("EV",))
                si = 0
                ui = 0
                for g, (lo, hi) in enumerate(groups):
                    if wkind == "lut":
                        if g >= 1:
                            v.wait_ge(p.sem["MM"], last_of_g[g - 1])
                        for wi, wbuf in enumerate((wga, wgb)):
                            for j in range(hi - lo):
                                sm, val = slab_ld[si]
                                v.wait_ge(p.sem[sm], val)
                                for vv in range(16):
                                    dstv = acf[:] if vv == 0 else tmf[:]
                                    v.tensor_scalar(out=dstv, in0=ibuf[si % 2][:], scalar1=float(vv),
                                                    scalar2=lutb_s[:, vv:vv + 1],
                                                    op0=ALU.is_equal, op1=ALU.mult)
                                    if vv > 0:
                                        v.tensor_tensor(out=acf[:], in0=acf[:], in1=tmf[:], op=ALU.add)
                                p.inc(v.tensor_copy(
                                    out=wbuf[:, 0:KC * (hi - lo) * 128]
                                    .rearrange("p (k m) -> p k m", k=KC)[:, :, j * 128:(j + 1) * 128],
                                    in_=acf[:].rearrange("p (k m) -> p k m", k=KC)), "EV")
                                si += 1
                    else:
                        v.wait_ge(p.sem["LGW"], lgw0 + 32 * (g + 1))
                        if g >= 1:
                            v.wait_ge(p.sem["MM"], last_of_g[g - 1])
                        p.inc(v.tensor_scalar_add(wga[:, 0:KC * (hi - lo) * 128],
                                                  wga[:, 0:KC * (hi - lo) * 128], -128.0), "EV")
                        p.inc(v.tensor_scalar_add(wgb[:, 0:KC * (hi - lo) * 128],
                                                  wgb[:, 0:KC * (hi - lo) * 128], -128.0), "EV")
                    for tc in range(NTC):
                        for j in range(hi - lo):
                            m = lo + j
                            v.wait_ge(p.sem["MM"], mm0 + ui + 1)
                            v.wait_ge(p.sem["ES"], es0 + ui + 1)
                            if scu is not None:
                                p.vsync(v, v.tensor_scalar_mul(ut[:], ps[(2 * ui + 1) % 4][:, 0:TC],
                                                               scu[:, m:m + 1]))
                            else:
                                p.vsync(v, v.tensor_copy(out=ut[:], in_=ps[(2 * ui + 1) % 4][:, 0:TC]))
                            if ui >= 4:
                                sm, val = st_done[ui - 4]
                                v.wait_ge(p.sem[sm], val)
                            p.inc(v.tensor_tensor(out=ot[ui % 4][:], in0=gt[ui % 2][:], in1=ut[:],
                                                  op=ALU.mult), "EV")
                            ui += 1

            @block.tensor
            def _(t):
                p.bwait(t, s0, skip=("MM",))
                i = 0
                for g, (lo, hi) in enumerate(groups):
                    if wkind != "lut":
                        t.wait_ge(p.sem["LGW"], lgw0 + 32 * (g + 1))
                    t.wait_ge(p.sem["EV"], nw_ev[g])
                    for tc in range(NTC):
                        u = g * NTC + tc
                        sm, val = ld_rhs[(g, tc)]
                        t.wait_ge(p.sem[sm], val)
                        for j in range(hi - lo):
                            if i >= 2:
                                t.wait_ge(p.sem["ES"], es0 + (i - 2) + 1)
                                t.wait_ge(p.sem["EV"], ev_unit[i - 2])
                            for k in range(KC):
                                t.matmul(ps[(2 * i) % 4][:, 0:TC],
                                         wga[:, (k * (hi - lo) + j) * 128:(k * (hi - lo) + j + 1) * 128],
                                         rb[u % 2][:, k * TC:(k + 1) * TC],
                                         start=(k == 0), stop=(k == KC - 1))
                            for k in range(KC):
                                mi = t.matmul(ps[(2 * i + 1) % 4][:, 0:TC],
                                              wgb[:, (k * (hi - lo) + j) * 128:(k * (hi - lo) + j + 1) * 128],
                                              rb[u % 2][:, k * TC:(k + 1) * TC],
                                              start=(k == 0), stop=(k == KC - 1))
                            p.inc(mi, "MM")
                            i += 1

            @block.scalar
            def _(sc):
                p.bwait(sc, s0, skip=("ES",))
                for i, (g, tc, j, m) in enumerate(units):
                    sc.wait_ge(p.sem["MM"], mm0 + i + 1)
                    if i >= 2:
                        sc.wait_ge(p.sem["EV"], ev_unit[i - 2])
                    if scg is not None:
                        ii = sc.activation(out=gt[i % 2][:], in_=ps[(2 * i) % 4][:, 0:TC],
                                           func=AF.Silu, scale=scg[:, m:m + 1])
                    else:
                        ii = sc.activation(out=gt[i % 2][:], in_=ps[(2 * i) % 4][:, 0:TC], func=AF.Silu)
                    p.inc(ii, "ES")

        def p_vtm():
            s0 = p.snap()
            mm0, es0 = s0["MM"], s0["ES"]
            wv = p.sb("vtw", [128, KD * 2048], BF16)
            rb = [p.sb("vtr", [128, KD * TC], BF16) for _ in range(2)]
            ot = [p.sb("vto", [128, 512], BF16) for _ in range(4)]
            lgw0 = s0["LGW"]
            ld_rhs, st_done = {}, {}
            TSC = TC // 128
            units = []
            for tc in range(NTC):
                for tsc in range(TSC):
                    for nb in range(4):
                        units.append((tc, tsc, nb))

            @block.gpsimd
            def _(g_):
                p.bwait(g_, s0, skip=("LGW",))
                p.inc(g_.dma_start(
                    out=wv[:].rearrange("p (k m) -> p k m", k=KD),
                    in_=w_qkv.rearrange("(k q) m -> q k m", q=128)[:, :, 4096:6144]), "LGW", 16)

            @block.sync
            def _(s):
                p.bwait(s, s0, skip=("LD0", "LD1", "ST0", "ST1", "ST2", "ST3"))
                i = 0
                for tc in range(NTC):
                    if tc >= 2:
                        s.wait_ge(p.sem["MM"], mm0 + (tc - 1) * TSC * 4)
                    sem = "LD" + str(tc % 2)
                    p.inc(s.dma_start(
                        out=rb[tc % 2][:].rearrange("p (k t) -> p k t", k=KD),
                        in_=lnx[:, :, tc * TC:(tc + 1) * TC].rearrange("k p t -> p k t")), sem, 16)
                    ld_rhs[tc] = (sem, p.c[sem])
                    for tsc in range(TSC):
                        for nb in range(4):
                            s.wait_ge(p.sem["ES"], es0 + i + 1)
                            ssem = "ST" + str(i % 4)
                            kb = tc * TSC + tsc
                            for j in range(4):
                                head = nb * 4 + j
                                p.inc(s.dma_start(out=vhm[head, kb],
                                                  in_=ot[i % 4][:, j * 128:(j + 1) * 128]), ssem, 16)
                            st_done[i] = (ssem, p.c[ssem])
                            i += 1

            @block.tensor
            def _(t):
                p.bwait(t, s0, skip=("MM",))
                t.wait_ge(p.sem["LGW"], lgw0 + 16)
                i = 0
                for tc in range(NTC):
                    sm, val = ld_rhs[tc]
                    t.wait_ge(p.sem[sm], val)
                    for tsc in range(TSC):
                        for nb in range(4):
                            if i >= 4:
                                t.wait_ge(p.sem["ES"], es0 + (i - 4) + 1)
                            for k in range(KD):
                                mi = t.matmul(
                                    ps[i % 4][:, 0:512],
                                    rb[tc % 2][:, k * TC + tsc * 128:k * TC + tsc * 128 + 128],
                                    wv[:, k * 2048 + nb * 512:k * 2048 + (nb + 1) * 512],
                                    start=(k == 0), stop=(k == KD - 1))
                            p.inc(mi, "MM")
                            i += 1

            @block.scalar
            def _(sc):
                p.bwait(sc, s0, skip=("ES",))
                for i in range(len(units)):
                    sc.wait_ge(p.sem["MM"], mm0 + i + 1)
                    if i >= 4:
                        sm, val = st_done[i - 4]
                        sc.wait_ge(p.sem[sm], val)
                    p.inc(sc.copy(out=ot[i % 4][:], in_=ps[i % 4][:, 0:512]), "ES")

        def p_attn():
            s0 = p.snap()
            NQB = S // 512
            adbg = getattr(p, "attn_dbg", False)
            if adbg:
                dbg_l = p.dram("dbg_l", [32, 128, 4], F32)
                dbg_p = p.dram("dbg_p", [8, 128, 512], BF16)
            qh = p.sb("atq", [128, S], BF16)
            kh = p.sb("atk", [128, S], BF16)
            vh = p.sb("atv", [128, (S // 128) * 128], BF16)
            Ssb = p.sb("atS", [128, S], F32)
            Psb = p.sb("atP", [128, S], BF16)
            Pt = p.sb("atPt", [128, (S // 128) * 512], BF16)
            mx = p.sb("atmx", [128, 1], F32)
            nb_ = p.sb("atnb", [128, 1], F32)
            lsum = p.sb("atl", [128, 1], F32)
            lsum2 = p.sb("atl2", [128, 1], F32)
            linv = p.sb("atli", [128, 1], F32)
            ot = [p.sb("atot", [128, 512], BF16) for _ in range(4)]

            marks = {}
            cn = {"MM": s0["MM"], "ES": s0["ES"], "EV": s0["EV"], "EG": s0["EG"]}

            def walk(record):
                for h in range(NHEAD):
                    record("loads", h)
                    for qb in range(NQB):
                        record("ptzero", (h, qb))
                        for qc4 in range(4):
                            qci = qb * 4 + qc4
                            for kb5 in range(qb + 1):
                                record("score", (h, qci, kb5))
                                record("scev", (h, qci, kb5))
                            record("mask", (h, qci))
                            record("rmax", (h, qci))
                            record("exp", (h, qci))
                            record("recip", (h, qci))
                            record("pscale", (h, qci))
                            ntr = qci + 1
                            for kg in range((ntr + 3) // 4):
                                record("trans", (h, qci, kg))
                                record("trev", (h, qci, kg))
                        record("av", (h, qb))
                        record("avev", (h, qb))
                        record("store", (h, qb))

            incs = {"loads": None, "store": None, "score": "MM", "scev": "ES", "mask": "EG",
                    "rmax": "EV", "exp": "ES", "recip": "EV", "pscale": "EV", "trans": "MM",
                    "trev": "ES", "av": "MM", "avev": "ES", "ptzero": "EV"}

            def rec(kind, key):
                sm = incs[kind]
                if sm is not None:
                    cn[sm] += 1
                marks[(kind, key)] = dict(cn)
            walk(rec)
            tot = dict(cn)
            ld_done, st_done, dbg_st = {}, {}, {}

            @block.sync
            def _(s):
                p.bwait(s, s0, skip=("LD0", "LD1", "ST0", "ST1", "ST2", "ST3"))
                for h in range(NHEAD):
                    if h >= 1:
                        s.wait_ge(p.sem["MM"], marks[("av", (h - 1, NQB - 1))]["MM"])
                    sem = "LD" + str(h % 2)
                    p.inc(s.dma_start(out=qh[:], in_=qf[h]), sem, 16)
                    p.inc(s.dma_start(out=kh[:], in_=kfd[h]), sem, 16)
                    p.inc(s.dma_start(out=vh[:].rearrange("p (kb d) -> p kb d", d=128),
                                      in_=vhm[h].rearrange("kb p d -> p kb d")), sem, 16)
                    ld_done[h] = (sem, p.c[sem])
                    for qb in range(NQB):
                        if adbg and h == 0:
                            for qc4 in range(4):
                                qci = qb * 4 + qc4
                                s.wait_ge(p.sem["EV"], marks[("pscale", (h, qci))]["EV"])
                                if qci < 32:
                                    p.inc(s.dma_start(allow_slow_non_contiguous=True, out=dbg_l[qci, :, 0:1], in_=nb_[:, 0:1]), "ST0", 16)
                                    p.inc(s.dma_start(allow_slow_non_contiguous=True, out=dbg_l[qci, :, 1:2], in_=nb_[:, 0:1]), "ST0", 16)
                                    p.inc(s.dma_start(allow_slow_non_contiguous=True, out=dbg_l[qci, :, 2:3], in_=lsum[:, 0:1]), "ST0", 16)
                                    p.inc(s.dma_start(allow_slow_non_contiguous=True, out=dbg_l[qci, :, 3:4], in_=lsum2[:, 0:1]), "ST0", 16)
                                if qci < 8:
                                    p.inc(s.dma_start(out=dbg_p[qci], in_=Psb[:, 0:512]), "ST0", 16)
                                dbg_st[qci] = p.c["ST0"]
                        s.wait_ge(p.sem["ES"], marks[("avev", (h, qb))]["ES"])
                        ssem = "ST" + str((h * NQB + qb) % 4)
                        p.inc(s.dma_start(out=ao[h, :, qb * 512:(qb + 1) * 512],
                                          in_=ot[(h * NQB + qb) % 4][:]), ssem, 16)
                        st_done[(h, qb)] = (ssem, p.c[ssem])

            @block.tensor
            def _(t):
                p.bwait(t, s0, skip=("MM",))
                for h in range(NHEAD):
                    sm, val = ld_done[h]
                    t.wait_ge(p.sem[sm], val)
                    for qb in range(NQB):
                        for qc4 in range(4):
                            qci = qb * 4 + qc4
                            for kb5 in range(qb + 1):
                                nw = 512 if kb5 < qb else (qc4 + 1) * 128
                                if (h, qci, kb5) != (0, 0, 0):
                                    t.wait_ge(p.sem["ES"], max(s0["ES"],
                                              marks[("scev", (h, qci, kb5))]["ES"] - 2))
                                p.inc(t.matmul(ps[(kb5 + qci) % 2][:, 0:nw],
                                               qh[:, qci * 128:(qci + 1) * 128],
                                               kh[:, kb5 * 512:kb5 * 512 + nw],
                                               start=True, stop=True), "MM")
                            t.wait_ge(p.sem["EV"], marks[("pscale", (h, qci))]["EV"])
                            ntr = qci + 1
                            for kg in range((ntr + 3) // 4):
                                wdt = min(4, ntr - kg * 4)
                                t.wait_ge(p.sem["ES"], max(s0["ES"],
                                          marks[("trev", (h, qci, kg))]["ES"] - 2))
                                for jj in range(wdt):
                                    kb = kg * 4 + jj
                                    mi = t.matmul(ps[4 + kg % 2][:, jj * 128:(jj + 1) * 128],
                                                  Psb[:, kb * 128:(kb + 1) * 128], ident[:],
                                                  start=True, stop=True)
                                p.inc(mi, "MM")
                        t.wait_ge(p.sem["ES"], marks[("trev", (h, qb * 4 + 3,
                                  (qb * 4 + 3) // 4))]["ES"])
                        nkb = 4 * qb + 4
                        fqb = h * NQB + qb
                        if fqb >= 2:
                            pk = fqb - 2
                            t.wait_ge(p.sem["ES"], marks[("avev", (pk // NQB, pk % NQB))]["ES"])
                        for kb in range(nkb):
                            mi = t.matmul(ps[6 + fqb % 2][:, 0:512],
                                          vh[:, kb * 128:(kb + 1) * 128],
                                          Pt[:, kb * 512:(kb + 1) * 512],
                                          start=(kb == 0), stop=(kb == nkb - 1))
                        p.inc(mi, "MM")

            @block.scalar
            def _(sc):
                p.bwait(sc, s0, skip=("ES",))
                for h in range(NHEAD):
                    for qb in range(NQB):
                        for qc4 in range(4):
                            qci = qb * 4 + qc4
                            for kb5 in range(qb + 1):
                                nw = 512 if kb5 < qb else (qc4 + 1) * 128
                                sc.wait_ge(p.sem["MM"], marks[("score", (h, qci, kb5))]["MM"])
                                p.inc(sc.copy(out=Ssb[:, kb5 * 512:kb5 * 512 + nw],
                                              in_=ps[(kb5 + qci) % 2][:, 0:nw]), "ES")
                            sc.wait_ge(p.sem["EV"], marks[("rmax", (h, qci))]["EV"])
                            if (h, qci) != (0, 0):
                                pq = (h, qci - 1) if qci >= 1 else (h - 1, 4 * NQB - 1)
                                pntr = pq[1] + 1
                                sc.wait_ge(p.sem["MM"],
                                           marks[("trans", (pq[0], pq[1], (pntr - 1) // 4))]["MM"])
                            ln_ = (qci + 1) * 128
                            if adbg and h == 0 and 1 <= qci <= 32:
                                sc.wait_ge(p.sem["ST0"], dbg_st[qci - 1])
                            sc.activation(out=Psb[:, 0:ln_], in_=Ssb[:, 0:ln_], func=AF.Exp,
                                          scale=1.0, bias=nb_[:, 0:1],
                                          accum_out=lsum[:, 0:1])
                            p.inc(sc.copy(out=lsum2[:, 0:1], in_=lsum[:, 0:1]), "ES")
                            ntr = qci + 1
                            for kg in range((ntr + 3) // 4):
                                wdt = min(4, ntr - kg * 4)
                                sc.wait_ge(p.sem["MM"], marks[("trans", (h, qci, kg))]["MM"])
                                p.inc(sc.copy(
                                    out=Pt[:, 0:(S // 128) * 512]
                                    .rearrange("p (kb q) -> p kb q", q=512)
                                    [:, kg * 4:kg * 4 + wdt, qc4 * 128:(qc4 + 1) * 128],
                                    in_=ps[4 + kg % 2][:, 0:wdt * 128]
                                    .rearrange("p (w q) -> p w q", q=128)), "ES")
                        sc.wait_ge(p.sem["MM"], marks[("av", (h, qb))]["MM"])
                        fqb = h * NQB + qb
                        i4 = fqb % 4
                        if fqb >= 4:
                            pk = fqb - 4
                            sm, val = st_done[(pk // NQB, pk % NQB)]
                            sc.wait_ge(p.sem[sm], val)
                        p.inc(sc.copy(out=ot[i4][:], in_=ps[6 + fqb % 2][:, 0:512]), "ES")

            @block.gpsimd
            def _(g_):
                p.bwait(g_, s0, skip=("EG",))
                fillreg = g_.to_reg(NEG)
                for h in range(NHEAD):
                    for qb in range(NQB):
                        for qc4 in range(4):
                            qci = qb * 4 + qc4
                            g_.wait_ge(p.sem["ES"], marks[("scev", (h, qci, qb))]["ES"])
                            p.inc(g_.affine_select(
                                out=Ssb[:, qci * 128:(qci + 1) * 128],
                                in_=Ssb[:, qci * 128:(qci + 1) * 128],
                                pattern=[[-1, 128]], compare_op=ALU.is_ge, fill=fillreg,
                                base=0, channel_multiplier=1), "EG")

            @block.vector
            def _(v):
                p.bwait(v, s0, skip=("EV",))
                for h in range(NHEAD):
                    for qb in range(NQB):
                        if (h, qb) != (0, 0):
                            pqb = qb - 1 if qb >= 1 else NQB - 1
                            ph = h if qb >= 1 else h - 1
                            v.wait_ge(p.sem["MM"], marks[("av", (ph, pqb))]["MM"])
                        p.inc(v.memset(Pt[:, qb * 4 * 512:(qb + 1) * 4 * 512], 0.0), "EV")
                        for qc4 in range(4):
                            qci = qb * 4 + qc4
                            v.wait_ge(p.sem["EG"], marks[("mask", (h, qci))]["EG"])
                            if adbg and h == 0 and 1 <= qci <= 32:
                                v.wait_ge(p.sem["ST0"], dbg_st[qci - 1])
                            ln_ = (qci + 1) * 128
                            p.inc(v.reduce_max(out=nb_[:, 0:1], in_=Ssb[:, 0:ln_],
                                               axis=mybir.AxisListType.X, negate=True), "EV")
                            v.wait_ge(p.sem["ES"], marks[("exp", (h, qci))]["ES"])
                            p.inc(v.reciprocal(out=linv[:, 0:1], in_=lsum2[:, 0:1]), "EV")
                            v.wait_ge(p.sem["EV"], marks[("recip", (h, qci))]["EV"])
                            p.inc(v.tensor_scalar_mul(Psb[:, 0:ln_], Psb[:, 0:ln_],
                                                      linv[:, 0:1]), "EV")

        def p_conv(xraw):
            s0 = p.snap()
            es0, ev0 = s0["ES"], s0["EV"]
            cv = [p.sb("cvb", [128, TC + 3], BF16) for _ in range(2)]
            t1 = [p.sb("cvt", [128, TC], F32) for _ in range(2)]
            ot = [p.sb("cvo", [128, TC], BF16) for _ in range(4)]
            ld_done, st_done = {}, {}
            NU = 32 * NTC

            @block.sync
            def _(s):
                p.bwait(s, s0, skip=("LD0", "LD1", "ST0", "ST1", "ST2", "ST3"))
                for i in range(NU):
                    cb, tc = divmod(i, NTC)
                    if i >= 2:
                        s.wait_ge(p.sem["ES"], es0 + (i - 2) + 1)
                    sem = "LD" + str(i % 2)
                    if tc == 0:
                        p.inc(s.dma_start(out=cv[i % 2][:, 3:3 + TC],
                                          in_=xraw[cb, :, 0:TC]), sem, 16)
                    else:
                        p.inc(s.dma_start(out=cv[i % 2][:, 0:TC + 3],
                                          in_=xraw[cb, :, tc * TC - 3:(tc + 1) * TC]), sem, 16)
                    ld_done[i] = (sem, p.c[sem])
                    s.wait_ge(p.sem["ES"], es0 + i + 1)
                    ssem = "ST" + str(i % 4)
                    p.inc(s.dma_start(out=xssm[cb, :, tc * TC:(tc + 1) * TC], in_=ot[i % 4][:]), ssem, 16)
                    st_done[i] = (ssem, p.c[ssem])

            @block.vector
            def _(v):
                p.bwait(v, s0, skip=("EV",))
                for i in range(NU):
                    cb, tc = divmod(i, NTC)
                    if tc == 0:
                        if i >= 2:
                            v.wait_ge(p.sem["ES"], es0 + (i - 2) + 1)
                        v.memset(cv[i % 2][:, 0:3], 0.0)
                    sm, val = ld_done[i]
                    v.wait_ge(p.sem[sm], val)
                    c = cv[i % 2]
                    v.tensor_scalar_mul(t1[i % 2][:], c[:, 3:3 + TC], convw_s[:, cb * 4 + 3:cb * 4 + 4])
                    for j in (2, 1, 0):
                        ii = v.scalar_tensor_tensor(out=t1[i % 2][:], in0=c[:, j:j + TC],
                                                    scalar=convw_s[:, cb * 4 + j:cb * 4 + j + 1],
                                                    in1=t1[i % 2][:], op0=ALU.mult, op1=ALU.add)
                    p.inc(ii, "EV")

            @block.scalar
            def _(sc):
                p.bwait(sc, s0, skip=("ES",))
                for i in range(NU):
                    cb, tc = divmod(i, NTC)
                    sc.wait_ge(p.sem["EV"], ev0 + i + 1)
                    if i >= 4:
                        sm, val = st_done[i - 4]
                        sc.wait_ge(p.sem[sm], val)
                    p.inc(sc.activation(out=ot[i % 4][:], in_=t1[i % 2][:], func=AF.Silu,
                                        bias=convb_s[:, cb:cb + 1]), "ES")

        def p_scan():
            s0 = p.snap()
            es0, ev0, mm0 = s0["ES"], s0["EV"], s0["MM"]
            xsb = p.sb("scxs", [128, KD * TC], BF16)
            gsb = p.sb("scgs", [128, KD * TC], BF16)
            dtin = p.sb("scdt", [128, TC], BF16)
            bcb = p.sb("scbc", [32, TC], BF16)
            bcbuf = p.sb("scbb", [128, 32 * TC], BF16)
            dtf = [p.sb("scdf", [128, TC], F32) for _ in range(2)]
            spr = p.sb("scsp", [128, TC], F32)
            spt = p.sb("scspt", [128, TC], F32)
            dab = [p.sb("scda", [128, NST * TC], F32) for _ in range(2)]
            uf = p.sb("scu", [128, TC], F32)
            dbu = p.sb("scdb", [128, TC], F32)
            stb = [p.sb("scst", [128, TC], BF16) for _ in range(2)]
            ytm = p.sb("scyt", [128, TC], BF16)
            gty = p.sb("scgy", [128, TC], BF16)
            yac = p.sb("scya", [128, TC], F32)
            yag = p.sb("scyag", [128, TC], F32)
            ygb = p.sb("scyg", [128, KD * TC], BF16)
            GSET = [n for n in range(NST) if n % 4 != 0]
            VSET = [n for n in range(NST) if n % 4 == 0]
            vs_mark = {}
            eg_mark = {}
            eg0 = s0["EG"]

            def eg_at(i, db, n):
                per_db = len(GSET)
                base = eg0 + (i * 16 + db) * per_db
                return base + GSET.index(n) + 1
            ld_done, st_done = {}, {}
            r_mm = 2 + 32 + 16
            r_es = 2 + 32 + 32
            r_ev = 32
            NU = NTC * NH

            def mm_at(i, stage, n=0):
                base = mm0 + i * r_mm
                if stage == "xp":
                    return base + 2
                if stage == "bc":
                    return base + 2 + n + 1
                return base + 2 + 32 + n + 1

            def es_at(i, stage, n=0):
                base = es0 + i * r_es
                if stage == "io":
                    return base + 2
                if stage == "bc":
                    return base + 2 + n + 1
                if stage == "db_a":
                    return base + 2 + 32 + 2 * n + 1
                return base + 2 + 32 + 2 * n + 2

            def ev_at(i, db, end=True):
                return ev0 + i * r_ev + 2 * db + (2 if end else 1)

            @block.sync
            def _(s):
                p.bwait(s, s0, skip=("LD0", "LD1", "ST0", "ST1", "ST2", "ST3"))
                for i in range(NU):
                    tc, h = divmod(i, NH)
                    if i >= 1:
                        s.wait_ge(p.sem["EV"], ev_at(i - 1, 15))
                    sem = "LD" + str(i % 2)
                    p.inc(s.dma_start(out=xsb[:].rearrange("p (c t) -> p c t", c=KD),
                                      in_=xssm[h * KD:(h + 1) * KD, :, tc * TC:(tc + 1) * TC]
                                      .rearrange("c p t -> p c t")), sem, 16)
                    p.inc(s.dma_start(out=gsb[:].rearrange("p (c t) -> p c t", c=KD),
                                      in_=gsil[h * KD:(h + 1) * KD, :, tc * TC:(tc + 1) * TC]
                                      .rearrange("c p t -> p c t")), sem, 16)
                    ld_done[i] = (sem, p.c[sem])
                    s.wait_ge(p.sem["EV"], ev_at(i, 15))
                    ssem = "ST" + str(i % 4)
                    p.inc(s.dma_start(out=ygat[h * KD:(h + 1) * KD, :, tc * TC:(tc + 1) * TC]
                                      .rearrange("c p t -> p c t"),
                                      in_=ygb[:].rearrange("p (c t) -> p c t", c=KD)), ssem, 16)
                    st_done[i] = (ssem, p.c[ssem])

            @block.tensor
            def _(t):
                p.bwait(t, s0, skip=("MM",))
                for i in range(NU):
                    tc, h = divmod(i, NH)
                    sm, val = ld_done[i]
                    t.wait_ge(p.sem[sm], val)
                    if i >= 1:
                        t.wait_ge(p.sem["ES"], es_at(i - 1, "db_a", 15))
                    for k in range(KD):
                        mi = t.matmul(ps[2][:, 0:TC],
                                      w_xp_s[:, ((h * KD + k) * 160):((h * KD + k) * 160 + 128)],
                                      xsb[:, k * TC:(k + 1) * TC], start=(k == 0), stop=(k == KD - 1))
                    p.inc(mi, "MM")
                    for k in range(KD):
                        mi = t.matmul(ps[3][0:32, 0:TC],
                                      w_xp_s[:, ((h * KD + k) * 160 + 128):((h * KD + k) * 160 + 160)],
                                      xsb[:, k * TC:(k + 1) * TC], start=(k == 0), stop=(k == KD - 1))
                    p.inc(mi, "MM")
                    t.wait_ge(p.sem["ES"], es_at(i, "io"))
                    for n in range(32):
                        if n >= 2:
                            t.wait_ge(p.sem["ES"], es_at(i, "bc", n - 2))
                        p.inc(t.matmul(ps[4 + n % 2][:, 0:TC], sel32[:, n * 128:(n + 1) * 128],
                                       bcb[:], start=True, stop=True), "MM")
                    for db in range(16):
                        if db >= 2:
                            t.wait_ge(p.sem["ES"], es_at(i, "db_a", db - 2))
                        p.inc(t.matmul(ps[db % 2][:, 0:TC],
                                       w_dt_s[:, (h * 16 + db) * 128:(h * 16 + db + 1) * 128],
                                       dtin[:], start=True, stop=True), "MM")

            @block.scalar
            def _(sc):
                p.bwait(sc, s0, skip=("ES",))
                for i in range(NU):
                    tc, h = divmod(i, NH)
                    sc.wait_ge(p.sem["MM"], mm_at(i, "xp"))
                    p.inc(sc.copy(out=dtin[:], in_=ps[2][:, 0:TC]), "ES")
                    p.inc(sc.copy(out=bcb[:], in_=ps[3][0:32, 0:TC]), "ES")
                    for n in range(32):
                        sc.wait_ge(p.sem["MM"], mm_at(i, "bc", n))
                        if i >= 1 and n == 0:
                            sc.wait_ge(p.sem["EV"], ev_at(i - 1, 15))
                        p.inc(sc.copy(out=bcbuf[:, n * TC:(n + 1) * TC], in_=ps[4 + n % 2][:, 0:TC]), "ES")
                    for db in range(16):
                        sc.wait_ge(p.sem["MM"], mm_at(i, "dt", db))
                        if db >= 2:
                            sc.wait_ge(p.sem["EV"], ev_at(i, db - 2))
                        bb = dtb_s[:, h * 16 + db:h * 16 + db + 1]
                        sc.activation(out=spr[:], in_=ps[db % 2][:, 0:TC], func=AF.Relu, bias=bb)
                        sc.activation(out=spt[:], in_=ps[db % 2][:, 0:TC], func=AF.Abs, bias=bb)
                        sc.activation(out=spt[:], in_=spt[:], func=AF.Exp, scale=-1.0)
                        p.inc(sc.activation(out=spt[:], in_=spt[:], func=AF.Ln, bias=1.0), "ES")
                        sc.wait_ge(p.sem["EV"], ev_at(i, db, end=False))
                        for n in range(NST):
                            ii = sc.activation(out=dab[db % 2][:, n * TC:(n + 1) * TC], in_=dtf[db % 2][:],
                                               func=AF.Exp,
                                               scale=a_s[:, (h * 16 + db) * NST + n:(h * 16 + db) * NST + n + 1])
                        p.inc(ii, "ES")

            @block.vector
            def _(v):
                p.bwait(v, s0, skip=("EV",))
                for i in range(NU):
                    tc, h = divmod(i, NH)
                    if i >= 1:
                        _sm, _val = st_done[i - 1]
                        v.wait_ge(p.sem[_sm], _val)
                    for db in range(16):
                        v.wait_ge(p.sem["ES"], es_at(i, "db_a", db))
                        p.inc(v.tensor_tensor(out=dtf[db % 2][:], in0=spr[:], in1=spt[:], op=ALU.add), "EV")
                        v.wait_ge(p.sem["ES"], es_at(i, "db_b", db))
                        v.tensor_tensor(out=uf[:], in0=dtf[db % 2][:], in1=xsb[:, db * TC:(db + 1) * TC],
                                        op=ALU.mult)
                        first_v = True
                        for n in range(NST):
                            cidx = (h * 16 + db) * NST + n
                            v.tensor_tensor(out=dbu[:], in0=uf[:], in1=bcbuf[:, n * TC:(n + 1) * TC],
                                            op=ALU.mult)
                            if n >= 2 and (n - 2) in GSET:
                                v.wait_ge(p.sem["EG"], eg_at(i, db, n - 2))
                            p.vsync(v, v.tensor_tensor_scan(out=stb[n % 2][:],
                                                            data0=dab[db % 2][:, n * TC:(n + 1) * TC],
                                                            data1=dbu[:], initial=carry[:, cidx:cidx + 1],
                                                            op0=ALU.mult, op1=ALU.add))
                            vs_mark[(i, db, n)] = p.c["VS"]
                            v.tensor_copy(out=carry[:, cidx:cidx + 1], in_=stb[n % 2][:, TC - 1:TC])
                            if n in VSET:
                                v.tensor_tensor(out=ytm[:], in0=stb[n % 2][:],
                                                in1=bcbuf[:, (16 + n) * TC:(17 + n) * TC], op=ALU.mult)
                                if first_v:
                                    v.tensor_copy(out=yac[:], in_=ytm[:])
                                    first_v = False
                                else:
                                    v.tensor_tensor(out=yac[:], in0=yac[:], in1=ytm[:], op=ALU.add)
                        v.wait_ge(p.sem["EG"], eg_at(i, db, GSET[-1]))
                        v.tensor_tensor(out=yac[:], in0=yac[:], in1=yag[:], op=ALU.add)
                        v.scalar_tensor_tensor(out=yac[:], in0=xsb[:, db * TC:(db + 1) * TC],
                                               scalar=d_s[:, h * 16 + db:h * 16 + db + 1],
                                               in1=yac[:], op0=ALU.mult, op1=ALU.add)
                        ii = v.tensor_tensor(out=ygb[:, db * TC:(db + 1) * TC], in0=yac[:],
                                             in1=gsb[:, db * TC:(db + 1) * TC], op=ALU.mult)
                        p.inc(ii, "EV")

            @block.gpsimd
            def _(g_):
                p.bwait(g_, s0, skip=("EG",))
                for i in range(NU):
                    tc, h = divmod(i, NH)
                    g_.wait_ge(p.sem["ES"], es_at(i, "bc", 31))
                    for db in range(16):
                        if (i, db) != (0, 0):
                            pi, pdb = (i, db - 1) if db >= 1 else (i - 1, 15)
                            g_.wait_ge(p.sem["EV"], ev_at(pi, pdb))
                        first_g = True
                        for n in GSET:
                            g_.wait_ge(p.sem["VS"], vs_mark[(i, db, n)])
                            g_.tensor_tensor(out=gty[:], in0=stb[n % 2][:],
                                             in1=bcbuf[:, (16 + n) * TC:(17 + n) * TC], op=ALU.mult)
                            if first_g:
                                p.inc(g_.tensor_copy(out=yag[:], in_=gty[:]), "EG")
                                first_g = False
                            else:
                                p.inc(g_.tensor_tensor(out=yag[:], in0=yag[:], in1=gty[:], op=ALU.add), "EG")

        xraw = p.dram("xraw", [32, 128, S], BF16)

        def ev_copy(m, pa, oa, sc):
            return sc.copy(out=oa, in_=pa)

        def ev_silu(m, pa, oa, sc):
            return sc.activation(out=oa, in_=pa, func=AF.Silu)

        def ev_scale_d(m, pa, oa, sc):
            return sc.activation(out=oa, in_=pa, func=AF.Identity, scale=scd_s[:, m:m + 1])

        def ev_scale_isq(m, pa, oa, sc):
            return sc.activation(out=oa, in_=pa, func=AF.Identity, scale=ISQ)

        g8 = [(k * 8, (k + 1) * 8) for k in range(8)]
        g43 = [(0, 8), (8, 16), (16, 24), (24, 32), (32, 40), (40, 43)]

        seq = [
            ("init", p_init),
            ("ln1", lambda: p_lnlike(xin, True, "pre", write_xres=xresA)),
            ("inpx", lambda: p_gemm(w_inx, "f32", D, [(0, 16), (16, 32)], lnx, ev_copy,
                                    lambda m, tc: xraw[m, :, tc * TC:(tc + 1) * TC])),
            ("inpg", lambda: p_gemm(w_ing, "f32", D, [(0, 16), (16, 32)], lnx, ev_silu,
                                    lambda m, tc: gsil[m, :, tc * TC:(tc + 1) * TC])),
            ("conv", lambda: p_conv(xraw)),
            ("scan", p_scan),
            ("outp", lambda: p_gemm(w_out, "f32", IMIX, [(0, 8), (8, 16)], ygat, ev_copy,
                                    lambda m, tc: z32[m, :, tc * TC:(tc + 1) * TC], out_f32=True)),
            ("pln1", lambda: p_lnlike(z32, True, "post", xold=xresA, dst=xresB)),
            ("ln2", lambda: p_lnlike(xresB, False, "pre")),
            ("ffgu", lambda: p_gemm_dual(w_fg, w_fu, "i32", D, g8, lnx, h1,
                                         scg=scg_s, scu=scu_s)),
            ("ffdn", lambda: p_gemm(w_fd, "i32", IFFN, [(0, 4), (4, 8), (8, 12), (12, 16)],
                                    h1, ev_scale_d,
                                    lambda m, tc: z32[m, :, tc * TC:(tc + 1) * TC],
                                    out_f32=True, nrb=1)),
            ("pln2", lambda: p_lnlike(z32, True, "post", xold=xresB, dst=xresA)),
            ("ln3", lambda: p_lnlike(xresA, False, "pre")),
            ("qp", lambda: p_gemm(w_qkv, "f32", D, [(0, 16)], lnx, ev_scale_isq,
                                  lambda m, tc: qf[m, :, tc * TC:(tc + 1) * TC])),
            ("kp", lambda: p_gemm(w_qkv, "f32", D, [(16, 32)], lnx, ev_copy,
                                  lambda m, tc: kfd[m - 16, :, tc * TC:(tc + 1) * TC])),
            ("vp", p_vtm),
            ("attn", p_attn),
            ("proj", lambda: p_gemm(w_pr, "f32", D, [(0, 8), (8, 16)], ao, ev_copy,
                                    lambda m, tc: z32[m, :, tc * TC:(tc + 1) * TC], out_f32=True)),
            ("pln3", lambda: p_lnlike(z32, True, "post", xold=xresA, dst=xresB)),
            ("ln4", lambda: p_lnlike(xresB, False, "pre")),
            ("f1gu", lambda: p_gemm_dual(i_g, i_u, "lut", D, g43, lnx, h2)),
            ("f1dn", lambda: p_gemm(i_d, "lut", IFF1, [(0, 4), (4, 8), (8, 12), (12, 16)],
                                    h2, ev_copy,
                                    lambda m, tc: z32[m, :, tc * TC:(tc + 1) * TC],
                                    out_f32=True, nrb=1)),
            ("pln4", lambda: p_lnlike(z32, True, "post", xold=xresB, dst=out)),
        ]
        stop = getattr(self, "stop_after", None)
        self.phase_counts = {}
        for n, fn in seq:
            with contextlib.ExitStack() as ph_es:
                self.ph_es = ph_es
                fn()
                self.ph_es = None
            self.phase_counts[n] = dict(self.c)
            if n == stop:
                break

        fin = self.snap()

        @block.sync
        def _(s):
            self.bwait(s, fin)

        @block.gpsimd
        def _(g):
            self.bwait(g, fin)
        self.tensors = dict(
            xin=xin, lnx=lnx, xresA=xresA, xresB=xresB, out=out, z32=z32,
            xssm=xssm, gsil=gsil, ygat=ygat, h1=h1, qf=qf, kfd=kfd, vhm=vhm,
            ao=ao, h2=h2)
        return nc


def _host_prep(inputs):
    c = np.ascontiguousarray
    ipw = inputs["L0_attn_in_proj_w"]
    m = {}
    m["w_inx"] = c(ipw[0::2].T)
    m["w_ing"] = c(ipw[1::2].T)
    m["convw"] = c(inputs["L0_attn_conv_w"].reshape(32, 128, 4).transpose(1, 0, 2)).reshape(128, 128)
    m["convb"] = c(inputs["L0_attn_conv_b"].reshape(32, 128).T)
    m["w_xp"] = c(inputs["L0_attn_x_proj_weight"].transpose(0, 2, 1)
                  .reshape(2, 16, 128, 160).transpose(2, 0, 1, 3)).reshape(128, 2 * 16 * 160)
    m["w_dt"] = c(inputs["L0_attn_dt_proj_weight"].transpose(0, 2, 1)
                  .transpose(1, 0, 2)).reshape(128, 2 * 2048)
    m["dtbi"] = c(inputs["L0_attn_dt_proj_bias"].reshape(2, 16, 128).transpose(2, 0, 1)).reshape(128, 32)
    m["alog"] = c(inputs["L0_attn_A_log"].reshape(2, 16, 128, 16).transpose(2, 0, 1, 3)).reshape(128, 512)
    m["dpar"] = c(inputs["L0_attn_D"].reshape(2, 16, 128).transpose(2, 0, 1)).reshape(128, 32)
    m["w_out"] = c(inputs["L0_attn_out_proj_w"].T)
    m["w_fg"] = c(inputs["L0_ffn_gate_w"].T)
    m["w_fu"] = c(inputs["L0_ffn_up_w"].T)
    m["w_fd"] = c(inputs["L0_ffn_down_w"].T)
    m["sc_g"] = c(inputs["L0_ffn_gate_scale"].reshape(64, 128).T)
    m["sc_u"] = c(inputs["L0_ffn_up_scale"].reshape(64, 128).T)
    m["sc_d"] = c(inputs["L0_ffn_down_scale"].reshape(16, 128).T)
    m["w_qkv"] = c(inputs["L1_attn_Wqkv"].T)
    m["w_pr"] = c(inputs["L1_attn_Wproj"].T)
    m["lutv"] = c(inputs["L1_ffn_lut"].reshape(1, 16))
    m["i_g"] = c(inputs["L1_ffn_gate_idx"].T)
    m["i_u"] = c(inputs["L1_ffn_up_idx"].T)
    m["i_d"] = c(inputs["L1_ffn_down_idx"].T)
    m["identf"] = np.eye(128, dtype=np.float32)
    sel = np.zeros((32, 32, 128), dtype=np.float32)
    for n in range(32):
        sel[n, n, :] = 1.0
    m["self32"] = c(sel.transpose(1, 0, 2)).reshape(32, 32 * 128)
    return m


def kernel(**inputs):
    S = 4096
    x = inputs["L0_attn_hidden_states"]
    NB = x.shape[0]
    shared = _host_prep(inputs)
    pr = Prog(S=S, TC=512)
    nc = pr.build()
    in_maps = []
    for b in range(NB):
        im = dict(shared)
        im["xin"] = np.ascontiguousarray(x[b].T).reshape(KD, 128, S)
        in_maps.append(im)
    res = run_bass_kernel_spmd(nc, in_maps, core_ids=list(range(NB)))
    outs = [np.asarray(r["out"]).reshape(D, S).T for r in res.results]
    return np.stack(outs).astype(np.float32)


# version 05
# speedup vs baseline: 1.0024x; 1.0024x over previous
import contextlib
import numpy as np
import concourse.bass as bass
import concourse.mybir as mybir
from concourse.bass_utils import run_bass_kernel_spmd

F32 = mybir.dt.float32
BF16 = mybir.dt.bfloat16
I32 = mybir.dt.int32
AF = mybir.ActivationFunctionType
ALU = mybir.AluOpType

D = 2048
KD = D // 128
NH = 2
HD = 2048
NST = 16
RNK = 128
IMIX = 4096
IFFN = 8192
IFF1 = 5504
KF1 = IFF1 // 128
NHEAD = 16
EPS = 1e-5
ISQ = float(1.0 / np.sqrt(128.0))
NEG = -1.0e30


class Prog:
    def __init__(self, S=4096, TC=512, debug_out=()):
        self.S, self.TC = S, TC
        self.NTC = S // TC
        self.debug_out = set(debug_out)
        self.nc = bass.Bass("TRN2", target_bir_lowering=False, detect_race_conditions=False)
        self.es = contextlib.ExitStack()
        self.es.__enter__()
        self.c = {k: 0 for k in ("LD0", "LD1", "LG0", "LG1", "LGW",
                                 "ST0", "ST1", "ST2", "ST3", "MM", "ES", "EV", "EG", "VS")}
        self.sem = {k: self.es.enter_context(self.nc.semaphore(k)) for k in self.c}
        self.uid = 0

    def nm(self, p):
        self.uid += 1
        return f"{p}{self.uid}"

    def inc(self, instr, name, k=1):
        instr.then_inc(self.sem[name], k)
        self.c[name] += k
        return self.c[name]

    def snap(self):
        return dict(self.c)

    def bwait(self, eng, snap, skip=()):
        for k, v in snap.items():
            if k not in skip and v > 0 and k != "VS":
                eng.wait_ge(self.sem[k], v)

    def vsync(self, eng, instr):
        self.inc(instr, "VS")
        eng.wait_ge(self.sem["VS"], self.c["VS"])

    def dram(self, name, shape, dt):
        kind = "ExternalOutput" if name in self.debug_out else "Internal"
        return self.nc.dram_tensor(name, shape, dt, kind=kind).ap()

    def sb(self, name, shape, dt):
        st = getattr(self, "ph_es", None) or self.es
        return st.enter_context(self.nc.sbuf_tensor(self.nm(name), shape, dt))

    def build(self):
        nc, es, S, TC, NTC = self.nc, self.es, self.S, self.TC, self.NTC
        NT128 = S // 128
        p = self

        EI = lambda n, sh, dt: nc.dram_tensor(n, sh, dt, kind="ExternalInput").ap()
        xin = EI("xin", [KD, 128, S], F32)
        w_inx = EI("w_inx", [D, IMIX], F32)
        w_ing = EI("w_ing", [D, IMIX], F32)
        convw = EI("convw", [128, 32 * 4], F32)
        convb = EI("convb", [128, 32], F32)
        w_xp = EI("w_xp", [128, NH * KD * 160], F32)
        w_dt = EI("w_dt", [128, NH * HD], F32)
        dtbi = EI("dtbi", [128, NH * 16], F32)
        alog = EI("alog", [128, NH * 16 * NST], F32)
        dpar = EI("dpar", [128, NH * 16], F32)
        w_out = EI("w_out", [IMIX, D], F32)
        w_fg = EI("w_fg", [D, IFFN], I32)
        w_fu = EI("w_fu", [D, IFFN], I32)
        w_fd = EI("w_fd", [IFFN, D], I32)
        sc_g = EI("sc_g", [128, 64], F32)
        sc_u = EI("sc_u", [128, 64], F32)
        sc_d = EI("sc_d", [128, 16], F32)
        w_qkv = EI("w_qkv", [D, 3 * D], F32)
        w_pr = EI("w_pr", [D, D], F32)
        lutv = EI("lutv", [1, 16], F32)
        i_g = EI("i_g", [D, IFF1], I32)
        i_u = EI("i_u", [D, IFF1], I32)
        i_d = EI("i_d", [IFF1, D], I32)
        identf = EI("identf", [128, 128], F32)
        self32 = EI("self32", [32, 32 * 128], F32)
        out = nc.dram_tensor("out", [KD, 128, S], F32, kind="ExternalOutput").ap()

        xresA = p.dram("xresA", [KD, 128, S], BF16)
        xresB = p.dram("xresB", [KD, 128, S], BF16)
        lnx = p.dram("lnx", [KD, 128, S], BF16)
        xssm = p.dram("xssm", [32, 128, S], BF16)
        gsil = p.dram("gsil", [32, 128, S], BF16)
        ygat = p.dram("ygat", [32, 128, S], BF16)
        z32 = p.dram("z32", [KD, 128, S], F32)
        h1 = p.dram("h1", [64, 128, S], BF16)
        qf = p.dram("qf", [NHEAD, 128, S], BF16)
        kfd = p.dram("kfd", [NHEAD, 128, S], BF16)
        vhm = p.dram("vhm", [NHEAD, NT128, 128, 128], BF16)
        ao = p.dram("ao", [KD, 128, S], BF16)
        h2 = p.dram("h2", [KF1, 128, S], BF16)

        ident = p.sb("ident", [128, 128], BF16)
        ones1 = p.sb("ones1", [1, 128], BF16)
        onesc = p.sb("onesc", [128, 1], BF16)
        sel32 = p.sb("sel32", [32, 32 * 128], BF16)
        convw_s = p.sb("convw_s", [128, 32 * 4], F32)
        convb_s = p.sb("convb_s", [128, 32], F32)
        dtb_s = p.sb("dtb_s", [128, NH * 16], F32)
        a_s = p.sb("a_s", [128, NH * 16 * NST], F32)
        d_s = p.sb("d_s", [128, NH * 16], F32)
        scg_s = p.sb("scg_s", [128, 64], F32)
        scu_s = p.sb("scu_s", [128, 64], F32)
        scd_s = p.sb("scd_s", [128, 16], F32)
        lut_s = p.sb("lut_s", [1, 16], F32)
        lutbf = p.sb("lutbf", [1, 16], BF16)
        lutb_s = p.sb("lutb_s", [128, 16], F32)
        carry = p.sb("carry", [128, NH * 16 * NST], F32)
        w_xp_s = p.sb("w_xp_s", [128, NH * KD * 160], BF16)
        w_dt_s = p.sb("w_dt_s", [128, NH * 16 * 128], BF16)
        ps = [es.enter_context(nc.psum_tensor(f"ps{i}", [128, 512], F32)) for i in range(8)]

        block = es.enter_context(nc.Block())

        def p_init():
            s0 = p.snap()

            @block.gpsimd
            def _(g):
                p.inc(g.dma_start(out=ident[:], in_=identf), "LGW", 16)
                p.inc(g.dma_start(out=sel32[:], in_=self32), "LGW", 16)
                p.inc(g.dma_start(out=w_xp_s[:], in_=w_xp), "LGW", 16)
                p.inc(g.dma_start(out=w_dt_s[:], in_=w_dt), "LGW", 16)

            @block.sync
            def _(s):
                p.inc(s.dma_start(out=convw_s[:], in_=convw), "LD0", 16)
                p.inc(s.dma_start(out=convb_s[:], in_=convb), "LD0", 16)
                p.inc(s.dma_start(out=dtb_s[:], in_=dtbi), "LD0", 16)
                p.inc(s.dma_start(out=a_s[:], in_=alog), "LD0", 16)
                p.inc(s.dma_start(out=d_s[:], in_=dpar), "LD0", 16)
                p.inc(s.dma_start(out=scg_s[:], in_=sc_g), "LD0", 16)
                p.inc(s.dma_start(out=scu_s[:], in_=sc_u), "LD0", 16)
                p.inc(s.dma_start(out=scd_s[:], in_=sc_d), "LD0", 16)
                p.inc(s.dma_start(out=lut_s[:], in_=lutv), "LD0", 16)

            ld1, lg1 = p.c["LD0"], p.c["LGW"]

            @block.scalar
            def _(sc):
                sc.wait_ge(p.sem["LD0"], ld1)
                sc.wait_ge(p.sem["LGW"], lg1)
                p.inc(sc.activation(out=a_s[:], in_=a_s[:], func=AF.Exp), "ES")

            es1 = p.c["ES"]

            @block.vector
            def _(v):
                v.wait_ge(p.sem["ES"], es1)
                v.wait_ge(p.sem["LD0"], ld1)
                v.tensor_scalar_mul(a_s[:], a_s[:], -1.0)
                v.memset(carry[:], 0.0)
                v.memset(ones1[:], 1.0)
                v.tensor_copy(out=lutbf[:], in_=lut_s[:])
                p.inc(v.memset(onesc[:], 1.0), "EV")

            ev1 = p.c["EV"]

            @block.tensor
            def _(t):
                t.wait_ge(p.sem["EV"], ev1)
                p.inc(t.matmul(ps[3][:, 0:16], ones1[:], lutbf[:], start=True, stop=True), "MM")

            mm1 = p.c["MM"]

            @block.vector
            def _(v):
                v.wait_ge(p.sem["MM"], mm1)
                p.inc(v.tensor_copy(out=lutb_s[:], in_=ps[3][:, 0:16]), "EV")

        def p_lnlike(src, src_f32, mode, xold=None, dst=None, write_xres=None):
            s0 = p.snap()
            KT = KD * TC
            final_f32 = (mode == "post" and dst is out)
            xb = [p.sb("lnxb", [128, KT], BF16) for _ in range(2)]
            xsq = p.sb("lnsq", [128, KT], BF16)
            onx = p.sb("lnonx", [128, KT], F32 if final_f32 else BF16)
            xob = [p.sb("lnxo", [128, KT], BF16) for _ in range(2)] if mode == "post" else None
            stat = p.sb("lnst", [1, 3 * TC], F32)
            coef = p.sb("lnco", [1, 2 * TC], BF16)
            abc = p.sb("lnab", [128, TC], BF16)
            bbc = p.sb("lnbb", [128, TC], BF16)

            mm0, es0, ev0 = s0["MM"], s0["ES"], s0["EV"]
            r_es, r_mm, r_ev = 3, 2, 3
            LQ = "LG" if src_f32 else "LD"
            ld_done = {}
            st_done = {}
            n_st = 2 if (mode == "pre" and write_xres is not None) else 1

            def loader(q, tc):
                if tc >= 2:
                    q.wait_ge(p.sem["EV"], ev0 + (tc - 2) * r_ev + r_ev)
                    pt = tc - 2
                    sm = "ST" + str(pt % 2)
                    q.wait_ge(p.sem[sm], s0[sm] + 16 * n_st * (pt // 2 + 1))
                sem = LQ + str(tc % 2)
                p.inc(q.dma_start(
                    out=xb[tc % 2][:].rearrange("p (k t) -> p k t", k=KD),
                    in_=src[:, :, tc * TC:(tc + 1) * TC].rearrange("k p t -> p k t")),
                    sem, 16)
                if mode == "post":
                    p.inc(q.dma_start(
                        out=xob[tc % 2][:].rearrange("p (k t) -> p k t", k=KD),
                        in_=xold[:, :, tc * TC:(tc + 1) * TC].rearrange("k p t -> p k t")),
                        sem, 16)
                ld_done[tc] = (sem, p.c[sem])

            if src_f32:
                @block.gpsimd
                def _(g):
                    p.bwait(g, s0, skip=("LG0", "LG1"))
                    for tc in range(NTC):
                        loader(g, tc)

            @block.sync
            def _(s):
                p.bwait(s, s0, skip=("LD0", "LD1", "ST0", "ST1"))
                for tc in range(NTC):
                    if not src_f32:
                        loader(s, tc)
                    s.wait_ge(p.sem["EV"], ev0 + tc * r_ev + 3)
                    sem = "ST" + str(tc % 2)
                    if mode == "pre":
                        p.inc(s.dma_start(
                            out=lnx[:, :, tc * TC:(tc + 1) * TC].rearrange("k p t -> p k t"),
                            in_=onx[:].rearrange("p (k t) -> p k t", k=KD)), sem, 16)
                        if write_xres is not None:
                            p.inc(s.dma_start(
                                out=write_xres[:, :, tc * TC:(tc + 1) * TC].rearrange("k p t -> p k t"),
                                in_=xb[tc % 2][:].rearrange("p (k t) -> p k t", k=KD)), sem, 16)
                    else:
                        p.inc(s.dma_start(
                            out=dst[:, :, tc * TC:(tc + 1) * TC].rearrange("k p t -> p k t"),
                            in_=onx[:].rearrange("p (k t) -> p k t", k=KD)), sem, 16)
                    st_done[tc] = (sem, p.c[sem])

            @block.scalar
            def _(sc):
                p.bwait(sc, s0, skip=("ES",))
                for tc in range(NTC):
                    sm, v = ld_done[tc]
                    sc.wait_ge(p.sem[sm], v)
                    p.inc(sc.activation(out=xsq[:], in_=xb[tc % 2][:], func=AF.Square), "ES")
                    sc.wait_ge(p.sem["EV"], ev0 + tc * r_ev + 1)
                    p.inc(sc.activation(out=stat[0:1, 2 * TC:3 * TC], in_=stat[0:1, TC:2 * TC],
                                        func=AF.Ln), "ES")
                    p.inc(sc.activation(out=stat[0:1, TC:2 * TC], in_=stat[0:1, 2 * TC:3 * TC],
                                        func=AF.Exp, scale=-0.5), "ES")

            @block.tensor
            def _(t):
                p.bwait(t, s0, skip=("MM",))
                for tc in range(NTC):
                    t.wait_ge(p.sem["ES"], es0 + tc * r_es + 1)
                    for k in range(KD):
                        t.matmul(ps[6][0:1, 0:TC], onesc[:], xb[tc % 2][:, k * TC:(k + 1) * TC],
                                 start=(k == 0), stop=(k == KD - 1))
                    for k in range(KD):
                        mi = t.matmul(ps[7][0:1, 0:TC], onesc[:], xsq[:, k * TC:(k + 1) * TC],
                                      start=(k == 0), stop=(k == KD - 1))
                    p.inc(mi, "MM")
                    t.wait_ge(p.sem["EV"], ev0 + tc * r_ev + 2)
                    t.matmul(ps[4][:, 0:TC], ones1[:], coef[0:1, 0:TC], start=True, stop=True)
                    mi = t.matmul(ps[5][:, 0:TC], ones1[:], coef[0:1, TC:2 * TC], start=True, stop=True)
                    p.inc(mi, "MM")

            @block.vector
            def _(v):
                p.bwait(v, s0, skip=("EV",))
                for tc in range(NTC):
                    v.wait_ge(p.sem["MM"], mm0 + tc * r_mm + 1)
                    v.tensor_scalar(out=stat[0:1, 0:TC], in0=ps[6][0:1, 0:TC],
                                    scalar1=1.0 / D, scalar2=None, op0=ALU.mult)
                    p.vsync(v, v.tensor_scalar(out=stat[0:1, TC:2 * TC], in0=ps[7][0:1, 0:TC],
                                               scalar1=1.0 / D, scalar2=float(EPS),
                                               op0=ALU.mult, op1=ALU.add))
                    v.tensor_tensor(out=stat[0:1, 2 * TC:3 * TC], in0=stat[0:1, 0:TC],
                                    in1=stat[0:1, 0:TC], op=ALU.mult)
                    p.inc(v.tensor_tensor(out=stat[0:1, TC:2 * TC], in0=stat[0:1, TC:2 * TC],
                                          in1=stat[0:1, 2 * TC:3 * TC], op=ALU.subtract), "EV")
                    v.wait_ge(p.sem["ES"], es0 + tc * r_es + 3)
                    v.tensor_copy(out=coef[0:1, 0:TC], in_=stat[0:1, TC:2 * TC])
                    v.tensor_tensor(out=stat[0:1, 2 * TC:3 * TC], in0=stat[0:1, 0:TC],
                                    in1=stat[0:1, TC:2 * TC], op=ALU.mult)
                    p.inc(v.tensor_scalar_mul(coef[0:1, TC:2 * TC], stat[0:1, 2 * TC:3 * TC], -1.0), "EV")
                    v.wait_ge(p.sem["MM"], mm0 + tc * r_mm + 2)
                    if tc >= 1:
                        _sm, _val = st_done[tc - 1]
                        v.wait_ge(p.sem[_sm], _val)
                    v.tensor_copy(out=abc[:], in_=ps[4][:, 0:TC])
                    p.vsync(v, v.tensor_copy(out=bbc[:], in_=ps[5][:, 0:TC]))
                    xv = xb[tc % 2][:].rearrange("p (k t) -> p k t", k=KD)
                    ov = onx[:].rearrange("p (k t) -> p k t", k=KD)
                    v.tensor_tensor(out=ov, in0=xv,
                                    in1=abc[:].unsqueeze(1).broadcast_to((128, KD, TC)), op=ALU.mult)
                    ii = v.tensor_tensor(out=ov, in0=ov,
                                         in1=bbc[:].unsqueeze(1).broadcast_to((128, KD, TC)), op=ALU.add)
                    if mode == "post":
                        ii = v.tensor_tensor(out=ov, in0=ov,
                                             in1=xob[tc % 2][:].rearrange("p (k t) -> p k t", k=KD),
                                             op=ALU.add)
                    p.inc(ii, "EV")

        def p_gemm(w, wkind, Kfeat, groups, rhs, evac, store_ap, out_f32=False, NPS=4, nrb=2):
            KC = Kfeat // 128
            s0 = p.snap()
            maxw = max(hi - lo for lo, hi in groups)
            wg = p.sb("gmw", [128, KC * maxw * 128], BF16)
            rb = [p.sb("gmr", [128, KC * TC], BF16) for _ in range(nrb)]
            ot = [p.sb("gmo", [128, TC], F32 if out_f32 else BF16) for _ in range(4)]
            mm0, es0, ev0, lgw0 = s0["MM"], s0["ES"], s0["EV"], s0["LGW"]
            NG = len(groups)
            units = []
            for g, (lo, hi) in enumerate(groups):
                for tc in range(NTC):
                    for j in range(hi - lo):
                        units.append((g, tc, j, lo + j))
            mm_of = {}
            last_of_g = {}
            n = mm0
            for g, (lo, hi) in enumerate(groups):
                for tc in range(NTC):
                    n += (hi - lo)
                    mm_of[(g, tc)] = n
                last_of_g[g] = n
            ld_rhs = {}
            st_done = {}

            nslab = {g: (groups[g][1] - groups[g][0]) for g in range(len(groups))}
            ibuf = [p.sb("gmib", [128, KC * 128], I32)] if wkind == "lut" else None
            acf = p.sb("gmac", [128, KC * 128], BF16) if wkind == "lut" else None
            tmf = p.sb("gmtm", [128, KC * 128], BF16) if wkind == "lut" else None
            slab_ld = {}
            nw_ev = {}
            slab_ev = {}
            if wkind == "lut":
                si = 0
                for g, (lo, hi) in enumerate(groups):
                    for j in range(hi - lo):
                        slab_ev[si] = ev0 + si + 1
                        si += 1

            @block.gpsimd
            def _(g_):
                p.bwait(g_, s0, skip=("LGW", "LG0", "LG1"))
                si = 0
                for g, (lo, hi) in enumerate(groups):
                    if g >= 1:
                        g_.wait_ge(p.sem["MM"], last_of_g[g - 1])
                        if wkind == "i32":
                            g_.wait_ge(p.sem["EV"], ev0 + g)
                    if wkind == "lut":
                        for j in range(hi - lo):
                            if si >= 1:
                                g_.wait_ge(p.sem["EV"], slab_ev[si - 1])
                            sem = "LG" + str(si % 2)
                            p.inc(g_.dma_start(
                                out=ibuf[0][:].rearrange("p (k m) -> p k m", k=KC),
                                in_=w.rearrange("(k q) m -> q k m", q=128)[:, :, (lo + j) * 128:(lo + j + 1) * 128]),
                                sem, 16)
                            slab_ld[si] = (sem, p.c[sem])
                            si += 1
                    else:
                        p.inc(g_.dma_start(
                            out=wg[:, 0:KC * (hi - lo) * 128].rearrange("p (k m) -> p k m", k=KC),
                            in_=w.rearrange("(k q) m -> q k m", q=128)[:, :, lo * 128:hi * 128]), "LGW", 16)

            if wkind == "i32":
                @block.vector
                def _(v):
                    p.bwait(v, s0, skip=("EV",))
                    for g, (lo, hi) in enumerate(groups):
                        v.wait_ge(p.sem["LGW"], lgw0 + 16 * (g + 1))
                        if g >= 1:
                            v.wait_ge(p.sem["MM"], last_of_g[g - 1])
                        p.inc(v.tensor_scalar_add(wg[:, 0:KC * (hi - lo) * 128],
                                                  wg[:, 0:KC * (hi - lo) * 128], -128.0), "EV")
                        nw_ev[g] = p.c["EV"]
            elif wkind == "lut":
                @block.vector
                def _(v):
                    p.bwait(v, s0, skip=("EV",))
                    si = 0
                    for g, (lo, hi) in enumerate(groups):
                        if g >= 1:
                            v.wait_ge(p.sem["MM"], last_of_g[g - 1])
                        for j in range(hi - lo):
                            sm, val = slab_ld[si]
                            v.wait_ge(p.sem[sm], val)
                            wsl = KC * 128
                            for vv in range(16):
                                dstv = acf[:] if vv == 0 else tmf[:]
                                v.tensor_scalar(out=dstv, in0=ibuf[0][:], scalar1=float(vv),
                                                scalar2=lutb_s[:, vv:vv + 1], op0=ALU.is_equal, op1=ALU.mult)
                                if vv > 0:
                                    v.tensor_tensor(out=acf[:], in0=acf[:], in1=tmf[:], op=ALU.add)
                            p.inc(v.tensor_copy(
                                out=wg[:, 0:KC * (hi - lo) * 128]
                                .rearrange("p (k m) -> p k m", k=KC)[:, :, j * 128:(j + 1) * 128],
                                in_=acf[:].rearrange("p (k m) -> p k m", k=KC)), "EV")
                            si += 1
                        nw_ev[g] = p.c["EV"]

            @block.sync
            def _(s):
                p.bwait(s, s0, skip=("LD0", "LD1", "ST0", "ST1", "ST2", "ST3"))
                i = 0
                for g, (lo, hi) in enumerate(groups):
                    for tc in range(NTC):
                        u = g * NTC + tc
                        if u >= nrb:
                            pg, ptc = divmod(u - nrb, NTC)
                            s.wait_ge(p.sem["MM"], mm_of[(pg, ptc)])
                        sem = "LD" + str(u % 2)
                        p.inc(s.dma_start(
                            out=rb[u % nrb][:].rearrange("p (k t) -> p k t", k=KC),
                            in_=rhs[:, :, tc * TC:(tc + 1) * TC].rearrange("k p t -> p k t")), sem, 16)
                        ld_rhs[(g, tc)] = (sem, p.c[sem])
                        for j in range(hi - lo):
                            s.wait_ge(p.sem["ES"], es0 + i + 1)
                            ssem = "ST" + str(i % 4)
                            p.inc(s.dma_start(out=store_ap(lo + j, tc), in_=ot[i % 4][:]), ssem, 16)
                            st_done[i] = (ssem, p.c[ssem])
                            i += 1

            @block.tensor
            def _(t):
                p.bwait(t, s0, skip=("MM",))
                i = 0
                for g, (lo, hi) in enumerate(groups):
                    if wkind == "f32":
                        t.wait_ge(p.sem["LGW"], lgw0 + 16 * (g + 1))
                    elif wkind == "i32":
                        t.wait_ge(p.sem["LGW"], lgw0 + 16 * (g + 1))
                        t.wait_ge(p.sem["EV"], nw_ev[g])
                    else:
                        t.wait_ge(p.sem["EV"], nw_ev[g])
                    for tc in range(NTC):
                        u = g * NTC + tc
                        sm, v = ld_rhs[(g, tc)]
                        t.wait_ge(p.sem[sm], v)
                        for j in range(hi - lo):
                            if i >= NPS:
                                t.wait_ge(p.sem["ES"], es0 + i - NPS + 1)
                            for k in range(KC):
                                mi = t.matmul(
                                    ps[i % NPS][:, 0:TC],
                                    wg[:, (k * (hi - lo) + j) * 128:(k * (hi - lo) + j + 1) * 128],
                                    rb[u % nrb][:, k * TC:(k + 1) * TC],
                                    start=(k == 0), stop=(k == KC - 1))
                            p.inc(mi, "MM")
                            i += 1

            @block.scalar
            def _(sc):
                p.bwait(sc, s0, skip=("ES",))
                for i, (g, tc, j, m) in enumerate(units):
                    sc.wait_ge(p.sem["MM"], mm0 + i + 1)
                    if i >= 4:
                        sm, v = st_done[i - 4]
                        sc.wait_ge(p.sem[sm], v)
                    p.inc(evac(m, ps[i % NPS][:, 0:TC], ot[i % 4][:], sc), "ES")

        def p_gemm_dual(wg_dram, wu_dram, wkind, Kfeat, groups, rhs, hdst, scg=None, scu=None):
            KC = Kfeat // 128
            s0 = p.snap()
            maxw = max(hi - lo for lo, hi in groups)
            wga = p.sb("duwa", [128, KC * maxw * 128], BF16)
            wgb = p.sb("duwb", [128, KC * maxw * 128], BF16)
            rb = [p.sb("durb", [128, KC * TC], BF16) for _ in range(2)]
            gt = [p.sb("dugt", [128, TC], BF16) for _ in range(2)]
            ut = p.sb("duut", [128, TC], BF16)
            ot = [p.sb("duot", [128, TC], BF16) for _ in range(4)]
            ibuf = [p.sb("duib", [128, KC * 128], I32) for _ in range(2)] if wkind == "lut" else None
            acf = p.sb("duac", [128, KC * 128], BF16) if wkind == "lut" else None
            tmf = p.sb("dutm", [128, KC * 128], BF16) if wkind == "lut" else None
            mm0, es0, ev0, lgw0 = s0["MM"], s0["ES"], s0["EV"], s0["LGW"]
            NG = len(groups)
            units = []
            for g, (lo, hi) in enumerate(groups):
                for tc in range(NTC):
                    for j in range(hi - lo):
                        units.append((g, tc, j, lo + j))
            mm_of, last_of_g = {}, {}
            n = mm0
            for g, (lo, hi) in enumerate(groups):
                for tc in range(NTC):
                    n += (hi - lo)
                    mm_of[(g, tc)] = n
                last_of_g[g] = n
            nw_ev = {}
            slab_ev = {}
            if wkind == "lut":
                si = 0
                for g, (lo, hi) in enumerate(groups):
                    si += 2 * (hi - lo)
                    nw_ev[g] = ev0 + si
            ev_unit = {}
            n = ev0
            si = 0
            for g, (lo, hi) in enumerate(groups):
                if wkind == "lut":
                    n += 2 * (hi - lo)
                else:
                    n += 2
                nw_ev[g] = n
                for tc in range(NTC):
                    for j in range(hi - lo):
                        n += 1
                        ev_unit[si] = n
                        si += 1
            ld_rhs, st_done, slab_ld = {}, {}, {}
            dq_ev = {}
            if wkind == "lut":
                n = ev0
                si = 0
                for g, (lo, hi) in enumerate(groups):
                    for j in range(2 * (hi - lo)):
                        n += 1
                        dq_ev[si] = n
                        si += 1
                    n += NTC * (hi - lo)

            @block.gpsimd
            def _(g_):
                p.bwait(g_, s0, skip=("LGW", "LG0", "LG1"))
                si = 0
                for g, (lo, hi) in enumerate(groups):
                    if g >= 1:
                        g_.wait_ge(p.sem["MM"], last_of_g[g - 1])
                        g_.wait_ge(p.sem["EV"], nw_ev[g - 1])
                    if wkind == "lut":
                        for wdram, wbuf in ((wg_dram, wga), (wu_dram, wgb)):
                            for j in range(hi - lo):
                                if si >= 2:
                                    g_.wait_ge(p.sem["EV"], dq_ev[si - 2])
                                sem = "LG" + str(si % 2)
                                p.inc(g_.dma_start(
                                    out=ibuf[si % 2][:].rearrange("p (k m) -> p k m", k=KC),
                                    in_=wdram.rearrange("(k q) m -> q k m", q=128)
                                    [:, :, (lo + j) * 128:(lo + j + 1) * 128]), sem, 16)
                                slab_ld[si] = (sem, p.c[sem])
                                si += 1
                    else:
                        p.inc(g_.dma_start(
                            out=wga[:, 0:KC * (hi - lo) * 128].rearrange("p (k m) -> p k m", k=KC),
                            in_=wg_dram.rearrange("(k q) m -> q k m", q=128)[:, :, lo * 128:hi * 128]),
                            "LGW", 16)
                        p.inc(g_.dma_start(
                            out=wgb[:, 0:KC * (hi - lo) * 128].rearrange("p (k m) -> p k m", k=KC),
                            in_=wu_dram.rearrange("(k q) m -> q k m", q=128)[:, :, lo * 128:hi * 128]),
                            "LGW", 16)

            @block.sync
            def _(s):
                p.bwait(s, s0, skip=("LD0", "LD1", "ST0", "ST1", "ST2", "ST3"))
                i = 0
                for g, (lo, hi) in enumerate(groups):
                    for tc in range(NTC):
                        u = g * NTC + tc
                        if u >= 2:
                            pg, ptc = divmod(u - 2, NTC)
                            s.wait_ge(p.sem["MM"], mm_of[(pg, ptc)])
                        sem = "LD" + str(u % 2)
                        p.inc(s.dma_start(
                            out=rb[u % 2][:].rearrange("p (k t) -> p k t", k=KC),
                            in_=rhs[:, :, tc * TC:(tc + 1) * TC].rearrange("k p t -> p k t")), sem, 16)
                        ld_rhs[(g, tc)] = (sem, p.c[sem])
                        for j in range(hi - lo):
                            s.wait_ge(p.sem["EV"], ev_unit[i])
                            ssem = "ST" + str(i % 4)
                            p.inc(s.dma_start(out=hdst[lo + j, :, tc * TC:(tc + 1) * TC],
                                              in_=ot[i % 4][:]), ssem, 16)
                            st_done[i] = (ssem, p.c[ssem])
                            i += 1

            @block.vector
            def _(v):
                p.bwait(v, s0, skip=("EV",))
                si = 0
                ui = 0
                for g, (lo, hi) in enumerate(groups):
                    if wkind == "lut":
                        if g >= 1:
                            v.wait_ge(p.sem["MM"], last_of_g[g - 1])
                        for wi, wbuf in enumerate((wga, wgb)):
                            for j in range(hi - lo):
                                sm, val = slab_ld[si]
                                v.wait_ge(p.sem[sm], val)
                                for vv in range(16):
                                    dstv = acf[:] if vv == 0 else tmf[:]
                                    v.tensor_scalar(out=dstv, in0=ibuf[si % 2][:], scalar1=float(vv),
                                                    scalar2=lutb_s[:, vv:vv + 1],
                                                    op0=ALU.is_equal, op1=ALU.mult)
                                    if vv > 0:
                                        v.tensor_tensor(out=acf[:], in0=acf[:], in1=tmf[:], op=ALU.add)
                                p.inc(v.tensor_copy(
                                    out=wbuf[:, 0:KC * (hi - lo) * 128]
                                    .rearrange("p (k m) -> p k m", k=KC)[:, :, j * 128:(j + 1) * 128],
                                    in_=acf[:].rearrange("p (k m) -> p k m", k=KC)), "EV")
                                si += 1
                    else:
                        v.wait_ge(p.sem["LGW"], lgw0 + 32 * (g + 1))
                        if g >= 1:
                            v.wait_ge(p.sem["MM"], last_of_g[g - 1])
                        p.inc(v.tensor_scalar_add(wga[:, 0:KC * (hi - lo) * 128],
                                                  wga[:, 0:KC * (hi - lo) * 128], -128.0), "EV")
                        p.inc(v.tensor_scalar_add(wgb[:, 0:KC * (hi - lo) * 128],
                                                  wgb[:, 0:KC * (hi - lo) * 128], -128.0), "EV")
                    for tc in range(NTC):
                        for j in range(hi - lo):
                            m = lo + j
                            v.wait_ge(p.sem["MM"], mm0 + ui + 1)
                            v.wait_ge(p.sem["ES"], es0 + ui + 1)
                            if scu is not None:
                                p.vsync(v, v.tensor_scalar_mul(ut[:], ps[(2 * ui + 1) % 4][:, 0:TC],
                                                               scu[:, m:m + 1]))
                            else:
                                p.vsync(v, v.tensor_copy(out=ut[:], in_=ps[(2 * ui + 1) % 4][:, 0:TC]))
                            if ui >= 4:
                                sm, val = st_done[ui - 4]
                                v.wait_ge(p.sem[sm], val)
                            p.inc(v.tensor_tensor(out=ot[ui % 4][:], in0=gt[ui % 2][:], in1=ut[:],
                                                  op=ALU.mult), "EV")
                            ui += 1

            @block.tensor
            def _(t):
                p.bwait(t, s0, skip=("MM",))
                i = 0
                for g, (lo, hi) in enumerate(groups):
                    if wkind != "lut":
                        t.wait_ge(p.sem["LGW"], lgw0 + 32 * (g + 1))
                    t.wait_ge(p.sem["EV"], nw_ev[g])
                    for tc in range(NTC):
                        u = g * NTC + tc
                        sm, val = ld_rhs[(g, tc)]
                        t.wait_ge(p.sem[sm], val)
                        for j in range(hi - lo):
                            if i >= 2:
                                t.wait_ge(p.sem["ES"], es0 + (i - 2) + 1)
                                t.wait_ge(p.sem["EV"], ev_unit[i - 2])
                            for k in range(KC):
                                t.matmul(ps[(2 * i) % 4][:, 0:TC],
                                         wga[:, (k * (hi - lo) + j) * 128:(k * (hi - lo) + j + 1) * 128],
                                         rb[u % 2][:, k * TC:(k + 1) * TC],
                                         start=(k == 0), stop=(k == KC - 1))
                            for k in range(KC):
                                mi = t.matmul(ps[(2 * i + 1) % 4][:, 0:TC],
                                              wgb[:, (k * (hi - lo) + j) * 128:(k * (hi - lo) + j + 1) * 128],
                                              rb[u % 2][:, k * TC:(k + 1) * TC],
                                              start=(k == 0), stop=(k == KC - 1))
                            p.inc(mi, "MM")
                            i += 1

            @block.scalar
            def _(sc):
                p.bwait(sc, s0, skip=("ES",))
                for i, (g, tc, j, m) in enumerate(units):
                    sc.wait_ge(p.sem["MM"], mm0 + i + 1)
                    if i >= 2:
                        sc.wait_ge(p.sem["EV"], ev_unit[i - 2])
                    if scg is not None:
                        ii = sc.activation(out=gt[i % 2][:], in_=ps[(2 * i) % 4][:, 0:TC],
                                           func=AF.Silu, scale=scg[:, m:m + 1])
                    else:
                        ii = sc.activation(out=gt[i % 2][:], in_=ps[(2 * i) % 4][:, 0:TC], func=AF.Silu)
                    p.inc(ii, "ES")

        def p_vtm():
            s0 = p.snap()
            mm0, es0 = s0["MM"], s0["ES"]
            wv = p.sb("vtw", [128, KD * 2048], BF16)
            rb = [p.sb("vtr", [128, KD * TC], BF16) for _ in range(2)]
            ot = [p.sb("vto", [128, 512], BF16) for _ in range(4)]
            lgw0 = s0["LGW"]
            ld_rhs, st_done = {}, {}
            TSC = TC // 128
            units = []
            for tc in range(NTC):
                for tsc in range(TSC):
                    for nb in range(4):
                        units.append((tc, tsc, nb))

            @block.gpsimd
            def _(g_):
                p.bwait(g_, s0, skip=("LGW",))
                p.inc(g_.dma_start(
                    out=wv[:].rearrange("p (k m) -> p k m", k=KD),
                    in_=w_qkv.rearrange("(k q) m -> q k m", q=128)[:, :, 4096:6144]), "LGW", 16)

            @block.sync
            def _(s):
                p.bwait(s, s0, skip=("LD0", "LD1", "ST0", "ST1", "ST2", "ST3"))
                i = 0
                for tc in range(NTC):
                    if tc >= 2:
                        s.wait_ge(p.sem["MM"], mm0 + (tc - 1) * TSC * 4)
                    sem = "LD" + str(tc % 2)
                    p.inc(s.dma_start(
                        out=rb[tc % 2][:].rearrange("p (k t) -> p k t", k=KD),
                        in_=lnx[:, :, tc * TC:(tc + 1) * TC].rearrange("k p t -> p k t")), sem, 16)
                    ld_rhs[tc] = (sem, p.c[sem])
                    for tsc in range(TSC):
                        for nb in range(4):
                            s.wait_ge(p.sem["ES"], es0 + i + 1)
                            ssem = "ST" + str(i % 4)
                            kb = tc * TSC + tsc
                            for j in range(4):
                                head = nb * 4 + j
                                p.inc(s.dma_start(out=vhm[head, kb],
                                                  in_=ot[i % 4][:, j * 128:(j + 1) * 128]), ssem, 16)
                            st_done[i] = (ssem, p.c[ssem])
                            i += 1

            @block.tensor
            def _(t):
                p.bwait(t, s0, skip=("MM",))
                t.wait_ge(p.sem["LGW"], lgw0 + 16)
                i = 0
                for tc in range(NTC):
                    sm, val = ld_rhs[tc]
                    t.wait_ge(p.sem[sm], val)
                    for tsc in range(TSC):
                        for nb in range(4):
                            if i >= 4:
                                t.wait_ge(p.sem["ES"], es0 + (i - 4) + 1)
                            for k in range(KD):
                                mi = t.matmul(
                                    ps[i % 4][:, 0:512],
                                    rb[tc % 2][:, k * TC + tsc * 128:k * TC + tsc * 128 + 128],
                                    wv[:, k * 2048 + nb * 512:k * 2048 + (nb + 1) * 512],
                                    start=(k == 0), stop=(k == KD - 1))
                            p.inc(mi, "MM")
                            i += 1

            @block.scalar
            def _(sc):
                p.bwait(sc, s0, skip=("ES",))
                for i in range(len(units)):
                    sc.wait_ge(p.sem["MM"], mm0 + i + 1)
                    if i >= 4:
                        sm, val = st_done[i - 4]
                        sc.wait_ge(p.sem[sm], val)
                    p.inc(sc.copy(out=ot[i % 4][:], in_=ps[i % 4][:, 0:512]), "ES")

        def p_attn():
            s0 = p.snap()
            NQB = S // 512
            adbg = getattr(p, "attn_dbg", False)
            if adbg:
                dbg_l = p.dram("dbg_l", [32, 128, 4], F32)
                dbg_p = p.dram("dbg_p", [8, 128, 512], BF16)
            qh = p.sb("atq", [128, S], BF16)
            kh = p.sb("atk", [128, S], BF16)
            vh = p.sb("atv", [128, (S // 128) * 128], BF16)
            Ssb = p.sb("atS", [128, S], F32)
            Psb = p.sb("atP", [128, S], BF16)
            Pt = p.sb("atPt", [128, (S // 128) * 512], BF16)
            mx = p.sb("atmx", [128, 1], F32)
            nb_ = p.sb("atnb", [128, 1], F32)
            lsum = p.sb("atl", [128, 1], F32)
            lsum2 = p.sb("atl2", [128, 1], F32)
            linv = p.sb("atli", [128, 1], F32)
            ot = [p.sb("atot", [128, 512], BF16) for _ in range(4)]

            marks = {}
            cn = {"MM": s0["MM"], "ES": s0["ES"], "EV": s0["EV"], "EG": s0["EG"]}

            def walk(record):
                for h in range(NHEAD):
                    record("loads", h)
                    for qb in range(NQB):
                        record("ptzero", (h, qb))
                        for qc4 in range(4):
                            qci = qb * 4 + qc4
                            for kb5 in range(qb + 1):
                                record("score", (h, qci, kb5))
                                record("scev", (h, qci, kb5))
                            record("mask", (h, qci))
                            record("rmax", (h, qci))
                            record("exp", (h, qci))
                            record("recip", (h, qci))
                            record("pscale", (h, qci))
                            ntr = qci + 1
                            for kg in range((ntr + 3) // 4):
                                record("trans", (h, qci, kg))
                                record("trev", (h, qci, kg))
                        record("av", (h, qb))
                        record("avev", (h, qb))
                        record("store", (h, qb))

            incs = {"loads": None, "store": None, "score": "MM", "scev": "ES", "mask": "EG",
                    "rmax": "EV", "exp": "ES", "recip": "EV", "pscale": "EV", "trans": "MM",
                    "trev": "ES", "av": "MM", "avev": "ES", "ptzero": "EV"}

            def rec(kind, key):
                sm = incs[kind]
                if sm is not None:
                    cn[sm] += 1
                marks[(kind, key)] = dict(cn)
            walk(rec)
            tot = dict(cn)
            ld_done, st_done, dbg_st = {}, {}, {}

            @block.sync
            def _(s):
                p.bwait(s, s0, skip=("LD0", "LD1", "ST0", "ST1", "ST2", "ST3"))
                for h in range(NHEAD):
                    if h >= 1:
                        s.wait_ge(p.sem["MM"], marks[("av", (h - 1, NQB - 1))]["MM"])
                    sem = "LD" + str(h % 2)
                    p.inc(s.dma_start(out=qh[:], in_=qf[h]), sem, 16)
                    p.inc(s.dma_start(out=kh[:], in_=kfd[h]), sem, 16)
                    p.inc(s.dma_start(out=vh[:].rearrange("p (kb d) -> p kb d", d=128),
                                      in_=vhm[h].rearrange("kb p d -> p kb d")), sem, 16)
                    ld_done[h] = (sem, p.c[sem])
                    for qb in range(NQB):
                        if adbg and h == 0:
                            for qc4 in range(4):
                                qci = qb * 4 + qc4
                                s.wait_ge(p.sem["EV"], marks[("pscale", (h, qci))]["EV"])
                                if qci < 32:
                                    p.inc(s.dma_start(allow_slow_non_contiguous=True, out=dbg_l[qci, :, 0:1], in_=nb_[:, 0:1]), "ST0", 16)
                                    p.inc(s.dma_start(allow_slow_non_contiguous=True, out=dbg_l[qci, :, 1:2], in_=nb_[:, 0:1]), "ST0", 16)
                                    p.inc(s.dma_start(allow_slow_non_contiguous=True, out=dbg_l[qci, :, 2:3], in_=lsum[:, 0:1]), "ST0", 16)
                                    p.inc(s.dma_start(allow_slow_non_contiguous=True, out=dbg_l[qci, :, 3:4], in_=lsum2[:, 0:1]), "ST0", 16)
                                if qci < 8:
                                    p.inc(s.dma_start(out=dbg_p[qci], in_=Psb[:, 0:512]), "ST0", 16)
                                dbg_st[qci] = p.c["ST0"]
                        s.wait_ge(p.sem["ES"], marks[("avev", (h, qb))]["ES"])
                        ssem = "ST" + str((h * NQB + qb) % 4)
                        p.inc(s.dma_start(out=ao[h, :, qb * 512:(qb + 1) * 512],
                                          in_=ot[(h * NQB + qb) % 4][:]), ssem, 16)
                        st_done[(h, qb)] = (ssem, p.c[ssem])

            @block.tensor
            def _(t):
                p.bwait(t, s0, skip=("MM",))
                for h in range(NHEAD):
                    sm, val = ld_done[h]
                    t.wait_ge(p.sem[sm], val)
                    for qb in range(NQB):
                        for qc4 in range(4):
                            qci = qb * 4 + qc4
                            for kb5 in range(qb + 1):
                                nw = 512 if kb5 < qb else (qc4 + 1) * 128
                                if (h, qci, kb5) != (0, 0, 0):
                                    t.wait_ge(p.sem["ES"], max(s0["ES"],
                                              marks[("scev", (h, qci, kb5))]["ES"] - 2))
                                p.inc(t.matmul(ps[(kb5 + qci) % 2][:, 0:nw],
                                               qh[:, qci * 128:(qci + 1) * 128],
                                               kh[:, kb5 * 512:kb5 * 512 + nw],
                                               start=True, stop=True), "MM")
                            t.wait_ge(p.sem["EV"], marks[("pscale", (h, qci))]["EV"])
                            ntr = qci + 1
                            for kg in range((ntr + 3) // 4):
                                wdt = min(4, ntr - kg * 4)
                                t.wait_ge(p.sem["ES"], max(s0["ES"],
                                          marks[("trev", (h, qci, kg))]["ES"] - 2))
                                for jj in range(wdt):
                                    kb = kg * 4 + jj
                                    mi = t.matmul(ps[4 + kg % 2][:, jj * 128:(jj + 1) * 128],
                                                  Psb[:, kb * 128:(kb + 1) * 128], ident[:],
                                                  start=True, stop=True)
                                p.inc(mi, "MM")
                        t.wait_ge(p.sem["ES"], marks[("trev", (h, qb * 4 + 3,
                                  (qb * 4 + 3) // 4))]["ES"])
                        nkb = 4 * qb + 4
                        fqb = h * NQB + qb
                        if fqb >= 2:
                            pk = fqb - 2
                            t.wait_ge(p.sem["ES"], marks[("avev", (pk // NQB, pk % NQB))]["ES"])
                        for kb in range(nkb):
                            mi = t.matmul(ps[6 + fqb % 2][:, 0:512],
                                          vh[:, kb * 128:(kb + 1) * 128],
                                          Pt[:, kb * 512:(kb + 1) * 512],
                                          start=(kb == 0), stop=(kb == nkb - 1))
                        p.inc(mi, "MM")

            @block.scalar
            def _(sc):
                p.bwait(sc, s0, skip=("ES",))
                for h in range(NHEAD):
                    for qb in range(NQB):
                        for qc4 in range(4):
                            qci = qb * 4 + qc4
                            for kb5 in range(qb + 1):
                                nw = 512 if kb5 < qb else (qc4 + 1) * 128
                                sc.wait_ge(p.sem["MM"], marks[("score", (h, qci, kb5))]["MM"])
                                p.inc(sc.copy(out=Ssb[:, kb5 * 512:kb5 * 512 + nw],
                                              in_=ps[(kb5 + qci) % 2][:, 0:nw]), "ES")
                            sc.wait_ge(p.sem["EV"], marks[("rmax", (h, qci))]["EV"])
                            if (h, qci) != (0, 0):
                                pq = (h, qci - 1) if qci >= 1 else (h - 1, 4 * NQB - 1)
                                pntr = pq[1] + 1
                                sc.wait_ge(p.sem["MM"],
                                           marks[("trans", (pq[0], pq[1], (pntr - 1) // 4))]["MM"])
                            ln_ = (qci + 1) * 128
                            if adbg and h == 0 and 1 <= qci <= 32:
                                sc.wait_ge(p.sem["ST0"], dbg_st[qci - 1])
                            sc.activation(out=Psb[:, 0:ln_], in_=Ssb[:, 0:ln_], func=AF.Exp,
                                          scale=1.0, bias=nb_[:, 0:1],
                                          accum_out=lsum[:, 0:1])
                            p.inc(sc.copy(out=lsum2[:, 0:1], in_=lsum[:, 0:1]), "ES")
                            ntr = qci + 1
                            for kg in range((ntr + 3) // 4):
                                wdt = min(4, ntr - kg * 4)
                                sc.wait_ge(p.sem["MM"], marks[("trans", (h, qci, kg))]["MM"])
                                p.inc(sc.copy(
                                    out=Pt[:, 0:(S // 128) * 512]
                                    .rearrange("p (kb q) -> p kb q", q=512)
                                    [:, kg * 4:kg * 4 + wdt, qc4 * 128:(qc4 + 1) * 128],
                                    in_=ps[4 + kg % 2][:, 0:wdt * 128]
                                    .rearrange("p (w q) -> p w q", q=128)), "ES")
                        sc.wait_ge(p.sem["MM"], marks[("av", (h, qb))]["MM"])
                        fqb = h * NQB + qb
                        i4 = fqb % 4
                        if fqb >= 4:
                            pk = fqb - 4
                            sm, val = st_done[(pk // NQB, pk % NQB)]
                            sc.wait_ge(p.sem[sm], val)
                        p.inc(sc.copy(out=ot[i4][:], in_=ps[6 + fqb % 2][:, 0:512]), "ES")

            @block.gpsimd
            def _(g_):
                p.bwait(g_, s0, skip=("EG",))
                fillreg = g_.to_reg(NEG)
                for h in range(NHEAD):
                    for qb in range(NQB):
                        for qc4 in range(4):
                            qci = qb * 4 + qc4
                            g_.wait_ge(p.sem["ES"], marks[("scev", (h, qci, qb))]["ES"])
                            p.inc(g_.affine_select(
                                out=Ssb[:, qci * 128:(qci + 1) * 128],
                                in_=Ssb[:, qci * 128:(qci + 1) * 128],
                                pattern=[[-1, 128]], compare_op=ALU.is_ge, fill=fillreg,
                                base=0, channel_multiplier=1), "EG")

            @block.vector
            def _(v):
                p.bwait(v, s0, skip=("EV",))
                for h in range(NHEAD):
                    for qb in range(NQB):
                        if (h, qb) != (0, 0):
                            pqb = qb - 1 if qb >= 1 else NQB - 1
                            ph = h if qb >= 1 else h - 1
                            v.wait_ge(p.sem["MM"], marks[("av", (ph, pqb))]["MM"])
                        p.inc(v.memset(Pt[:, qb * 4 * 512:(qb + 1) * 4 * 512], 0.0), "EV")
                        for qc4 in range(4):
                            qci = qb * 4 + qc4
                            v.wait_ge(p.sem["EG"], marks[("mask", (h, qci))]["EG"])
                            if adbg and h == 0 and 1 <= qci <= 32:
                                v.wait_ge(p.sem["ST0"], dbg_st[qci - 1])
                            ln_ = (qci + 1) * 128
                            p.inc(v.reduce_max(out=nb_[:, 0:1], in_=Ssb[:, 0:ln_],
                                               axis=mybir.AxisListType.X, negate=True), "EV")
                            v.wait_ge(p.sem["ES"], marks[("exp", (h, qci))]["ES"])
                            p.inc(v.reciprocal(out=linv[:, 0:1], in_=lsum2[:, 0:1]), "EV")
                            v.wait_ge(p.sem["EV"], marks[("recip", (h, qci))]["EV"])
                            p.inc(v.tensor_scalar_mul(Psb[:, 0:ln_], Psb[:, 0:ln_],
                                                      linv[:, 0:1]), "EV")

        def p_conv(xraw):
            s0 = p.snap()
            es0, ev0 = s0["ES"], s0["EV"]
            cv = [p.sb("cvb", [128, TC + 3], BF16) for _ in range(2)]
            t1 = [p.sb("cvt", [128, TC], F32) for _ in range(2)]
            ot = [p.sb("cvo", [128, TC], BF16) for _ in range(4)]
            ld_done, st_done = {}, {}
            NU = 32 * NTC

            @block.sync
            def _(s):
                p.bwait(s, s0, skip=("LD0", "LD1", "ST0", "ST1", "ST2", "ST3"))
                for i in range(NU):
                    cb, tc = divmod(i, NTC)
                    if i >= 2:
                        s.wait_ge(p.sem["ES"], es0 + (i - 2) + 1)
                    sem = "LD" + str(i % 2)
                    if tc == 0:
                        p.inc(s.dma_start(out=cv[i % 2][:, 3:3 + TC],
                                          in_=xraw[cb, :, 0:TC]), sem, 16)
                    else:
                        p.inc(s.dma_start(out=cv[i % 2][:, 0:TC + 3],
                                          in_=xraw[cb, :, tc * TC - 3:(tc + 1) * TC]), sem, 16)
                    ld_done[i] = (sem, p.c[sem])
                    s.wait_ge(p.sem["ES"], es0 + i + 1)
                    ssem = "ST" + str(i % 4)
                    p.inc(s.dma_start(out=xssm[cb, :, tc * TC:(tc + 1) * TC], in_=ot[i % 4][:]), ssem, 16)
                    st_done[i] = (ssem, p.c[ssem])

            @block.vector
            def _(v):
                p.bwait(v, s0, skip=("EV",))
                for i in range(NU):
                    cb, tc = divmod(i, NTC)
                    if tc == 0:
                        if i >= 2:
                            v.wait_ge(p.sem["ES"], es0 + (i - 2) + 1)
                        v.memset(cv[i % 2][:, 0:3], 0.0)
                    sm, val = ld_done[i]
                    v.wait_ge(p.sem[sm], val)
                    c = cv[i % 2]
                    v.tensor_scalar_mul(t1[i % 2][:], c[:, 3:3 + TC], convw_s[:, cb * 4 + 3:cb * 4 + 4])
                    for j in (2, 1, 0):
                        ii = v.scalar_tensor_tensor(out=t1[i % 2][:], in0=c[:, j:j + TC],
                                                    scalar=convw_s[:, cb * 4 + j:cb * 4 + j + 1],
                                                    in1=t1[i % 2][:], op0=ALU.mult, op1=ALU.add)
                    p.inc(ii, "EV")

            @block.scalar
            def _(sc):
                p.bwait(sc, s0, skip=("ES",))
                for i in range(NU):
                    cb, tc = divmod(i, NTC)
                    sc.wait_ge(p.sem["EV"], ev0 + i + 1)
                    if i >= 4:
                        sm, val = st_done[i - 4]
                        sc.wait_ge(p.sem[sm], val)
                    p.inc(sc.activation(out=ot[i % 4][:], in_=t1[i % 2][:], func=AF.Silu,
                                        bias=convb_s[:, cb:cb + 1]), "ES")

        def p_scan():
            s0 = p.snap()
            es0, ev0, mm0, eg0 = s0["ES"], s0["EV"], s0["MM"], s0["EG"]
            xsb = p.sb("scxs", [128, KD * TC], BF16)
            gsb = p.sb("scgs", [128, KD * TC], BF16)
            dtin = p.sb("scdt", [128, TC], BF16)
            bcb = p.sb("scbc", [32, TC], BF16)
            bcbuf = p.sb("scbb", [128, 32 * TC], BF16)
            dtf = [p.sb("scdf", [128, TC], F32) for _ in range(2)]
            spr = p.sb("scsp", [128, TC], F32)
            spt = p.sb("scspt", [128, TC], F32)
            dab = [p.sb("scda", [128, NST * TC], F32) for _ in range(2)]
            uf = p.sb("scu", [128, TC], F32)
            dbu = p.sb("scdb", [128, TC], F32)
            stb = [p.sb("scst", [128, NST * TC], BF16) for _ in range(2)]
            ytm = p.sb("scyt", [128, TC], BF16)
            gty = p.sb("scgy", [128, TC], BF16)
            yacv = [p.sb("scya", [128, TC], F32) for _ in range(2)]
            yag = p.sb("scyag", [128, TC], F32)
            uf2 = p.sb("scu2", [128, TC], F32)
            ygb = p.sb("scyg", [128, KD * TC], BF16)
            VSET = list(range(6))
            GSET = [n for n in range(NST) if n not in VSET]
            vs_mark = {}
            ld_done, st_done = {}, {}
            NU = NTC * NH
            r_mm = 2 + 32 + 16
            r_es = 2 + 32 + 32
            r_ev = 32

            def mm_at(i, stage, n=0):
                base = mm0 + i * r_mm
                if stage == "xp":
                    return base + 2
                if stage == "bc":
                    return base + 2 + n + 1
                return base + 2 + 32 + n + 1

            def es_at(i, stage, n=0):
                base = es0 + i * r_es
                if stage == "io":
                    return base + 2
                if stage == "bc":
                    return base + 2 + n + 1
                if stage == "db_a":
                    return base + 2 + 32 + 2 * n + 1
                return base + 2 + 32 + 2 * n + 2

            def ev_at(i, db, end=True):
                return ev0 + i * r_ev + 2 * db + (2 if end else 1)

            def eg_at(i, db):
                return eg0 + i * 16 + db + 1

            @block.sync
            def _(s):
                p.bwait(s, s0, skip=("LD0", "LD1", "ST0", "ST1", "ST2", "ST3"))
                for i in range(NU):
                    tc, h = divmod(i, NH)
                    if i >= 1:
                        s.wait_ge(p.sem["EG"], eg_at(i - 1, 15))
                    sem = "LD" + str(i % 2)
                    p.inc(s.dma_start(out=xsb[:].rearrange("p (c t) -> p c t", c=KD),
                                      in_=xssm[h * KD:(h + 1) * KD, :, tc * TC:(tc + 1) * TC]
                                      .rearrange("c p t -> p c t")), sem, 16)
                    p.inc(s.dma_start(out=gsb[:].rearrange("p (c t) -> p c t", c=KD),
                                      in_=gsil[h * KD:(h + 1) * KD, :, tc * TC:(tc + 1) * TC]
                                      .rearrange("c p t -> p c t")), sem, 16)
                    ld_done[i] = (sem, p.c[sem])
                    s.wait_ge(p.sem["EG"], eg_at(i, 15))
                    ssem = "ST" + str(i % 4)
                    p.inc(s.dma_start(out=ygat[h * KD:(h + 1) * KD, :, tc * TC:(tc + 1) * TC]
                                      .rearrange("c p t -> p c t"),
                                      in_=ygb[:].rearrange("p (c t) -> p c t", c=KD)), ssem, 16)
                    st_done[i] = (ssem, p.c[ssem])

            @block.tensor
            def _(t):
                p.bwait(t, s0, skip=("MM",))
                for i in range(NU):
                    tc, h = divmod(i, NH)
                    sm, val = ld_done[i]
                    t.wait_ge(p.sem[sm], val)
                    if i >= 1:
                        t.wait_ge(p.sem["ES"], es_at(i - 1, "db_a", 15))
                    for k in range(KD):
                        mi = t.matmul(ps[2][:, 0:TC],
                                      w_xp_s[:, ((h * KD + k) * 160):((h * KD + k) * 160 + 128)],
                                      xsb[:, k * TC:(k + 1) * TC], start=(k == 0), stop=(k == KD - 1))
                    p.inc(mi, "MM")
                    for k in range(KD):
                        mi = t.matmul(ps[3][0:32, 0:TC],
                                      w_xp_s[:, ((h * KD + k) * 160 + 128):((h * KD + k) * 160 + 160)],
                                      xsb[:, k * TC:(k + 1) * TC], start=(k == 0), stop=(k == KD - 1))
                    p.inc(mi, "MM")
                    t.wait_ge(p.sem["ES"], es_at(i, "io"))
                    for n in range(32):
                        if n >= 2:
                            t.wait_ge(p.sem["ES"], es_at(i, "bc", n - 2))
                        p.inc(t.matmul(ps[4 + n % 2][:, 0:TC], sel32[:, n * 128:(n + 1) * 128],
                                       bcb[:], start=True, stop=True), "MM")
                    for db in range(16):
                        if db >= 2:
                            t.wait_ge(p.sem["ES"], es_at(i, "db_a", db - 2))
                        p.inc(t.matmul(ps[db % 2][:, 0:TC],
                                       w_dt_s[:, (h * 16 + db) * 128:(h * 16 + db + 1) * 128],
                                       dtin[:], start=True, stop=True), "MM")

            @block.scalar
            def _(sc):
                p.bwait(sc, s0, skip=("ES",))
                for i in range(NU):
                    tc, h = divmod(i, NH)
                    sc.wait_ge(p.sem["MM"], mm_at(i, "xp"))
                    p.inc(sc.copy(out=dtin[:], in_=ps[2][:, 0:TC]), "ES")
                    p.inc(sc.copy(out=bcb[:], in_=ps[3][0:32, 0:TC]), "ES")
                    for n in range(32):
                        sc.wait_ge(p.sem["MM"], mm_at(i, "bc", n))
                        if i >= 1 and n == 0:
                            sc.wait_ge(p.sem["EG"], eg_at(i - 1, 15))
                        p.inc(sc.copy(out=bcbuf[:, n * TC:(n + 1) * TC], in_=ps[4 + n % 2][:, 0:TC]), "ES")
                    for db in range(16):
                        sc.wait_ge(p.sem["MM"], mm_at(i, "dt", db))
                        if db >= 2:
                            sc.wait_ge(p.sem["EV"], ev_at(i, db - 2))
                        bb = dtb_s[:, h * 16 + db:h * 16 + db + 1]
                        sc.activation(out=spr[:], in_=ps[db % 2][:, 0:TC], func=AF.Relu, bias=bb)
                        sc.activation(out=spt[:], in_=ps[db % 2][:, 0:TC], func=AF.Abs, bias=bb)
                        sc.activation(out=spt[:], in_=spt[:], func=AF.Exp, scale=-1.0)
                        p.inc(sc.activation(out=spt[:], in_=spt[:], func=AF.Ln, bias=1.0), "ES")
                        sc.wait_ge(p.sem["EV"], ev_at(i, db, end=False))
                        for n in range(NST):
                            ii = sc.activation(out=dab[db % 2][:, n * TC:(n + 1) * TC], in_=dtf[db % 2][:],
                                               func=AF.Exp,
                                               scale=a_s[:, (h * 16 + db) * NST + n:(h * 16 + db) * NST + n + 1])
                        p.inc(ii, "ES")

            @block.vector
            def _(v):
                p.bwait(v, s0, skip=("EV",))
                for i in range(NU):
                    tc, h = divmod(i, NH)
                    for db in range(16):
                        v.wait_ge(p.sem["ES"], es_at(i, "db_a", db))
                        p.inc(v.tensor_tensor(out=dtf[db % 2][:], in0=spr[:], in1=spt[:], op=ALU.add), "EV")
                        v.wait_ge(p.sem["ES"], es_at(i, "db_b", db))
                        gi = i * 16 + db - 2
                        if gi >= 0:
                            v.wait_ge(p.sem["EG"], eg0 + gi + 1)
                        v.tensor_tensor(out=uf[:], in0=dtf[db % 2][:], in1=xsb[:, db * TC:(db + 1) * TC],
                                        op=ALU.mult)
                        first_v = True
                        for n in range(NST):
                            cidx = (h * 16 + db) * NST + n
                            v.tensor_tensor(out=dbu[:], in0=uf[:], in1=bcbuf[:, n * TC:(n + 1) * TC],
                                            op=ALU.mult)
                            p.vsync(v, v.tensor_tensor_scan(
                                out=stb[db % 2][:, n * TC:(n + 1) * TC],
                                data0=dab[db % 2][:, n * TC:(n + 1) * TC],
                                data1=dbu[:], initial=carry[:, cidx:cidx + 1],
                                op0=ALU.mult, op1=ALU.add))
                            v.tensor_copy(out=carry[:, cidx:cidx + 1],
                                          in_=stb[db % 2][:, (n + 1) * TC - 1:(n + 1) * TC])
                            if n in VSET:
                                v.tensor_tensor(out=ytm[:], in0=stb[db % 2][:, n * TC:(n + 1) * TC],
                                                in1=bcbuf[:, (16 + n) * TC:(17 + n) * TC], op=ALU.mult)
                                if first_v:
                                    v.tensor_copy(out=yacv[db % 2][:], in_=ytm[:])
                                    first_v = False
                                else:
                                    v.tensor_tensor(out=yacv[db % 2][:], in0=yacv[db % 2][:],
                                                    in1=ytm[:], op=ALU.add)
                        ii = v.tensor_copy(out=carry[:, cidx:cidx + 1],
                                           in_=stb[db % 2][:, (NST) * TC - 1:(NST) * TC])
                        p.inc(ii, "EV")
                        vs_mark[(i, db)] = p.c["VS"]

            @block.gpsimd
            def _(g_):
                p.bwait(g_, s0, skip=("EG",))
                for i in range(NU):
                    tc, h = divmod(i, NH)
                    for db in range(16):
                        g_.wait_ge(p.sem["VS"], vs_mark[(i, db)])
                        g_.wait_ge(p.sem["EV"], ev_at(i, db))
                        if (i, db) != (0, 0):
                            if db == 0 and i >= 1:
                                smv, val = st_done[i - 1]
                                g_.wait_ge(p.sem[smv], val)
                        first_g = True
                        for n in GSET:
                            g_.tensor_tensor(out=gty[:], in0=stb[db % 2][:, n * TC:(n + 1) * TC],
                                             in1=bcbuf[:, (16 + n) * TC:(17 + n) * TC], op=ALU.mult)
                            if first_g:
                                g_.tensor_copy(out=yag[:], in_=gty[:])
                                first_g = False
                            else:
                                g_.tensor_tensor(out=yag[:], in0=yag[:], in1=gty[:], op=ALU.add)
                        g_.tensor_tensor(out=yag[:], in0=yag[:], in1=yacv[db % 2][:], op=ALU.add)
                        g_.tensor_scalar(out=uf2[:],
                                         in0=xsb[:, db * TC:(db + 1) * TC],
                                         scalar1=d_s[:, h * 16 + db:h * 16 + db + 1],
                                         scalar2=0.0, op0=ALU.mult, op1=ALU.add)
                        g_.tensor_tensor(out=yag[:], in0=yag[:], in1=uf2[:], op=ALU.add)
                        p.inc(g_.tensor_tensor(out=ygb[:, db * TC:(db + 1) * TC], in0=yag[:],
                                               in1=gsb[:, db * TC:(db + 1) * TC], op=ALU.mult), "EG")

        def p_conv(xraw):
            s0 = p.snap()
            es0, ev0 = s0["ES"], s0["EV"]
            cv = [p.sb("cvb", [128, TC + 3], BF16) for _ in range(2)]
            t1 = [p.sb("cvt", [128, TC], F32) for _ in range(2)]
            ot = [p.sb("cvo", [128, TC], BF16) for _ in range(4)]
            ld_done, st_done = {}, {}
            NU = 32 * NTC

            @block.sync
            def _(s):
                p.bwait(s, s0, skip=("LD0", "LD1", "ST0", "ST1", "ST2", "ST3"))
                for i in range(NU):
                    cb, tc = divmod(i, NTC)
                    if i >= 2:
                        s.wait_ge(p.sem["ES"], es0 + (i - 2) + 1)
                    sem = "LD" + str(i % 2)
                    if tc == 0:
                        p.inc(s.dma_start(out=cv[i % 2][:, 3:3 + TC],
                                          in_=xraw[cb, :, 0:TC]), sem, 16)
                    else:
                        p.inc(s.dma_start(out=cv[i % 2][:, 0:TC + 3],
                                          in_=xraw[cb, :, tc * TC - 3:(tc + 1) * TC]), sem, 16)
                    ld_done[i] = (sem, p.c[sem])
                    s.wait_ge(p.sem["ES"], es0 + i + 1)
                    ssem = "ST" + str(i % 4)
                    p.inc(s.dma_start(out=xssm[cb, :, tc * TC:(tc + 1) * TC], in_=ot[i % 4][:]), ssem, 16)
                    st_done[i] = (ssem, p.c[ssem])

            @block.vector
            def _(v):
                p.bwait(v, s0, skip=("EV",))
                for i in range(NU):
                    cb, tc = divmod(i, NTC)
                    if tc == 0:
                        if i >= 2:
                            v.wait_ge(p.sem["ES"], es0 + (i - 2) + 1)
                        v.memset(cv[i % 2][:, 0:3], 0.0)
                    sm, val = ld_done[i]
                    v.wait_ge(p.sem[sm], val)
                    c = cv[i % 2]
                    v.tensor_scalar_mul(t1[i % 2][:], c[:, 3:3 + TC], convw_s[:, cb * 4 + 3:cb * 4 + 4])
                    for j in (2, 1, 0):
                        ii = v.scalar_tensor_tensor(out=t1[i % 2][:], in0=c[:, j:j + TC],
                                                    scalar=convw_s[:, cb * 4 + j:cb * 4 + j + 1],
                                                    in1=t1[i % 2][:], op0=ALU.mult, op1=ALU.add)
                    p.inc(ii, "EV")

            @block.scalar
            def _(sc):
                p.bwait(sc, s0, skip=("ES",))
                for i in range(NU):
                    cb, tc = divmod(i, NTC)
                    sc.wait_ge(p.sem["EV"], ev0 + i + 1)
                    if i >= 4:
                        sm, val = st_done[i - 4]
                        sc.wait_ge(p.sem[sm], val)
                    p.inc(sc.activation(out=ot[i % 4][:], in_=t1[i % 2][:], func=AF.Silu,
                                        bias=convb_s[:, cb:cb + 1]), "ES")

        def p_scan():
            s0 = p.snap()
            es0, ev0, mm0 = s0["ES"], s0["EV"], s0["MM"]
            xsb = p.sb("scxs", [128, KD * TC], BF16)
            gsb = p.sb("scgs", [128, KD * TC], BF16)
            dtin = p.sb("scdt", [128, TC], BF16)
            bcb = p.sb("scbc", [32, TC], BF16)
            bcbuf = p.sb("scbb", [128, 32 * TC], BF16)
            dtf = [p.sb("scdf", [128, TC], F32) for _ in range(2)]
            spr = p.sb("scsp", [128, TC], F32)
            spt = p.sb("scspt", [128, TC], F32)
            dab = [p.sb("scda", [128, NST * TC], F32) for _ in range(2)]
            uf = p.sb("scu", [128, TC], F32)
            dbu = p.sb("scdb", [128, TC], F32)
            stb = [p.sb("scst", [128, TC], BF16) for _ in range(2)]
            ytm = p.sb("scyt", [128, TC], BF16)
            gty = p.sb("scgy", [128, TC], BF16)
            yac = p.sb("scya", [128, TC], F32)
            yag = p.sb("scyag", [128, TC], F32)
            uf2 = p.sb("scu2", [128, TC], F32)
            ygb = p.sb("scyg", [128, KD * TC], BF16)
            GSET = [n for n in range(NST) if n % 4 != 0]
            VSET = [n for n in range(NST) if n % 4 == 0]
            vs_mark = {}
            eg_mark = {}
            eg0 = s0["EG"]

            def eg_at(i, db, n):
                per_db = len(GSET)
                base = eg0 + (i * 16 + db) * per_db
                return base + GSET.index(n) + 1
            ld_done, st_done = {}, {}
            r_mm = 2 + 32 + 16
            r_es = 2 + 32 + 32
            r_ev = 32
            NU = NTC * NH

            def mm_at(i, stage, n=0):
                base = mm0 + i * r_mm
                if stage == "xp":
                    return base + 2
                if stage == "bc":
                    return base + 2 + n + 1
                return base + 2 + 32 + n + 1

            def es_at(i, stage, n=0):
                base = es0 + i * r_es
                if stage == "io":
                    return base + 2
                if stage == "bc":
                    return base + 2 + n + 1
                if stage == "db_a":
                    return base + 2 + 32 + 2 * n + 1
                return base + 2 + 32 + 2 * n + 2

            def ev_at(i, db, end=True):
                return ev0 + i * r_ev + 2 * db + (2 if end else 1)

            @block.sync
            def _(s):
                p.bwait(s, s0, skip=("LD0", "LD1", "ST0", "ST1", "ST2", "ST3"))
                for i in range(NU):
                    tc, h = divmod(i, NH)
                    if i >= 1:
                        s.wait_ge(p.sem["EV"], ev_at(i - 1, 15))
                    sem = "LD" + str(i % 2)
                    p.inc(s.dma_start(out=xsb[:].rearrange("p (c t) -> p c t", c=KD),
                                      in_=xssm[h * KD:(h + 1) * KD, :, tc * TC:(tc + 1) * TC]
                                      .rearrange("c p t -> p c t")), sem, 16)
                    p.inc(s.dma_start(out=gsb[:].rearrange("p (c t) -> p c t", c=KD),
                                      in_=gsil[h * KD:(h + 1) * KD, :, tc * TC:(tc + 1) * TC]
                                      .rearrange("c p t -> p c t")), sem, 16)
                    ld_done[i] = (sem, p.c[sem])
                    s.wait_ge(p.sem["EV"], ev_at(i, 15))
                    ssem = "ST" + str(i % 4)
                    p.inc(s.dma_start(out=ygat[h * KD:(h + 1) * KD, :, tc * TC:(tc + 1) * TC]
                                      .rearrange("c p t -> p c t"),
                                      in_=ygb[:].rearrange("p (c t) -> p c t", c=KD)), ssem, 16)
                    st_done[i] = (ssem, p.c[ssem])

            @block.tensor
            def _(t):
                p.bwait(t, s0, skip=("MM",))
                for i in range(NU):
                    tc, h = divmod(i, NH)
                    sm, val = ld_done[i]
                    t.wait_ge(p.sem[sm], val)
                    if i >= 1:
                        t.wait_ge(p.sem["ES"], es_at(i - 1, "db_a", 15))
                    for k in range(KD):
                        mi = t.matmul(ps[2][:, 0:TC],
                                      w_xp_s[:, ((h * KD + k) * 160):((h * KD + k) * 160 + 128)],
                                      xsb[:, k * TC:(k + 1) * TC], start=(k == 0), stop=(k == KD - 1))
                    p.inc(mi, "MM")
                    for k in range(KD):
                        mi = t.matmul(ps[3][0:32, 0:TC],
                                      w_xp_s[:, ((h * KD + k) * 160 + 128):((h * KD + k) * 160 + 160)],
                                      xsb[:, k * TC:(k + 1) * TC], start=(k == 0), stop=(k == KD - 1))
                    p.inc(mi, "MM")
                    t.wait_ge(p.sem["ES"], es_at(i, "io"))
                    for n in range(32):
                        if n >= 2:
                            t.wait_ge(p.sem["ES"], es_at(i, "bc", n - 2))
                        p.inc(t.matmul(ps[4 + n % 2][:, 0:TC], sel32[:, n * 128:(n + 1) * 128],
                                       bcb[:], start=True, stop=True), "MM")
                    for db in range(16):
                        if db >= 2:
                            t.wait_ge(p.sem["ES"], es_at(i, "db_a", db - 2))
                        p.inc(t.matmul(ps[db % 2][:, 0:TC],
                                       w_dt_s[:, (h * 16 + db) * 128:(h * 16 + db + 1) * 128],
                                       dtin[:], start=True, stop=True), "MM")

            @block.scalar
            def _(sc):
                p.bwait(sc, s0, skip=("ES",))
                for i in range(NU):
                    tc, h = divmod(i, NH)
                    sc.wait_ge(p.sem["MM"], mm_at(i, "xp"))
                    p.inc(sc.copy(out=dtin[:], in_=ps[2][:, 0:TC]), "ES")
                    p.inc(sc.copy(out=bcb[:], in_=ps[3][0:32, 0:TC]), "ES")
                    for n in range(32):
                        sc.wait_ge(p.sem["MM"], mm_at(i, "bc", n))
                        if i >= 1 and n == 0:
                            sc.wait_ge(p.sem["EV"], ev_at(i - 1, 15))
                        p.inc(sc.copy(out=bcbuf[:, n * TC:(n + 1) * TC], in_=ps[4 + n % 2][:, 0:TC]), "ES")
                    for db in range(16):
                        sc.wait_ge(p.sem["MM"], mm_at(i, "dt", db))
                        if db >= 2:
                            sc.wait_ge(p.sem["EV"], ev_at(i, db - 2))
                        bb = dtb_s[:, h * 16 + db:h * 16 + db + 1]
                        sc.activation(out=spr[:], in_=ps[db % 2][:, 0:TC], func=AF.Relu, bias=bb)
                        sc.activation(out=spt[:], in_=ps[db % 2][:, 0:TC], func=AF.Abs, bias=bb)
                        sc.activation(out=spt[:], in_=spt[:], func=AF.Exp, scale=-1.0)
                        p.inc(sc.activation(out=spt[:], in_=spt[:], func=AF.Ln, bias=1.0), "ES")
                        sc.wait_ge(p.sem["EV"], ev_at(i, db, end=False))
                        for n in range(NST):
                            ii = sc.activation(out=dab[db % 2][:, n * TC:(n + 1) * TC], in_=dtf[db % 2][:],
                                               func=AF.Exp,
                                               scale=a_s[:, (h * 16 + db) * NST + n:(h * 16 + db) * NST + n + 1])
                        p.inc(ii, "ES")

            @block.vector
            def _(v):
                p.bwait(v, s0, skip=("EV",))
                for i in range(NU):
                    tc, h = divmod(i, NH)
                    if i >= 1:
                        _sm, _val = st_done[i - 1]
                        v.wait_ge(p.sem[_sm], _val)
                    for db in range(16):
                        v.wait_ge(p.sem["ES"], es_at(i, "db_a", db))
                        p.inc(v.tensor_tensor(out=dtf[db % 2][:], in0=spr[:], in1=spt[:], op=ALU.add), "EV")
                        v.wait_ge(p.sem["ES"], es_at(i, "db_b", db))
                        v.tensor_tensor(out=uf[:], in0=dtf[db % 2][:], in1=xsb[:, db * TC:(db + 1) * TC],
                                        op=ALU.mult)
                        first_v = True
                        for n in range(NST):
                            cidx = (h * 16 + db) * NST + n
                            v.tensor_tensor(out=dbu[:], in0=uf[:], in1=bcbuf[:, n * TC:(n + 1) * TC],
                                            op=ALU.mult)
                            if n >= 2 and (n - 2) in GSET:
                                v.wait_ge(p.sem["EG"], eg_at(i, db, n - 2))
                            p.vsync(v, v.tensor_tensor_scan(out=stb[n % 2][:],
                                                            data0=dab[db % 2][:, n * TC:(n + 1) * TC],
                                                            data1=dbu[:], initial=carry[:, cidx:cidx + 1],
                                                            op0=ALU.mult, op1=ALU.add))
                            vs_mark[(i, db, n)] = p.c["VS"]
                            v.tensor_copy(out=carry[:, cidx:cidx + 1], in_=stb[n % 2][:, TC - 1:TC])
                            if n in VSET:
                                v.tensor_tensor(out=ytm[:], in0=stb[n % 2][:],
                                                in1=bcbuf[:, (16 + n) * TC:(17 + n) * TC], op=ALU.mult)
                                if first_v:
                                    v.tensor_copy(out=yac[:], in_=ytm[:])
                                    first_v = False
                                else:
                                    v.tensor_tensor(out=yac[:], in0=yac[:], in1=ytm[:], op=ALU.add)
                        v.wait_ge(p.sem["EG"], eg_at(i, db, GSET[-1]))
                        v.tensor_tensor(out=yac[:], in0=yac[:], in1=yag[:], op=ALU.add)
                        v.scalar_tensor_tensor(out=yac[:], in0=xsb[:, db * TC:(db + 1) * TC],
                                               scalar=d_s[:, h * 16 + db:h * 16 + db + 1],
                                               in1=yac[:], op0=ALU.mult, op1=ALU.add)
                        ii = v.tensor_tensor(out=ygb[:, db * TC:(db + 1) * TC], in0=yac[:],
                                             in1=gsb[:, db * TC:(db + 1) * TC], op=ALU.mult)
                        p.inc(ii, "EV")

            @block.gpsimd
            def _(g_):
                p.bwait(g_, s0, skip=("EG",))
                for i in range(NU):
                    tc, h = divmod(i, NH)
                    g_.wait_ge(p.sem["ES"], es_at(i, "bc", 31))
                    for db in range(16):
                        if (i, db) != (0, 0):
                            pi, pdb = (i, db - 1) if db >= 1 else (i - 1, 15)
                            g_.wait_ge(p.sem["EV"], ev_at(pi, pdb))
                        first_g = True
                        for n in GSET:
                            g_.wait_ge(p.sem["VS"], vs_mark[(i, db, n)])
                            g_.tensor_tensor(out=gty[:], in0=stb[n % 2][:],
                                             in1=bcbuf[:, (16 + n) * TC:(17 + n) * TC], op=ALU.mult)
                            if first_g:
                                p.inc(g_.tensor_copy(out=yag[:], in_=gty[:]), "EG")
                                first_g = False
                            else:
                                p.inc(g_.tensor_tensor(out=yag[:], in0=yag[:], in1=gty[:], op=ALU.add), "EG")

        xraw = p.dram("xraw", [32, 128, S], BF16)

        def ev_copy(m, pa, oa, sc):
            return sc.copy(out=oa, in_=pa)

        def ev_silu(m, pa, oa, sc):
            return sc.activation(out=oa, in_=pa, func=AF.Silu)

        def ev_scale_d(m, pa, oa, sc):
            return sc.activation(out=oa, in_=pa, func=AF.Identity, scale=scd_s[:, m:m + 1])

        def ev_scale_isq(m, pa, oa, sc):
            return sc.activation(out=oa, in_=pa, func=AF.Identity, scale=ISQ)

        g8 = [(k * 8, (k + 1) * 8) for k in range(8)]
        g43 = [(0, 8), (8, 16), (16, 24), (24, 32), (32, 40), (40, 43)]

        seq = [
            ("init", p_init),
            ("ln1", lambda: p_lnlike(xin, True, "pre", write_xres=xresA)),
            ("inpx", lambda: p_gemm(w_inx, "f32", D, [(0, 16), (16, 32)], lnx, ev_copy,
                                    lambda m, tc: xraw[m, :, tc * TC:(tc + 1) * TC])),
            ("inpg", lambda: p_gemm(w_ing, "f32", D, [(0, 16), (16, 32)], lnx, ev_silu,
                                    lambda m, tc: gsil[m, :, tc * TC:(tc + 1) * TC])),
            ("conv", lambda: p_conv(xraw)),
            ("scan", p_scan),
            ("outp", lambda: p_gemm(w_out, "f32", IMIX, [(0, 8), (8, 16)], ygat, ev_copy,
                                    lambda m, tc: z32[m, :, tc * TC:(tc + 1) * TC], out_f32=True)),
            ("pln1", lambda: p_lnlike(z32, True, "post", xold=xresA, dst=xresB)),
            ("ln2", lambda: p_lnlike(xresB, False, "pre")),
            ("ffgu", lambda: p_gemm_dual(w_fg, w_fu, "i32", D, g8, lnx, h1,
                                         scg=scg_s, scu=scu_s)),
            ("ffdn", lambda: p_gemm(w_fd, "i32", IFFN, [(0, 4), (4, 8), (8, 12), (12, 16)],
                                    h1, ev_scale_d,
                                    lambda m, tc: z32[m, :, tc * TC:(tc + 1) * TC],
                                    out_f32=True, nrb=1)),
            ("pln2", lambda: p_lnlike(z32, True, "post", xold=xresB, dst=xresA)),
            ("ln3", lambda: p_lnlike(xresA, False, "pre")),
            ("qp", lambda: p_gemm(w_qkv, "f32", D, [(0, 16)], lnx, ev_scale_isq,
                                  lambda m, tc: qf[m, :, tc * TC:(tc + 1) * TC])),
            ("kp", lambda: p_gemm(w_qkv, "f32", D, [(16, 32)], lnx, ev_copy,
                                  lambda m, tc: kfd[m - 16, :, tc * TC:(tc + 1) * TC])),
            ("vp", p_vtm),
            ("attn", p_attn),
            ("proj", lambda: p_gemm(w_pr, "f32", D, [(0, 8), (8, 16)], ao, ev_copy,
                                    lambda m, tc: z32[m, :, tc * TC:(tc + 1) * TC], out_f32=True)),
            ("pln3", lambda: p_lnlike(z32, True, "post", xold=xresA, dst=xresB)),
            ("ln4", lambda: p_lnlike(xresB, False, "pre")),
            ("f1gu", lambda: p_gemm_dual(i_g, i_u, "lut", D, g43, lnx, h2)),
            ("f1dn", lambda: p_gemm(i_d, "lut", IFF1, [(0, 4), (4, 8), (8, 12), (12, 16)],
                                    h2, ev_copy,
                                    lambda m, tc: z32[m, :, tc * TC:(tc + 1) * TC],
                                    out_f32=True, nrb=1)),
            ("pln4", lambda: p_lnlike(z32, True, "post", xold=xresB, dst=out)),
        ]
        stop = getattr(self, "stop_after", None)
        self.phase_counts = {}
        for n, fn in seq:
            with contextlib.ExitStack() as ph_es:
                self.ph_es = ph_es
                fn()
                self.ph_es = None
            self.phase_counts[n] = dict(self.c)
            if n == stop:
                break

        fin = self.snap()

        @block.sync
        def _(s):
            self.bwait(s, fin)

        @block.gpsimd
        def _(g):
            self.bwait(g, fin)
        self.tensors = dict(
            xin=xin, lnx=lnx, xresA=xresA, xresB=xresB, out=out, z32=z32,
            xssm=xssm, gsil=gsil, ygat=ygat, h1=h1, qf=qf, kfd=kfd, vhm=vhm,
            ao=ao, h2=h2)
        return nc


def _host_prep(inputs):
    c = np.ascontiguousarray
    ipw = inputs["L0_attn_in_proj_w"]
    m = {}
    m["w_inx"] = c(ipw[0::2].T)
    m["w_ing"] = c(ipw[1::2].T)
    m["convw"] = c(inputs["L0_attn_conv_w"].reshape(32, 128, 4).transpose(1, 0, 2)).reshape(128, 128)
    m["convb"] = c(inputs["L0_attn_conv_b"].reshape(32, 128).T)
    m["w_xp"] = c(inputs["L0_attn_x_proj_weight"].transpose(0, 2, 1)
                  .reshape(2, 16, 128, 160).transpose(2, 0, 1, 3)).reshape(128, 2 * 16 * 160)
    m["w_dt"] = c(inputs["L0_attn_dt_proj_weight"].transpose(0, 2, 1)
                  .transpose(1, 0, 2)).reshape(128, 2 * 2048)
    m["dtbi"] = c(inputs["L0_attn_dt_proj_bias"].reshape(2, 16, 128).transpose(2, 0, 1)).reshape(128, 32)
    m["alog"] = c(inputs["L0_attn_A_log"].reshape(2, 16, 128, 16).transpose(2, 0, 1, 3)).reshape(128, 512)
    m["dpar"] = c(inputs["L0_attn_D"].reshape(2, 16, 128).transpose(2, 0, 1)).reshape(128, 32)
    m["w_out"] = c(inputs["L0_attn_out_proj_w"].T)
    m["w_fg"] = c(inputs["L0_ffn_gate_w"].T)
    m["w_fu"] = c(inputs["L0_ffn_up_w"].T)
    m["w_fd"] = c(inputs["L0_ffn_down_w"].T)
    m["sc_g"] = c(inputs["L0_ffn_gate_scale"].reshape(64, 128).T)
    m["sc_u"] = c(inputs["L0_ffn_up_scale"].reshape(64, 128).T)
    m["sc_d"] = c(inputs["L0_ffn_down_scale"].reshape(16, 128).T)
    m["w_qkv"] = c(inputs["L1_attn_Wqkv"].T)
    m["w_pr"] = c(inputs["L1_attn_Wproj"].T)
    m["lutv"] = c(inputs["L1_ffn_lut"].reshape(1, 16))
    m["i_g"] = c(inputs["L1_ffn_gate_idx"].T)
    m["i_u"] = c(inputs["L1_ffn_up_idx"].T)
    m["i_d"] = c(inputs["L1_ffn_down_idx"].T)
    m["identf"] = np.eye(128, dtype=np.float32)
    sel = np.zeros((32, 32, 128), dtype=np.float32)
    for n in range(32):
        sel[n, n, :] = 1.0
    m["self32"] = c(sel.transpose(1, 0, 2)).reshape(32, 32 * 128)
    return m


def kernel(**inputs):
    S = 4096
    x = inputs["L0_attn_hidden_states"]
    NB = x.shape[0]
    shared = _host_prep(inputs)
    pr = Prog(S=S, TC=512)
    nc = pr.build()
    in_maps = []
    for b in range(NB):
        im = dict(shared)
        im["xin"] = np.ascontiguousarray(x[b].T).reshape(KD, 128, S)
        in_maps.append(im)
    res = run_bass_kernel_spmd(nc, in_maps, core_ids=list(range(NB)))
    outs = [np.asarray(r["out"]).reshape(D, S).T for r in res.results]
    return np.stack(outs).astype(np.float32)


# version 06
# speedup vs baseline: 1.0731x; 1.0705x over previous
import contextlib
import numpy as np
import concourse.bass as bass
import concourse.mybir as mybir
from concourse.bass_utils import run_bass_kernel_spmd

F32 = mybir.dt.float32
BF16 = mybir.dt.bfloat16
I32 = mybir.dt.int32
AF = mybir.ActivationFunctionType
ALU = mybir.AluOpType

D = 2048
KD = D // 128
NH = 2
HD = 2048
NST = 16
RNK = 128
IMIX = 4096
IFFN = 8192
IFF1 = 5504
KF1 = IFF1 // 128
NHEAD = 16
EPS = 1e-5
ISQ = float(1.0 / np.sqrt(128.0))
NEG = -1.0e30


class Prog:
    def __init__(self, S=4096, TC=512, debug_out=()):
        self.S, self.TC = S, TC
        self.NTC = S // TC
        self.debug_out = set(debug_out)
        self.nc = bass.Bass("TRN2", target_bir_lowering=False, detect_race_conditions=False)
        self.es = contextlib.ExitStack()
        self.es.__enter__()
        self.c = {k: 0 for k in ("LD0", "LD1", "LG0", "LG1", "LGW",
                                 "ST0", "ST1", "ST2", "ST3", "MM", "ES", "EV", "EG", "VS")}
        self.sem = {k: self.es.enter_context(self.nc.semaphore(k)) for k in self.c}
        self.uid = 0

    def nm(self, p):
        self.uid += 1
        return f"{p}{self.uid}"

    def inc(self, instr, name, k=1):
        instr.then_inc(self.sem[name], k)
        self.c[name] += k
        return self.c[name]

    def snap(self):
        return dict(self.c)

    def bwait(self, eng, snap, skip=()):
        for k, v in snap.items():
            if k not in skip and v > 0 and k != "VS":
                eng.wait_ge(self.sem[k], v)

    def vsync(self, eng, instr):
        self.inc(instr, "VS")
        eng.wait_ge(self.sem["VS"], self.c["VS"])

    def dram(self, name, shape, dt):
        kind = "ExternalOutput" if name in self.debug_out else "Internal"
        return self.nc.dram_tensor(name, shape, dt, kind=kind).ap()

    def sb(self, name, shape, dt):
        st = getattr(self, "ph_es", None) or self.es
        return st.enter_context(self.nc.sbuf_tensor(self.nm(name), shape, dt))

    def build(self):
        nc, es, S, TC, NTC = self.nc, self.es, self.S, self.TC, self.NTC
        NT128 = S // 128
        p = self

        EI = lambda n, sh, dt: nc.dram_tensor(n, sh, dt, kind="ExternalInput").ap()
        xin = EI("xin", [KD, 128, S], F32)
        w_inx = EI("w_inx", [D, IMIX], F32)
        w_ing = EI("w_ing", [D, IMIX], F32)
        convw = EI("convw", [128, 32 * 4], F32)
        convb = EI("convb", [128, 32], F32)
        w_xp = EI("w_xp", [128, NH * KD * 160], F32)
        w_dt = EI("w_dt", [128, NH * HD], F32)
        dtbi = EI("dtbi", [128, NH * 16], F32)
        alog = EI("alog", [128, NH * 16 * NST], F32)
        dpar = EI("dpar", [128, NH * 16], F32)
        w_out = EI("w_out", [IMIX, D], F32)
        w_fg = EI("w_fg", [D, IFFN], I32)
        w_fu = EI("w_fu", [D, IFFN], I32)
        w_fd = EI("w_fd", [IFFN, D], I32)
        sc_g = EI("sc_g", [128, 64], F32)
        sc_u = EI("sc_u", [128, 64], F32)
        sc_d = EI("sc_d", [128, 16], F32)
        w_qkv = EI("w_qkv", [D, 3 * D], F32)
        w_pr = EI("w_pr", [D, D], F32)
        lutv = EI("lutv", [1, 16], F32)
        i_g = EI("i_g", [D, IFF1], I32)
        i_u = EI("i_u", [D, IFF1], I32)
        i_d = EI("i_d", [IFF1, D], I32)
        identf = EI("identf", [128, 128], F32)
        self32 = EI("self32", [32, 32 * 128], F32)
        out = nc.dram_tensor("out", [KD, 128, S], F32, kind="ExternalOutput").ap()

        xresA = p.dram("xresA", [KD, 128, S], BF16)
        xresB = p.dram("xresB", [KD, 128, S], BF16)
        lnx = p.dram("lnx", [KD, 128, S], BF16)
        xssm = p.dram("xssm", [32, 128, S], BF16)
        gsil = p.dram("gsil", [32, 128, S], BF16)
        ygat = p.dram("ygat", [32, 128, S], BF16)
        z32 = p.dram("z32", [KD, 128, S], F32)
        h1 = p.dram("h1", [64, 128, S], BF16)
        qf = p.dram("qf", [NHEAD, 128, S], BF16)
        kfd = p.dram("kfd", [NHEAD, 128, S], BF16)
        vhm = p.dram("vhm", [NHEAD, NT128, 128, 128], BF16)
        ao = p.dram("ao", [KD, 128, S], BF16)
        h2 = p.dram("h2", [KF1, 128, S], BF16)

        ident = p.sb("ident", [128, 128], BF16)
        ones1 = p.sb("ones1", [1, 128], BF16)
        onesc = p.sb("onesc", [128, 1], BF16)
        sel32 = p.sb("sel32", [32, 32 * 128], BF16)
        convw_s = p.sb("convw_s", [128, 32 * 4], F32)
        convb_s = p.sb("convb_s", [128, 32], F32)
        dtb_s = p.sb("dtb_s", [128, NH * 16], F32)
        a_s = p.sb("a_s", [128, NH * 16 * NST], F32)
        d_s = p.sb("d_s", [128, NH * 16], F32)
        scg_s = p.sb("scg_s", [128, 64], F32)
        scu_s = p.sb("scu_s", [128, 64], F32)
        scd_s = p.sb("scd_s", [128, 16], F32)
        lut_s = p.sb("lut_s", [1, 16], F32)
        lutbf = p.sb("lutbf", [1, 16], BF16)
        lutb_s = p.sb("lutb_s", [128, 16], F32)
        carry = p.sb("carry", [128, NH * 16 * NST], F32)
        w_xp_s = p.sb("w_xp_s", [128, NH * KD * 160], BF16)
        w_dt_s = p.sb("w_dt_s", [128, NH * 16 * 128], BF16)
        ps = [es.enter_context(nc.psum_tensor(f"ps{i}", [128, 512], F32)) for i in range(8)]

        block = es.enter_context(nc.Block())

        def p_init():
            s0 = p.snap()

            @block.gpsimd
            def _(g):
                p.inc(g.dma_start(out=ident[:], in_=identf), "LGW", 16)
                p.inc(g.dma_start(out=sel32[:], in_=self32), "LGW", 16)
                p.inc(g.dma_start(out=w_xp_s[:], in_=w_xp), "LGW", 16)
                p.inc(g.dma_start(out=w_dt_s[:], in_=w_dt), "LGW", 16)

            @block.sync
            def _(s):
                p.inc(s.dma_start(out=convw_s[:], in_=convw), "LD0", 16)
                p.inc(s.dma_start(out=convb_s[:], in_=convb), "LD0", 16)
                p.inc(s.dma_start(out=dtb_s[:], in_=dtbi), "LD0", 16)
                p.inc(s.dma_start(out=a_s[:], in_=alog), "LD0", 16)
                p.inc(s.dma_start(out=d_s[:], in_=dpar), "LD0", 16)
                p.inc(s.dma_start(out=scg_s[:], in_=sc_g), "LD0", 16)
                p.inc(s.dma_start(out=scu_s[:], in_=sc_u), "LD0", 16)
                p.inc(s.dma_start(out=scd_s[:], in_=sc_d), "LD0", 16)
                p.inc(s.dma_start(out=lut_s[:], in_=lutv), "LD0", 16)

            ld1, lg1 = p.c["LD0"], p.c["LGW"]

            @block.scalar
            def _(sc):
                sc.wait_ge(p.sem["LD0"], ld1)
                sc.wait_ge(p.sem["LGW"], lg1)
                p.inc(sc.activation(out=a_s[:], in_=a_s[:], func=AF.Exp), "ES")

            es1 = p.c["ES"]

            @block.vector
            def _(v):
                v.wait_ge(p.sem["ES"], es1)
                v.wait_ge(p.sem["LD0"], ld1)
                v.tensor_scalar_mul(a_s[:], a_s[:], -1.0)
                v.memset(carry[:], 0.0)
                v.memset(ones1[:], 1.0)
                v.tensor_copy(out=lutbf[:], in_=lut_s[:])
                p.inc(v.memset(onesc[:], 1.0), "EV")

            ev1 = p.c["EV"]

            @block.tensor
            def _(t):
                t.wait_ge(p.sem["EV"], ev1)
                p.inc(t.matmul(ps[3][:, 0:16], ones1[:], lutbf[:], start=True, stop=True), "MM")

            mm1 = p.c["MM"]

            @block.vector
            def _(v):
                v.wait_ge(p.sem["MM"], mm1)
                p.inc(v.tensor_copy(out=lutb_s[:], in_=ps[3][:, 0:16]), "EV")

        def p_lnlike(src, src_f32, mode, xold=None, dst=None, write_xres=None):
            s0 = p.snap()
            KT = KD * TC
            final_f32 = (mode == "post" and dst is out)
            xb = [p.sb("lnxb", [128, KT], BF16) for _ in range(2)]
            xsq = p.sb("lnsq", [128, KT], BF16)
            onx = p.sb("lnonx", [128, KT], F32 if final_f32 else BF16)
            xob = [p.sb("lnxo", [128, KT], BF16) for _ in range(2)] if mode == "post" else None
            stat = p.sb("lnst", [1, 3 * TC], F32)
            coef = p.sb("lnco", [1, 2 * TC], BF16)
            abc = p.sb("lnab", [128, TC], BF16)
            bbc = p.sb("lnbb", [128, TC], BF16)

            mm0, es0, ev0 = s0["MM"], s0["ES"], s0["EV"]
            r_es, r_mm, r_ev = 3, 2, 3
            LQ = "LG" if src_f32 else "LD"
            ld_done = {}
            st_done = {}
            n_st = 2 if (mode == "pre" and write_xres is not None) else 1

            def loader(q, tc):
                if tc >= 2:
                    q.wait_ge(p.sem["EV"], ev0 + (tc - 2) * r_ev + r_ev)
                    pt = tc - 2
                    sm = "ST" + str(pt % 2)
                    q.wait_ge(p.sem[sm], s0[sm] + 16 * n_st * (pt // 2 + 1))
                sem = LQ + str(tc % 2)
                p.inc(q.dma_start(
                    out=xb[tc % 2][:].rearrange("p (k t) -> p k t", k=KD),
                    in_=src[:, :, tc * TC:(tc + 1) * TC].rearrange("k p t -> p k t")),
                    sem, 16)
                if mode == "post":
                    p.inc(q.dma_start(
                        out=xob[tc % 2][:].rearrange("p (k t) -> p k t", k=KD),
                        in_=xold[:, :, tc * TC:(tc + 1) * TC].rearrange("k p t -> p k t")),
                        sem, 16)
                ld_done[tc] = (sem, p.c[sem])

            if src_f32:
                @block.gpsimd
                def _(g):
                    p.bwait(g, s0, skip=("LG0", "LG1"))
                    for tc in range(NTC):
                        loader(g, tc)

            @block.sync
            def _(s):
                p.bwait(s, s0, skip=("LD0", "LD1", "ST0", "ST1"))
                for tc in range(NTC):
                    if not src_f32:
                        loader(s, tc)
                    s.wait_ge(p.sem["EV"], ev0 + tc * r_ev + 3)
                    sem = "ST" + str(tc % 2)
                    if mode == "pre":
                        p.inc(s.dma_start(
                            out=lnx[:, :, tc * TC:(tc + 1) * TC].rearrange("k p t -> p k t"),
                            in_=onx[:].rearrange("p (k t) -> p k t", k=KD)), sem, 16)
                        if write_xres is not None:
                            p.inc(s.dma_start(
                                out=write_xres[:, :, tc * TC:(tc + 1) * TC].rearrange("k p t -> p k t"),
                                in_=xb[tc % 2][:].rearrange("p (k t) -> p k t", k=KD)), sem, 16)
                    else:
                        p.inc(s.dma_start(
                            out=dst[:, :, tc * TC:(tc + 1) * TC].rearrange("k p t -> p k t"),
                            in_=onx[:].rearrange("p (k t) -> p k t", k=KD)), sem, 16)
                    st_done[tc] = (sem, p.c[sem])

            @block.scalar
            def _(sc):
                p.bwait(sc, s0, skip=("ES",))
                for tc in range(NTC):
                    sm, v = ld_done[tc]
                    sc.wait_ge(p.sem[sm], v)
                    p.inc(sc.activation(out=xsq[:], in_=xb[tc % 2][:], func=AF.Square), "ES")
                    sc.wait_ge(p.sem["EV"], ev0 + tc * r_ev + 1)
                    p.inc(sc.activation(out=stat[0:1, 2 * TC:3 * TC], in_=stat[0:1, TC:2 * TC],
                                        func=AF.Ln), "ES")
                    p.inc(sc.activation(out=stat[0:1, TC:2 * TC], in_=stat[0:1, 2 * TC:3 * TC],
                                        func=AF.Exp, scale=-0.5), "ES")

            @block.tensor
            def _(t):
                p.bwait(t, s0, skip=("MM",))
                for tc in range(NTC):
                    t.wait_ge(p.sem["ES"], es0 + tc * r_es + 1)
                    for k in range(KD):
                        t.matmul(ps[6][0:1, 0:TC], onesc[:], xb[tc % 2][:, k * TC:(k + 1) * TC],
                                 start=(k == 0), stop=(k == KD - 1))
                    for k in range(KD):
                        mi = t.matmul(ps[7][0:1, 0:TC], onesc[:], xsq[:, k * TC:(k + 1) * TC],
                                      start=(k == 0), stop=(k == KD - 1))
                    p.inc(mi, "MM")
                    t.wait_ge(p.sem["EV"], ev0 + tc * r_ev + 2)
                    t.matmul(ps[4][:, 0:TC], ones1[:], coef[0:1, 0:TC], start=True, stop=True)
                    mi = t.matmul(ps[5][:, 0:TC], ones1[:], coef[0:1, TC:2 * TC], start=True, stop=True)
                    p.inc(mi, "MM")

            @block.vector
            def _(v):
                p.bwait(v, s0, skip=("EV",))
                for tc in range(NTC):
                    v.wait_ge(p.sem["MM"], mm0 + tc * r_mm + 1)
                    v.tensor_scalar(out=stat[0:1, 0:TC], in0=ps[6][0:1, 0:TC],
                                    scalar1=1.0 / D, scalar2=None, op0=ALU.mult)
                    p.vsync(v, v.tensor_scalar(out=stat[0:1, TC:2 * TC], in0=ps[7][0:1, 0:TC],
                                               scalar1=1.0 / D, scalar2=float(EPS),
                                               op0=ALU.mult, op1=ALU.add))
                    v.tensor_tensor(out=stat[0:1, 2 * TC:3 * TC], in0=stat[0:1, 0:TC],
                                    in1=stat[0:1, 0:TC], op=ALU.mult)
                    p.inc(v.tensor_tensor(out=stat[0:1, TC:2 * TC], in0=stat[0:1, TC:2 * TC],
                                          in1=stat[0:1, 2 * TC:3 * TC], op=ALU.subtract), "EV")
                    v.wait_ge(p.sem["ES"], es0 + tc * r_es + 3)
                    v.tensor_copy(out=coef[0:1, 0:TC], in_=stat[0:1, TC:2 * TC])
                    v.tensor_tensor(out=stat[0:1, 2 * TC:3 * TC], in0=stat[0:1, 0:TC],
                                    in1=stat[0:1, TC:2 * TC], op=ALU.mult)
                    p.inc(v.tensor_scalar_mul(coef[0:1, TC:2 * TC], stat[0:1, 2 * TC:3 * TC], -1.0), "EV")
                    v.wait_ge(p.sem["MM"], mm0 + tc * r_mm + 2)
                    if tc >= 1:
                        _sm, _val = st_done[tc - 1]
                        v.wait_ge(p.sem[_sm], _val)
                    v.tensor_copy(out=abc[:], in_=ps[4][:, 0:TC])
                    p.vsync(v, v.tensor_copy(out=bbc[:], in_=ps[5][:, 0:TC]))
                    xv = xb[tc % 2][:].rearrange("p (k t) -> p k t", k=KD)
                    ov = onx[:].rearrange("p (k t) -> p k t", k=KD)
                    v.tensor_tensor(out=ov, in0=xv,
                                    in1=abc[:].unsqueeze(1).broadcast_to((128, KD, TC)), op=ALU.mult)
                    ii = v.tensor_tensor(out=ov, in0=ov,
                                         in1=bbc[:].unsqueeze(1).broadcast_to((128, KD, TC)), op=ALU.add)
                    if mode == "post":
                        ii = v.tensor_tensor(out=ov, in0=ov,
                                             in1=xob[tc % 2][:].rearrange("p (k t) -> p k t", k=KD),
                                             op=ALU.add)
                    p.inc(ii, "EV")

        def p_gemm(w, wkind, Kfeat, groups, rhs, evac, store_ap, out_f32=False, NPS=4, nrb=2):
            KC = Kfeat // 128
            s0 = p.snap()
            maxw = max(hi - lo for lo, hi in groups)
            NWB = 2 if wkind != "f32" else 1
            wgs = [p.sb("gmw", [128, KC * maxw * 128], BF16) for _ in range(NWB)]
            rb = [p.sb("gmr", [128, KC * TC], BF16) for _ in range(nrb)]
            ot = [p.sb("gmo", [128, TC], F32 if out_f32 else BF16) for _ in range(4)]
            mm0, es0, ev0, lgw0 = s0["MM"], s0["ES"], s0["EV"], s0["LGW"]
            NG = len(groups)
            units = []
            for g, (lo, hi) in enumerate(groups):
                for tc in range(NTC):
                    for j in range(hi - lo):
                        units.append((g, tc, j, lo + j))
            mm_of = {}
            last_of_g = {}
            n = mm0
            for g, (lo, hi) in enumerate(groups):
                for tc in range(NTC):
                    n += (hi - lo)
                    mm_of[(g, tc)] = n
                last_of_g[g] = n
            ld_rhs = {}
            st_done = {}

            nslab = {g: (groups[g][1] - groups[g][0]) for g in range(len(groups))}
            ibuf = [p.sb("gmib", [128, KC * 128], I32)] if wkind == "lut" else None
            acf = p.sb("gmac", [128, KC * 128], BF16) if wkind == "lut" else None
            tmf = p.sb("gmtm", [128, KC * 128], BF16) if wkind == "lut" else None
            slab_ld = {}
            nw_ev = {}
            slab_ev = {}
            if wkind == "lut":
                si = 0
                for g, (lo, hi) in enumerate(groups):
                    for j in range(hi - lo):
                        slab_ev[si] = ev0 + si + 1
                        si += 1

            @block.gpsimd
            def _(g_):
                p.bwait(g_, s0, skip=("LGW", "LG0", "LG1"))
                si = 0
                for g, (lo, hi) in enumerate(groups):
                    if g >= NWB:
                        g_.wait_ge(p.sem["MM"], last_of_g[g - NWB])
                    wg = wgs[g % NWB]
                    if wkind == "lut":
                        for j in range(hi - lo):
                            if si >= 1:
                                g_.wait_ge(p.sem["EV"], slab_ev[si - 1])
                            sem = "LG" + str(si % 2)
                            p.inc(g_.dma_start(
                                out=ibuf[0][:].rearrange("p (k m) -> p k m", k=KC),
                                in_=w.rearrange("(k q) m -> q k m", q=128)[:, :, (lo + j) * 128:(lo + j + 1) * 128]),
                                sem, 16)
                            slab_ld[si] = (sem, p.c[sem])
                            si += 1
                    else:
                        p.inc(g_.dma_start(
                            out=wg[:, 0:KC * (hi - lo) * 128].rearrange("p (k m) -> p k m", k=KC),
                            in_=w.rearrange("(k q) m -> q k m", q=128)[:, :, lo * 128:hi * 128]), "LGW", 16)

            if wkind == "i32":
                @block.vector
                def _(v):
                    p.bwait(v, s0, skip=("EV",))
                    for g, (lo, hi) in enumerate(groups):
                        wg = wgs[g % NWB]
                        v.wait_ge(p.sem["LGW"], lgw0 + 16 * (g + 1))
                        if g >= NWB:
                            v.wait_ge(p.sem["MM"], last_of_g[g - NWB])
                        p.inc(v.tensor_scalar_add(wg[:, 0:KC * (hi - lo) * 128],
                                                  wg[:, 0:KC * (hi - lo) * 128], -128.0), "EV")
                        nw_ev[g] = p.c["EV"]
            elif wkind == "lut":
                @block.vector
                def _(v):
                    p.bwait(v, s0, skip=("EV",))
                    si = 0
                    for g, (lo, hi) in enumerate(groups):
                        wg = wgs[g % NWB]
                        if g >= NWB:
                            v.wait_ge(p.sem["MM"], last_of_g[g - NWB])
                        for j in range(hi - lo):
                            sm, val = slab_ld[si]
                            v.wait_ge(p.sem[sm], val)
                            for vv in range(16):
                                dstv = acf[:] if vv == 0 else tmf[:]
                                v.tensor_scalar(out=dstv, in0=ibuf[0][:], scalar1=float(vv),
                                                scalar2=lutb_s[:, vv:vv + 1], op0=ALU.is_equal, op1=ALU.mult)
                                if vv > 0:
                                    v.tensor_tensor(out=acf[:], in0=acf[:], in1=tmf[:], op=ALU.add)
                            p.inc(v.tensor_copy(
                                out=wg[:, 0:KC * (hi - lo) * 128]
                                .rearrange("p (k m) -> p k m", k=KC)[:, :, j * 128:(j + 1) * 128],
                                in_=acf[:].rearrange("p (k m) -> p k m", k=KC)), "EV")
                            si += 1
                        nw_ev[g] = p.c["EV"]

            @block.sync
            def _(s):
                p.bwait(s, s0, skip=("LD0", "LD1", "ST0", "ST1", "ST2", "ST3"))
                i = 0
                for g, (lo, hi) in enumerate(groups):
                    for tc in range(NTC):
                        u = g * NTC + tc
                        if u >= nrb:
                            pg, ptc = divmod(u - nrb, NTC)
                            s.wait_ge(p.sem["MM"], mm_of[(pg, ptc)])
                        sem = "LD" + str(u % 2)
                        p.inc(s.dma_start(
                            out=rb[u % nrb][:].rearrange("p (k t) -> p k t", k=KC),
                            in_=rhs[:, :, tc * TC:(tc + 1) * TC].rearrange("k p t -> p k t")), sem, 16)
                        ld_rhs[(g, tc)] = (sem, p.c[sem])
                        for j in range(hi - lo):
                            s.wait_ge(p.sem["ES"], es0 + i + 1)
                            ssem = "ST" + str(i % 4)
                            p.inc(s.dma_start(out=store_ap(lo + j, tc), in_=ot[i % 4][:]), ssem, 16)
                            st_done[i] = (ssem, p.c[ssem])
                            i += 1

            @block.tensor
            def _(t):
                p.bwait(t, s0, skip=("MM",))
                i = 0
                for g, (lo, hi) in enumerate(groups):
                    wg = wgs[g % NWB]
                    if wkind == "f32":
                        t.wait_ge(p.sem["LGW"], lgw0 + 16 * (g + 1))
                    elif wkind == "i32":
                        t.wait_ge(p.sem["LGW"], lgw0 + 16 * (g + 1))
                        t.wait_ge(p.sem["EV"], nw_ev[g])
                    else:
                        t.wait_ge(p.sem["EV"], nw_ev[g])
                    for tc in range(NTC):
                        u = g * NTC + tc
                        sm, v = ld_rhs[(g, tc)]
                        t.wait_ge(p.sem[sm], v)
                        for j in range(hi - lo):
                            if i >= NPS:
                                t.wait_ge(p.sem["ES"], es0 + i - NPS + 1)
                            for k in range(KC):
                                mi = t.matmul(
                                    ps[i % NPS][:, 0:TC],
                                    wg[:, (k * (hi - lo) + j) * 128:(k * (hi - lo) + j + 1) * 128],
                                    rb[u % nrb][:, k * TC:(k + 1) * TC],
                                    start=(k == 0), stop=(k == KC - 1))
                            p.inc(mi, "MM")
                            i += 1

            @block.scalar
            def _(sc):
                p.bwait(sc, s0, skip=("ES",))
                for i, (g, tc, j, m) in enumerate(units):
                    sc.wait_ge(p.sem["MM"], mm0 + i + 1)
                    if i >= 4:
                        sm, v = st_done[i - 4]
                        sc.wait_ge(p.sem[sm], v)
                    p.inc(evac(m, ps[i % NPS][:, 0:TC], ot[i % 4][:], sc), "ES")

        def p_gemm_dual(wg_dram, wu_dram, wkind, Kfeat, groups, rhs, hdst, scg=None, scu=None):
            KC = Kfeat // 128
            s0 = p.snap()
            maxw = max(hi - lo for lo, hi in groups)
            wgas = [p.sb("duwa", [128, KC * maxw * 128], BF16) for _ in range(2)]
            wgbs = [p.sb("duwb", [128, KC * maxw * 128], BF16) for _ in range(2)]
            rb = [p.sb("durb", [128, KC * TC], BF16) for _ in range(2)]
            gt = [p.sb("dugt", [128, TC], BF16) for _ in range(2)]
            ut = p.sb("duut", [128, TC], BF16)
            ot = [p.sb("duot", [128, TC], BF16) for _ in range(4)]
            ibuf = [p.sb("duib", [128, KC * 128], I32) for _ in range(2)] if wkind == "lut" else None
            acf = p.sb("duac", [128, KC * 128], BF16) if wkind == "lut" else None
            tmf = p.sb("dutm", [128, KC * 128], BF16) if wkind == "lut" else None
            mm0, es0, ev0, lgw0 = s0["MM"], s0["ES"], s0["EV"], s0["LGW"]
            NG = len(groups)
            units = []
            for g, (lo, hi) in enumerate(groups):
                for tc in range(NTC):
                    for j in range(hi - lo):
                        units.append((g, tc, j, lo + j))
            mm_of, last_of_g = {}, {}
            n = mm0
            for g, (lo, hi) in enumerate(groups):
                for tc in range(NTC):
                    n += (hi - lo)
                    mm_of[(g, tc)] = n
                last_of_g[g] = n
            nw_ev = {}
            slab_ev = {}
            if wkind == "lut":
                si = 0
                for g, (lo, hi) in enumerate(groups):
                    si += 2 * (hi - lo)
                    nw_ev[g] = ev0 + si
            ev_unit = {}
            n = ev0
            si = 0
            for g, (lo, hi) in enumerate(groups):
                if wkind == "lut":
                    n += 2 * (hi - lo)
                else:
                    n += 2
                nw_ev[g] = n
                for tc in range(NTC):
                    for j in range(hi - lo):
                        n += 1
                        ev_unit[si] = n
                        si += 1
            ld_rhs, st_done, slab_ld = {}, {}, {}
            dq_ev = {}
            if wkind == "lut":
                n = ev0
                si = 0
                for g, (lo, hi) in enumerate(groups):
                    for j in range(2 * (hi - lo)):
                        n += 1
                        dq_ev[si] = n
                        si += 1
                    n += NTC * (hi - lo)

            @block.gpsimd
            def _(g_):
                p.bwait(g_, s0, skip=("LGW", "LG0", "LG1"))
                si = 0
                for g, (lo, hi) in enumerate(groups):
                    wga, wgb = wgas[g % 2], wgbs[g % 2]
                    if g >= 2:
                        g_.wait_ge(p.sem["MM"], last_of_g[g - 2])
                    if wkind == "lut":
                        for wdram, wbuf in ((wg_dram, wga), (wu_dram, wgb)):
                            for j in range(hi - lo):
                                if si >= 2:
                                    g_.wait_ge(p.sem["EV"], dq_ev[si - 2])
                                sem = "LG" + str(si % 2)
                                p.inc(g_.dma_start(
                                    out=ibuf[si % 2][:].rearrange("p (k m) -> p k m", k=KC),
                                    in_=wdram.rearrange("(k q) m -> q k m", q=128)
                                    [:, :, (lo + j) * 128:(lo + j + 1) * 128]), sem, 16)
                                slab_ld[si] = (sem, p.c[sem])
                                si += 1
                    else:
                        p.inc(g_.dma_start(
                            out=wga[:, 0:KC * (hi - lo) * 128].rearrange("p (k m) -> p k m", k=KC),
                            in_=wg_dram.rearrange("(k q) m -> q k m", q=128)[:, :, lo * 128:hi * 128]),
                            "LGW", 16)
                        p.inc(g_.dma_start(
                            out=wgb[:, 0:KC * (hi - lo) * 128].rearrange("p (k m) -> p k m", k=KC),
                            in_=wu_dram.rearrange("(k q) m -> q k m", q=128)[:, :, lo * 128:hi * 128]),
                            "LGW", 16)

            @block.sync
            def _(s):
                p.bwait(s, s0, skip=("LD0", "LD1", "ST0", "ST1", "ST2", "ST3"))
                i = 0
                for g, (lo, hi) in enumerate(groups):
                    for tc in range(NTC):
                        u = g * NTC + tc
                        if u >= 2:
                            pg, ptc = divmod(u - 2, NTC)
                            s.wait_ge(p.sem["MM"], mm_of[(pg, ptc)])
                        sem = "LD" + str(u % 2)
                        p.inc(s.dma_start(
                            out=rb[u % 2][:].rearrange("p (k t) -> p k t", k=KC),
                            in_=rhs[:, :, tc * TC:(tc + 1) * TC].rearrange("k p t -> p k t")), sem, 16)
                        ld_rhs[(g, tc)] = (sem, p.c[sem])
                        for j in range(hi - lo):
                            s.wait_ge(p.sem["EV"], ev_unit[i])
                            ssem = "ST" + str(i % 4)
                            p.inc(s.dma_start(out=hdst[lo + j, :, tc * TC:(tc + 1) * TC],
                                              in_=ot[i % 4][:]), ssem, 16)
                            st_done[i] = (ssem, p.c[ssem])
                            i += 1

            @block.vector
            def _(v):
                p.bwait(v, s0, skip=("EV",))
                si = 0
                ui = 0
                for g, (lo, hi) in enumerate(groups):
                    wga, wgb = wgas[g % 2], wgbs[g % 2]
                    if wkind == "lut":
                        if g >= 2:
                            v.wait_ge(p.sem["MM"], last_of_g[g - 2])
                        for wi, wbuf in enumerate((wga, wgb)):
                            for j in range(hi - lo):
                                sm, val = slab_ld[si]
                                v.wait_ge(p.sem[sm], val)
                                for vv in range(16):
                                    dstv = acf[:] if vv == 0 else tmf[:]
                                    v.tensor_scalar(out=dstv, in0=ibuf[si % 2][:], scalar1=float(vv),
                                                    scalar2=lutb_s[:, vv:vv + 1],
                                                    op0=ALU.is_equal, op1=ALU.mult)
                                    if vv > 0:
                                        v.tensor_tensor(out=acf[:], in0=acf[:], in1=tmf[:], op=ALU.add)
                                p.inc(v.tensor_copy(
                                    out=wbuf[:, 0:KC * (hi - lo) * 128]
                                    .rearrange("p (k m) -> p k m", k=KC)[:, :, j * 128:(j + 1) * 128],
                                    in_=acf[:].rearrange("p (k m) -> p k m", k=KC)), "EV")
                                si += 1
                    else:
                        v.wait_ge(p.sem["LGW"], lgw0 + 32 * (g + 1))
                        if g >= 2:
                            v.wait_ge(p.sem["MM"], last_of_g[g - 2])
                        p.inc(v.tensor_scalar_add(wga[:, 0:KC * (hi - lo) * 128],
                                                  wga[:, 0:KC * (hi - lo) * 128], -128.0), "EV")
                        p.inc(v.tensor_scalar_add(wgb[:, 0:KC * (hi - lo) * 128],
                                                  wgb[:, 0:KC * (hi - lo) * 128], -128.0), "EV")
                    for tc in range(NTC):
                        for j in range(hi - lo):
                            m = lo + j
                            v.wait_ge(p.sem["MM"], mm0 + ui + 1)
                            v.wait_ge(p.sem["ES"], es0 + ui + 1)
                            if scu is not None:
                                p.vsync(v, v.tensor_scalar_mul(ut[:], ps[(2 * ui + 1) % 4][:, 0:TC],
                                                               scu[:, m:m + 1]))
                            else:
                                p.vsync(v, v.tensor_copy(out=ut[:], in_=ps[(2 * ui + 1) % 4][:, 0:TC]))
                            if ui >= 4:
                                sm, val = st_done[ui - 4]
                                v.wait_ge(p.sem[sm], val)
                            p.inc(v.tensor_tensor(out=ot[ui % 4][:], in0=gt[ui % 2][:], in1=ut[:],
                                                  op=ALU.mult), "EV")
                            ui += 1

            @block.tensor
            def _(t):
                p.bwait(t, s0, skip=("MM",))
                i = 0
                for g, (lo, hi) in enumerate(groups):
                    wga, wgb = wgas[g % 2], wgbs[g % 2]
                    if wkind != "lut":
                        t.wait_ge(p.sem["LGW"], lgw0 + 32 * (g + 1))
                    t.wait_ge(p.sem["EV"], nw_ev[g])
                    for tc in range(NTC):
                        u = g * NTC + tc
                        sm, val = ld_rhs[(g, tc)]
                        t.wait_ge(p.sem[sm], val)
                        for j in range(hi - lo):
                            if i >= 2:
                                t.wait_ge(p.sem["ES"], es0 + (i - 2) + 1)
                                t.wait_ge(p.sem["EV"], ev_unit[i - 2])
                            for k in range(KC):
                                t.matmul(ps[(2 * i) % 4][:, 0:TC],
                                         wga[:, (k * (hi - lo) + j) * 128:(k * (hi - lo) + j + 1) * 128],
                                         rb[u % 2][:, k * TC:(k + 1) * TC],
                                         start=(k == 0), stop=(k == KC - 1))
                            for k in range(KC):
                                mi = t.matmul(ps[(2 * i + 1) % 4][:, 0:TC],
                                              wgb[:, (k * (hi - lo) + j) * 128:(k * (hi - lo) + j + 1) * 128],
                                              rb[u % 2][:, k * TC:(k + 1) * TC],
                                              start=(k == 0), stop=(k == KC - 1))
                            p.inc(mi, "MM")
                            i += 1

            @block.scalar
            def _(sc):
                p.bwait(sc, s0, skip=("ES",))
                for i, (g, tc, j, m) in enumerate(units):
                    sc.wait_ge(p.sem["MM"], mm0 + i + 1)
                    if i >= 2:
                        sc.wait_ge(p.sem["EV"], ev_unit[i - 2])
                    if scg is not None:
                        ii = sc.activation(out=gt[i % 2][:], in_=ps[(2 * i) % 4][:, 0:TC],
                                           func=AF.Silu, scale=scg[:, m:m + 1])
                    else:
                        ii = sc.activation(out=gt[i % 2][:], in_=ps[(2 * i) % 4][:, 0:TC], func=AF.Silu)
                    p.inc(ii, "ES")

        def p_vtm():
            s0 = p.snap()
            mm0, es0 = s0["MM"], s0["ES"]
            wv = p.sb("vtw", [128, KD * 2048], BF16)
            rb = [p.sb("vtr", [128, KD * TC], BF16) for _ in range(2)]
            ot = [p.sb("vto", [128, 512], BF16) for _ in range(4)]
            lgw0 = s0["LGW"]
            ld_rhs, st_done = {}, {}
            TSC = TC // 128
            units = []
            for tc in range(NTC):
                for tsc in range(TSC):
                    for nb in range(4):
                        units.append((tc, tsc, nb))

            @block.gpsimd
            def _(g_):
                p.bwait(g_, s0, skip=("LGW",))
                p.inc(g_.dma_start(
                    out=wv[:].rearrange("p (k m) -> p k m", k=KD),
                    in_=w_qkv.rearrange("(k q) m -> q k m", q=128)[:, :, 4096:6144]), "LGW", 16)

            @block.sync
            def _(s):
                p.bwait(s, s0, skip=("LD0", "LD1", "ST0", "ST1", "ST2", "ST3"))
                i = 0
                for tc in range(NTC):
                    if tc >= 2:
                        s.wait_ge(p.sem["MM"], mm0 + (tc - 1) * TSC * 4)
                    sem = "LD" + str(tc % 2)
                    p.inc(s.dma_start(
                        out=rb[tc % 2][:].rearrange("p (k t) -> p k t", k=KD),
                        in_=lnx[:, :, tc * TC:(tc + 1) * TC].rearrange("k p t -> p k t")), sem, 16)
                    ld_rhs[tc] = (sem, p.c[sem])
                    for tsc in range(TSC):
                        for nb in range(4):
                            s.wait_ge(p.sem["ES"], es0 + i + 1)
                            ssem = "ST" + str(i % 4)
                            kb = tc * TSC + tsc
                            for j in range(4):
                                head = nb * 4 + j
                                p.inc(s.dma_start(out=vhm[head, kb],
                                                  in_=ot[i % 4][:, j * 128:(j + 1) * 128]), ssem, 16)
                            st_done[i] = (ssem, p.c[ssem])
                            i += 1

            @block.tensor
            def _(t):
                p.bwait(t, s0, skip=("MM",))
                t.wait_ge(p.sem["LGW"], lgw0 + 16)
                i = 0
                for tc in range(NTC):
                    sm, val = ld_rhs[tc]
                    t.wait_ge(p.sem[sm], val)
                    for tsc in range(TSC):
                        for nb in range(4):
                            if i >= 4:
                                t.wait_ge(p.sem["ES"], es0 + (i - 4) + 1)
                            for k in range(KD):
                                mi = t.matmul(
                                    ps[i % 4][:, 0:512],
                                    rb[tc % 2][:, k * TC + tsc * 128:k * TC + tsc * 128 + 128],
                                    wv[:, k * 2048 + nb * 512:k * 2048 + (nb + 1) * 512],
                                    start=(k == 0), stop=(k == KD - 1))
                            p.inc(mi, "MM")
                            i += 1

            @block.scalar
            def _(sc):
                p.bwait(sc, s0, skip=("ES",))
                for i in range(len(units)):
                    sc.wait_ge(p.sem["MM"], mm0 + i + 1)
                    if i >= 4:
                        sm, val = st_done[i - 4]
                        sc.wait_ge(p.sem[sm], val)
                    p.inc(sc.copy(out=ot[i % 4][:], in_=ps[i % 4][:, 0:512]), "ES")

        def p_attn():
            s0 = p.snap()
            NQB = S // 512
            adbg = getattr(p, "attn_dbg", False)
            if adbg:
                dbg_l = p.dram("dbg_l", [32, 128, 4], F32)
                dbg_p = p.dram("dbg_p", [8, 128, 512], BF16)
            qh = p.sb("atq", [128, S], BF16)
            kh = p.sb("atk", [128, S], BF16)
            vh = p.sb("atv", [128, (S // 128) * 128], BF16)
            Ssb = p.sb("atS", [128, S], F32)
            Psb = p.sb("atP", [128, S], BF16)
            Pt = p.sb("atPt", [128, (S // 128) * 512], BF16)
            mx = p.sb("atmx", [128, 1], F32)
            nb_ = p.sb("atnb", [128, 1], F32)
            lsum = p.sb("atl", [128, 1], F32)
            lsum2 = p.sb("atl2", [128, 1], F32)
            linv = p.sb("atli", [128, 1], F32)
            ot = [p.sb("atot", [128, 512], BF16) for _ in range(4)]

            marks = {}
            cn = {"MM": s0["MM"], "ES": s0["ES"], "EV": s0["EV"], "EG": s0["EG"]}

            def walk(record):
                for h in range(NHEAD):
                    record("loads", h)
                    for qb in range(NQB):
                        record("ptzero", (h, qb))
                        for qc4 in range(4):
                            qci = qb * 4 + qc4
                            for kb5 in range(qb + 1):
                                record("score", (h, qci, kb5))
                                record("scev", (h, qci, kb5))
                            record("mask", (h, qci))
                            record("rmax", (h, qci))
                            record("exp", (h, qci))
                            record("recip", (h, qci))
                            record("pscale", (h, qci))
                            ntr = qci + 1
                            for kg in range((ntr + 3) // 4):
                                record("trans", (h, qci, kg))
                                record("trev", (h, qci, kg))
                        record("av", (h, qb))
                        record("avev", (h, qb))
                        record("store", (h, qb))

            incs = {"loads": None, "store": None, "score": "MM", "scev": "ES", "mask": "EG",
                    "rmax": "EV", "exp": "ES", "recip": "EV", "pscale": "EV", "trans": "MM",
                    "trev": "ES", "av": "MM", "avev": "ES", "ptzero": "EV"}

            def rec(kind, key):
                sm = incs[kind]
                if sm is not None:
                    cn[sm] += 1
                marks[(kind, key)] = dict(cn)
            walk(rec)
            tot = dict(cn)
            ld_done, st_done, dbg_st = {}, {}, {}

            @block.sync
            def _(s):
                p.bwait(s, s0, skip=("LD0", "LD1", "ST0", "ST1", "ST2", "ST3"))
                for h in range(NHEAD):
                    if h >= 1:
                        s.wait_ge(p.sem["MM"], marks[("av", (h - 1, NQB - 1))]["MM"])
                    sem = "LD" + str(h % 2)
                    p.inc(s.dma_start(out=qh[:], in_=qf[h]), sem, 16)
                    p.inc(s.dma_start(out=kh[:], in_=kfd[h]), sem, 16)
                    p.inc(s.dma_start(out=vh[:].rearrange("p (kb d) -> p kb d", d=128),
                                      in_=vhm[h].rearrange("kb p d -> p kb d")), sem, 16)
                    ld_done[h] = (sem, p.c[sem])
                    for qb in range(NQB):
                        if adbg and h == 0:
                            for qc4 in range(4):
                                qci = qb * 4 + qc4
                                s.wait_ge(p.sem["EV"], marks[("pscale", (h, qci))]["EV"])
                                if qci < 32:
                                    p.inc(s.dma_start(allow_slow_non_contiguous=True, out=dbg_l[qci, :, 0:1], in_=nb_[:, 0:1]), "ST0", 16)
                                    p.inc(s.dma_start(allow_slow_non_contiguous=True, out=dbg_l[qci, :, 1:2], in_=nb_[:, 0:1]), "ST0", 16)
                                    p.inc(s.dma_start(allow_slow_non_contiguous=True, out=dbg_l[qci, :, 2:3], in_=lsum[:, 0:1]), "ST0", 16)
                                    p.inc(s.dma_start(allow_slow_non_contiguous=True, out=dbg_l[qci, :, 3:4], in_=lsum2[:, 0:1]), "ST0", 16)
                                if qci < 8:
                                    p.inc(s.dma_start(out=dbg_p[qci], in_=Psb[:, 0:512]), "ST0", 16)
                                dbg_st[qci] = p.c["ST0"]
                        s.wait_ge(p.sem["ES"], marks[("avev", (h, qb))]["ES"])
                        ssem = "ST" + str((h * NQB + qb) % 4)
                        p.inc(s.dma_start(out=ao[h, :, qb * 512:(qb + 1) * 512],
                                          in_=ot[(h * NQB + qb) % 4][:]), ssem, 16)
                        st_done[(h, qb)] = (ssem, p.c[ssem])

            @block.tensor
            def _(t):
                p.bwait(t, s0, skip=("MM",))
                for h in range(NHEAD):
                    sm, val = ld_done[h]
                    t.wait_ge(p.sem[sm], val)
                    for qb in range(NQB):
                        for qc4 in range(4):
                            qci = qb * 4 + qc4
                            for kb5 in range(qb + 1):
                                nw = 512 if kb5 < qb else (qc4 + 1) * 128
                                if (h, qci, kb5) != (0, 0, 0):
                                    t.wait_ge(p.sem["ES"], max(s0["ES"],
                                              marks[("scev", (h, qci, kb5))]["ES"] - 2))
                                p.inc(t.matmul(ps[(kb5 + qci) % 2][:, 0:nw],
                                               qh[:, qci * 128:(qci + 1) * 128],
                                               kh[:, kb5 * 512:kb5 * 512 + nw],
                                               start=True, stop=True), "MM")
                            t.wait_ge(p.sem["EV"], marks[("pscale", (h, qci))]["EV"])
                            ntr = qci + 1
                            for kg in range((ntr + 3) // 4):
                                wdt = min(4, ntr - kg * 4)
                                t.wait_ge(p.sem["ES"], max(s0["ES"],
                                          marks[("trev", (h, qci, kg))]["ES"] - 2))
                                for jj in range(wdt):
                                    kb = kg * 4 + jj
                                    mi = t.matmul(ps[4 + kg % 2][:, jj * 128:(jj + 1) * 128],
                                                  Psb[:, kb * 128:(kb + 1) * 128], ident[:],
                                                  start=True, stop=True)
                                p.inc(mi, "MM")
                        t.wait_ge(p.sem["ES"], marks[("trev", (h, qb * 4 + 3,
                                  (qb * 4 + 3) // 4))]["ES"])
                        nkb = 4 * qb + 4
                        fqb = h * NQB + qb
                        if fqb >= 2:
                            pk = fqb - 2
                            t.wait_ge(p.sem["ES"], marks[("avev", (pk // NQB, pk % NQB))]["ES"])
                        for kb in range(nkb):
                            mi = t.matmul(ps[6 + fqb % 2][:, 0:512],
                                          vh[:, kb * 128:(kb + 1) * 128],
                                          Pt[:, kb * 512:(kb + 1) * 512],
                                          start=(kb == 0), stop=(kb == nkb - 1))
                        p.inc(mi, "MM")

            @block.scalar
            def _(sc):
                p.bwait(sc, s0, skip=("ES",))
                for h in range(NHEAD):
                    for qb in range(NQB):
                        for qc4 in range(4):
                            qci = qb * 4 + qc4
                            for kb5 in range(qb + 1):
                                nw = 512 if kb5 < qb else (qc4 + 1) * 128
                                sc.wait_ge(p.sem["MM"], marks[("score", (h, qci, kb5))]["MM"])
                                p.inc(sc.copy(out=Ssb[:, kb5 * 512:kb5 * 512 + nw],
                                              in_=ps[(kb5 + qci) % 2][:, 0:nw]), "ES")
                            sc.wait_ge(p.sem["EV"], marks[("rmax", (h, qci))]["EV"])
                            if (h, qci) != (0, 0):
                                pq = (h, qci - 1) if qci >= 1 else (h - 1, 4 * NQB - 1)
                                pntr = pq[1] + 1
                                sc.wait_ge(p.sem["MM"],
                                           marks[("trans", (pq[0], pq[1], (pntr - 1) // 4))]["MM"])
                            ln_ = (qci + 1) * 128
                            if adbg and h == 0 and 1 <= qci <= 32:
                                sc.wait_ge(p.sem["ST0"], dbg_st[qci - 1])
                            sc.activation(out=Psb[:, 0:ln_], in_=Ssb[:, 0:ln_], func=AF.Exp,
                                          scale=1.0, bias=nb_[:, 0:1],
                                          accum_out=lsum[:, 0:1])
                            p.inc(sc.copy(out=lsum2[:, 0:1], in_=lsum[:, 0:1]), "ES")
                            ntr = qci + 1
                            for kg in range((ntr + 3) // 4):
                                wdt = min(4, ntr - kg * 4)
                                sc.wait_ge(p.sem["MM"], marks[("trans", (h, qci, kg))]["MM"])
                                p.inc(sc.copy(
                                    out=Pt[:, 0:(S // 128) * 512]
                                    .rearrange("p (kb q) -> p kb q", q=512)
                                    [:, kg * 4:kg * 4 + wdt, qc4 * 128:(qc4 + 1) * 128],
                                    in_=ps[4 + kg % 2][:, 0:wdt * 128]
                                    .rearrange("p (w q) -> p w q", q=128)), "ES")
                        sc.wait_ge(p.sem["MM"], marks[("av", (h, qb))]["MM"])
                        fqb = h * NQB + qb
                        i4 = fqb % 4
                        if fqb >= 4:
                            pk = fqb - 4
                            sm, val = st_done[(pk // NQB, pk % NQB)]
                            sc.wait_ge(p.sem[sm], val)
                        p.inc(sc.copy(out=ot[i4][:], in_=ps[6 + fqb % 2][:, 0:512]), "ES")

            @block.gpsimd
            def _(g_):
                p.bwait(g_, s0, skip=("EG",))
                fillreg = g_.to_reg(NEG)
                for h in range(NHEAD):
                    for qb in range(NQB):
                        for qc4 in range(4):
                            qci = qb * 4 + qc4
                            g_.wait_ge(p.sem["ES"], marks[("scev", (h, qci, qb))]["ES"])
                            p.inc(g_.affine_select(
                                out=Ssb[:, qci * 128:(qci + 1) * 128],
                                in_=Ssb[:, qci * 128:(qci + 1) * 128],
                                pattern=[[-1, 128]], compare_op=ALU.is_ge, fill=fillreg,
                                base=0, channel_multiplier=1), "EG")

            @block.vector
            def _(v):
                p.bwait(v, s0, skip=("EV",))
                for h in range(NHEAD):
                    for qb in range(NQB):
                        if (h, qb) != (0, 0):
                            pqb = qb - 1 if qb >= 1 else NQB - 1
                            ph = h if qb >= 1 else h - 1
                            v.wait_ge(p.sem["MM"], marks[("av", (ph, pqb))]["MM"])
                        p.inc(v.memset(Pt[:, qb * 4 * 512:(qb + 1) * 4 * 512], 0.0), "EV")
                        for qc4 in range(4):
                            qci = qb * 4 + qc4
                            v.wait_ge(p.sem["EG"], marks[("mask", (h, qci))]["EG"])
                            if adbg and h == 0 and 1 <= qci <= 32:
                                v.wait_ge(p.sem["ST0"], dbg_st[qci - 1])
                            ln_ = (qci + 1) * 128
                            p.inc(v.reduce_max(out=nb_[:, 0:1], in_=Ssb[:, 0:ln_],
                                               axis=mybir.AxisListType.X, negate=True), "EV")
                            v.wait_ge(p.sem["ES"], marks[("exp", (h, qci))]["ES"])
                            p.inc(v.reciprocal(out=linv[:, 0:1], in_=lsum2[:, 0:1]), "EV")
                            v.wait_ge(p.sem["EV"], marks[("recip", (h, qci))]["EV"])
                            p.inc(v.tensor_scalar_mul(Psb[:, 0:ln_], Psb[:, 0:ln_],
                                                      linv[:, 0:1]), "EV")

        def p_conv(xraw):
            s0 = p.snap()
            es0, ev0 = s0["ES"], s0["EV"]
            cv = [p.sb("cvb", [128, TC + 3], BF16) for _ in range(2)]
            t1 = [p.sb("cvt", [128, TC], F32) for _ in range(2)]
            ot = [p.sb("cvo", [128, TC], BF16) for _ in range(4)]
            ld_done, st_done = {}, {}
            NU = 32 * NTC

            @block.sync
            def _(s):
                p.bwait(s, s0, skip=("LD0", "LD1", "ST0", "ST1", "ST2", "ST3"))
                for i in range(NU):
                    cb, tc = divmod(i, NTC)
                    if i >= 2:
                        s.wait_ge(p.sem["ES"], es0 + (i - 2) + 1)
                    sem = "LD" + str(i % 2)
                    if tc == 0:
                        p.inc(s.dma_start(out=cv[i % 2][:, 3:3 + TC],
                                          in_=xraw[cb, :, 0:TC]), sem, 16)
                    else:
                        p.inc(s.dma_start(out=cv[i % 2][:, 0:TC + 3],
                                          in_=xraw[cb, :, tc * TC - 3:(tc + 1) * TC]), sem, 16)
                    ld_done[i] = (sem, p.c[sem])
                    s.wait_ge(p.sem["ES"], es0 + i + 1)
                    ssem = "ST" + str(i % 4)
                    p.inc(s.dma_start(out=xssm[cb, :, tc * TC:(tc + 1) * TC], in_=ot[i % 4][:]), ssem, 16)
                    st_done[i] = (ssem, p.c[ssem])

            @block.vector
            def _(v):
                p.bwait(v, s0, skip=("EV",))
                for i in range(NU):
                    cb, tc = divmod(i, NTC)
                    if tc == 0:
                        if i >= 2:
                            v.wait_ge(p.sem["ES"], es0 + (i - 2) + 1)
                        v.memset(cv[i % 2][:, 0:3], 0.0)
                    sm, val = ld_done[i]
                    v.wait_ge(p.sem[sm], val)
                    c = cv[i % 2]
                    v.tensor_scalar_mul(t1[i % 2][:], c[:, 3:3 + TC], convw_s[:, cb * 4 + 3:cb * 4 + 4])
                    for j in (2, 1, 0):
                        ii = v.scalar_tensor_tensor(out=t1[i % 2][:], in0=c[:, j:j + TC],
                                                    scalar=convw_s[:, cb * 4 + j:cb * 4 + j + 1],
                                                    in1=t1[i % 2][:], op0=ALU.mult, op1=ALU.add)
                    p.inc(ii, "EV")

            @block.scalar
            def _(sc):
                p.bwait(sc, s0, skip=("ES",))
                for i in range(NU):
                    cb, tc = divmod(i, NTC)
                    sc.wait_ge(p.sem["EV"], ev0 + i + 1)
                    if i >= 4:
                        sm, val = st_done[i - 4]
                        sc.wait_ge(p.sem[sm], val)
                    p.inc(sc.activation(out=ot[i % 4][:], in_=t1[i % 2][:], func=AF.Silu,
                                        bias=convb_s[:, cb:cb + 1]), "ES")

        def p_scan():
            s0 = p.snap()
            es0, ev0, mm0, eg0 = s0["ES"], s0["EV"], s0["MM"], s0["EG"]
            xsb = p.sb("scxs", [128, KD * TC], BF16)
            gsb = p.sb("scgs", [128, KD * TC], BF16)
            dtin = p.sb("scdt", [128, TC], BF16)
            bcb = p.sb("scbc", [32, TC], BF16)
            bcbuf = p.sb("scbb", [128, 32 * TC], BF16)
            dtf = [p.sb("scdf", [128, TC], F32) for _ in range(2)]
            spr = p.sb("scsp", [128, TC], F32)
            spt = p.sb("scspt", [128, TC], F32)
            dab = [p.sb("scda", [128, (NST // 2) * TC], F32) for _ in range(2)]
            uf = p.sb("scu", [128, TC], F32)
            uf2 = p.sb("scu2", [128, TC], F32)
            dbu = p.sb("scdb", [128, NST * TC], BF16)
            stb = [p.sb("scst", [128, NST * TC], BF16) for _ in range(2)]
            ytm = p.sb("scyt", [128, TC], BF16)
            gty = p.sb("scgy", [128, TC], BF16)
            yacv = [p.sb("scya", [128, TC], F32) for _ in range(2)]
            yag = p.sb("scyag", [128, TC], F32)
            ygb = [p.sb("scyg", [128, TC], BF16) for _ in range(2)]
            VSET = list(range(6))
            GSET = [n for n in range(NST) if n not in VSET]
            ld_done, st_done = {}, {}
            NU = NTC * NH
            r_mm = 2 + 32 + 16
            r_es = 2 + 32 + 48
            r_ev = 48

            def mm_at(i, stage, n=0):
                base = mm0 + i * r_mm
                if stage == "xp":
                    return base + 2
                if stage == "bc":
                    return base + 2 + n + 1
                return base + 2 + 32 + n + 1

            def es_at(i, stage, n=0):
                base = es0 + i * r_es
                if stage == "io":
                    return base + 2
                if stage == "bc":
                    return base + 2 + n + 1
                if stage == "db_a":
                    return base + 2 + 32 + 3 * n + 1
                if stage == "db_h0":
                    return base + 2 + 32 + 3 * n + 2
                return base + 2 + 32 + 3 * n + 3

            def ev_at(i, db, stage="end"):
                base = ev0 + i * r_ev + 3 * db
                return base + {"dtf": 1, "h0": 2, "end": 3}[stage]

            def eg_at(i, db):
                return eg0 + i * 16 + db + 1

            @block.sync
            def _(s):
                p.bwait(s, s0, skip=("LD0", "LD1", "ST0", "ST1", "ST2", "ST3"))
                for i in range(NU):
                    tc, h = divmod(i, NH)
                    if i >= 1:
                        s.wait_ge(p.sem["EG"], eg_at(i - 1, 15))
                    sem = "LD" + str(i % 2)
                    p.inc(s.dma_start(out=xsb[:].rearrange("p (c t) -> p c t", c=KD),
                                      in_=xssm[h * KD:(h + 1) * KD, :, tc * TC:(tc + 1) * TC]
                                      .rearrange("c p t -> p c t")), sem, 16)
                    p.inc(s.dma_start(out=gsb[:].rearrange("p (c t) -> p c t", c=KD),
                                      in_=gsil[h * KD:(h + 1) * KD, :, tc * TC:(tc + 1) * TC]
                                      .rearrange("c p t -> p c t")), sem, 16)
                    ld_done[i] = (sem, p.c[sem])
                    for db in range(16):
                        s.wait_ge(p.sem["EG"], eg_at(i, db))
                        ssem = "ST" + str(db % 2)
                        p.inc(s.dma_start(out=ygat[h * 16 + db, :, tc * TC:(tc + 1) * TC],
                                          in_=ygb[db % 2][:]), ssem, 16)
                        st_done[(i, db)] = (ssem, p.c[ssem])

            @block.tensor
            def _(t):
                p.bwait(t, s0, skip=("MM",))
                for i in range(NU):
                    tc, h = divmod(i, NH)
                    sm, val = ld_done[i]
                    t.wait_ge(p.sem[sm], val)
                    if i >= 1:
                        t.wait_ge(p.sem["ES"], es_at(i - 1, "db_a", 15))
                    for k in range(KD):
                        mi = t.matmul(ps[2][:, 0:TC],
                                      w_xp_s[:, ((h * KD + k) * 160):((h * KD + k) * 160 + 128)],
                                      xsb[:, k * TC:(k + 1) * TC], start=(k == 0), stop=(k == KD - 1))
                    p.inc(mi, "MM")
                    for k in range(KD):
                        mi = t.matmul(ps[3][0:32, 0:TC],
                                      w_xp_s[:, ((h * KD + k) * 160 + 128):((h * KD + k) * 160 + 160)],
                                      xsb[:, k * TC:(k + 1) * TC], start=(k == 0), stop=(k == KD - 1))
                    p.inc(mi, "MM")
                    t.wait_ge(p.sem["ES"], es_at(i, "io"))
                    for n in range(32):
                        if n >= 2:
                            t.wait_ge(p.sem["ES"], es_at(i, "bc", n - 2))
                        p.inc(t.matmul(ps[4 + n % 2][:, 0:TC], sel32[:, n * 128:(n + 1) * 128],
                                       bcb[:], start=True, stop=True), "MM")
                    for db in range(16):
                        if db >= 2:
                            t.wait_ge(p.sem["ES"], es_at(i, "db_a", db - 2))
                        p.inc(t.matmul(ps[db % 2][:, 0:TC],
                                       w_dt_s[:, (h * 16 + db) * 128:(h * 16 + db + 1) * 128],
                                       dtin[:], start=True, stop=True), "MM")

            @block.scalar
            def _(sc):
                p.bwait(sc, s0, skip=("ES",))
                for i in range(NU):
                    tc, h = divmod(i, NH)
                    sc.wait_ge(p.sem["MM"], mm_at(i, "xp"))
                    p.inc(sc.copy(out=dtin[:], in_=ps[2][:, 0:TC]), "ES")
                    p.inc(sc.copy(out=bcb[:], in_=ps[3][0:32, 0:TC]), "ES")
                    for n in range(32):
                        sc.wait_ge(p.sem["MM"], mm_at(i, "bc", n))
                        if i >= 1 and n == 0:
                            sc.wait_ge(p.sem["EG"], eg_at(i - 1, 15))
                        p.inc(sc.copy(out=bcbuf[:, n * TC:(n + 1) * TC], in_=ps[4 + n % 2][:, 0:TC]), "ES")
                    for db in range(16):
                        sc.wait_ge(p.sem["MM"], mm_at(i, "dt", db))
                        bb = dtb_s[:, h * 16 + db:h * 16 + db + 1]
                        sc.activation(out=spr[:], in_=ps[db % 2][:, 0:TC], func=AF.Relu, bias=bb)
                        sc.activation(out=spt[:], in_=ps[db % 2][:, 0:TC], func=AF.Abs, bias=bb)
                        sc.activation(out=spt[:], in_=spt[:], func=AF.Exp, scale=-1.0)
                        p.inc(sc.activation(out=spt[:], in_=spt[:], func=AF.Ln, bias=1.0), "ES")
                        sc.wait_ge(p.sem["EV"], ev_at(i, db, "dtf"))
                        for half in range(2):
                            hidx = (i * 16 + db) * 2 + half
                            if hidx >= 2:
                                ph = hidx - 2
                                pi2, r2 = divmod(ph, 32)
                                pdb, phalf = divmod(r2, 2)
                                sc.wait_ge(p.sem["EV"],
                                           ev_at(pi2, pdb, "h0" if phalf == 0 else "end"))
                            for nn in range(NST // 2):
                                n = half * (NST // 2) + nn
                                ii = sc.activation(
                                    out=dab[hidx % 2][:, nn * TC:(nn + 1) * TC], in_=dtf[db % 2][:],
                                    func=AF.Exp,
                                    scale=a_s[:, (h * 16 + db) * NST + n:(h * 16 + db) * NST + n + 1])
                            p.inc(ii, "ES")

            @block.vector
            def _(v):
                p.bwait(v, s0, skip=("EV",))
                for i in range(NU):
                    tc, h = divmod(i, NH)
                    for db in range(16):
                        v.wait_ge(p.sem["ES"], es_at(i, "db_a", db))
                        p.inc(v.tensor_tensor(out=dtf[db % 2][:], in0=spr[:], in1=spt[:], op=ALU.add), "EV")
                        gi = i * 16 + db - 2
                        if gi >= 0:
                            v.wait_ge(p.sem["EG"], eg0 + gi + 1)
                        v.tensor_tensor(out=uf[:], in0=dtf[db % 2][:], in1=xsb[:, db * TC:(db + 1) * TC],
                                        op=ALU.mult)
                        v.tensor_tensor(out=dbu[:].rearrange("p (n t) -> p n t", n=NST),
                                        in0=uf[:].unsqueeze(1).broadcast_to((128, NST, TC)),
                                        in1=bcbuf[:, 0:NST * TC].rearrange("p (n t) -> p n t", n=NST),
                                        op=ALU.mult)
                        cbase = (h * 16 + db) * NST
                        hidx = (i * 16 + db) * 2
                        v.wait_ge(p.sem["ES"], es_at(i, "db_h0", db))
                        for n in range(NST // 2):
                            v.tensor_tensor_scan(
                                out=stb[db % 2][:, n * TC:(n + 1) * TC],
                                data0=dab[hidx % 2][:, n * TC:(n + 1) * TC],
                                data1=dbu[:, n * TC:(n + 1) * TC],
                                initial=carry[:, cbase + n:cbase + n + 1],
                                op0=ALU.mult, op1=ALU.add)
                        p.inc(v.tensor_copy(out=ytm[:, 0:1], in_=stb[db % 2][:, TC - 1:TC]), "EV")
                        v.wait_ge(p.sem["ES"], es_at(i, "db_h1", db))
                        for nn in range(NST // 2):
                            n = NST // 2 + nn
                            v.tensor_tensor_scan(
                                out=stb[db % 2][:, n * TC:(n + 1) * TC],
                                data0=dab[(hidx + 1) % 2][:, nn * TC:(nn + 1) * TC],
                                data1=dbu[:, n * TC:(n + 1) * TC],
                                initial=carry[:, cbase + n:cbase + n + 1],
                                op0=ALU.mult, op1=ALU.add)
                        p.vsync(v, v.tensor_copy(
                            out=carry[:, cbase:cbase + NST],
                            in_=stb[db % 2][:].rearrange("p (n t) -> p n t", n=NST)[:, :, TC - 1:TC]
                            .rearrange("p n t -> p (n t)")))
                        first_v = True
                        for n in VSET:
                            v.tensor_tensor(out=ytm[:], in0=stb[db % 2][:, n * TC:(n + 1) * TC],
                                            in1=bcbuf[:, (16 + n) * TC:(17 + n) * TC], op=ALU.mult)
                            if first_v:
                                v.tensor_copy(out=yacv[db % 2][:], in_=ytm[:])
                                first_v = False
                            else:
                                v.tensor_tensor(out=yacv[db % 2][:], in0=yacv[db % 2][:],
                                                in1=ytm[:], op=ALU.add)
                        p.inc(v.tensor_copy(out=uf2[:, 0:1], in_=yacv[db % 2][:, 0:1]), "EV")

            @block.gpsimd
            def _(g_):
                p.bwait(g_, s0, skip=("EG",))
                for i in range(NU):
                    tc, h = divmod(i, NH)
                    for db in range(16):
                        g_.wait_ge(p.sem["EV"], ev_at(i, db))
                        gdb = i * 16 + db
                        if gdb >= 2:
                            smv, val = st_done[((gdb - 2) // 16, (gdb - 2) % 16)]
                            g_.wait_ge(p.sem[smv], val)
                        first_g = True
                        for n in GSET:
                            g_.tensor_tensor(out=gty[:], in0=stb[db % 2][:, n * TC:(n + 1) * TC],
                                             in1=bcbuf[:, (16 + n) * TC:(17 + n) * TC], op=ALU.mult)
                            if first_g:
                                g_.tensor_copy(out=yag[:], in_=gty[:])
                                first_g = False
                            else:
                                g_.tensor_tensor(out=yag[:], in0=yag[:], in1=gty[:], op=ALU.add)
                        g_.tensor_tensor(out=yag[:], in0=yag[:], in1=yacv[db % 2][:], op=ALU.add)
                        g_.tensor_scalar(out=uf2[:],
                                         in0=xsb[:, db * TC:(db + 1) * TC],
                                         scalar1=d_s[:, h * 16 + db:h * 16 + db + 1],
                                         scalar2=0.0, op0=ALU.mult, op1=ALU.add)
                        g_.tensor_tensor(out=yag[:], in0=yag[:], in1=uf2[:], op=ALU.add)
                        p.inc(g_.tensor_tensor(out=ygb[db % 2][:], in0=yag[:],
                                               in1=gsb[:, db * TC:(db + 1) * TC], op=ALU.mult), "EG")

        xraw = p.dram("xraw", [32, 128, S], BF16)

        def ev_copy(m, pa, oa, sc):
            return sc.copy(out=oa, in_=pa)

        def ev_silu(m, pa, oa, sc):
            return sc.activation(out=oa, in_=pa, func=AF.Silu)

        def ev_scale_d(m, pa, oa, sc):
            return sc.activation(out=oa, in_=pa, func=AF.Identity, scale=scd_s[:, m:m + 1])

        def ev_scale_isq(m, pa, oa, sc):
            return sc.activation(out=oa, in_=pa, func=AF.Identity, scale=ISQ)

        g8 = [(k * 6, min((k + 1) * 6, 64)) for k in range(11)]
        g43 = [(k * 6, min((k + 1) * 6, 43)) for k in range(8)]
        gd16 = [(k * 3, min((k + 1) * 3, 16)) for k in range(6)]

        seq = [
            ("init", p_init),
            ("ln1", lambda: p_lnlike(xin, True, "pre", write_xres=xresA)),
            ("inpx", lambda: p_gemm(w_inx, "f32", D, [(0, 16), (16, 32)], lnx, ev_copy,
                                    lambda m, tc: xraw[m, :, tc * TC:(tc + 1) * TC])),
            ("inpg", lambda: p_gemm(w_ing, "f32", D, [(0, 16), (16, 32)], lnx, ev_silu,
                                    lambda m, tc: gsil[m, :, tc * TC:(tc + 1) * TC])),
            ("conv", lambda: p_conv(xraw)),
            ("scan", p_scan),
            ("outp", lambda: p_gemm(w_out, "f32", IMIX, [(0, 8), (8, 16)], ygat, ev_copy,
                                    lambda m, tc: z32[m, :, tc * TC:(tc + 1) * TC], out_f32=True)),
            ("pln1", lambda: p_lnlike(z32, True, "post", xold=xresA, dst=xresB)),
            ("ln2", lambda: p_lnlike(xresB, False, "pre")),
            ("ffgu", lambda: p_gemm_dual(w_fg, w_fu, "i32", D, g8, lnx, h1,
                                         scg=scg_s, scu=scu_s)),
            ("ffdn", lambda: p_gemm(w_fd, "i32", IFFN, gd16,
                                    h1, ev_scale_d,
                                    lambda m, tc: z32[m, :, tc * TC:(tc + 1) * TC],
                                    out_f32=True, nrb=1)),
            ("pln2", lambda: p_lnlike(z32, True, "post", xold=xresB, dst=xresA)),
            ("ln3", lambda: p_lnlike(xresA, False, "pre")),
            ("qp", lambda: p_gemm(w_qkv, "f32", D, [(0, 16)], lnx, ev_scale_isq,
                                  lambda m, tc: qf[m, :, tc * TC:(tc + 1) * TC])),
            ("kp", lambda: p_gemm(w_qkv, "f32", D, [(16, 32)], lnx, ev_copy,
                                  lambda m, tc: kfd[m - 16, :, tc * TC:(tc + 1) * TC])),
            ("vp", p_vtm),
            ("attn", p_attn),
            ("proj", lambda: p_gemm(w_pr, "f32", D, [(0, 8), (8, 16)], ao, ev_copy,
                                    lambda m, tc: z32[m, :, tc * TC:(tc + 1) * TC], out_f32=True)),
            ("pln3", lambda: p_lnlike(z32, True, "post", xold=xresA, dst=xresB)),
            ("ln4", lambda: p_lnlike(xresB, False, "pre")),
            ("f1gu", lambda: p_gemm_dual(i_g, i_u, "lut", D, g43, lnx, h2)),
            ("f1dn", lambda: p_gemm(i_d, "lut", IFF1, gd16,
                                    h2, ev_copy,
                                    lambda m, tc: z32[m, :, tc * TC:(tc + 1) * TC],
                                    out_f32=True, nrb=1)),
            ("pln4", lambda: p_lnlike(z32, True, "post", xold=xresB, dst=out)),
        ]
        stop = getattr(self, "stop_after", None)
        self.phase_counts = {}
        for n, fn in seq:
            with contextlib.ExitStack() as ph_es:
                self.ph_es = ph_es
                fn()
                self.ph_es = None
            self.phase_counts[n] = dict(self.c)
            if n == stop:
                break

        fin = self.snap()

        @block.sync
        def _(s):
            self.bwait(s, fin)

        @block.gpsimd
        def _(g):
            self.bwait(g, fin)
        self.tensors = dict(
            xin=xin, lnx=lnx, xresA=xresA, xresB=xresB, out=out, z32=z32,
            xssm=xssm, gsil=gsil, ygat=ygat, h1=h1, qf=qf, kfd=kfd, vhm=vhm,
            ao=ao, h2=h2)
        return nc


def _host_prep(inputs):
    c = np.ascontiguousarray
    ipw = inputs["L0_attn_in_proj_w"]
    m = {}
    m["w_inx"] = c(ipw[0::2].T)
    m["w_ing"] = c(ipw[1::2].T)
    m["convw"] = c(inputs["L0_attn_conv_w"].reshape(32, 128, 4).transpose(1, 0, 2)).reshape(128, 128)
    m["convb"] = c(inputs["L0_attn_conv_b"].reshape(32, 128).T)
    m["w_xp"] = c(inputs["L0_attn_x_proj_weight"].transpose(0, 2, 1)
                  .reshape(2, 16, 128, 160).transpose(2, 0, 1, 3)).reshape(128, 2 * 16 * 160)
    m["w_dt"] = c(inputs["L0_attn_dt_proj_weight"].transpose(0, 2, 1)
                  .transpose(1, 0, 2)).reshape(128, 2 * 2048)
    m["dtbi"] = c(inputs["L0_attn_dt_proj_bias"].reshape(2, 16, 128).transpose(2, 0, 1)).reshape(128, 32)
    m["alog"] = c(inputs["L0_attn_A_log"].reshape(2, 16, 128, 16).transpose(2, 0, 1, 3)).reshape(128, 512)
    m["dpar"] = c(inputs["L0_attn_D"].reshape(2, 16, 128).transpose(2, 0, 1)).reshape(128, 32)
    m["w_out"] = c(inputs["L0_attn_out_proj_w"].T)
    m["w_fg"] = c(inputs["L0_ffn_gate_w"].T)
    m["w_fu"] = c(inputs["L0_ffn_up_w"].T)
    m["w_fd"] = c(inputs["L0_ffn_down_w"].T)
    m["sc_g"] = c(inputs["L0_ffn_gate_scale"].reshape(64, 128).T)
    m["sc_u"] = c(inputs["L0_ffn_up_scale"].reshape(64, 128).T)
    m["sc_d"] = c(inputs["L0_ffn_down_scale"].reshape(16, 128).T)
    m["w_qkv"] = c(inputs["L1_attn_Wqkv"].T)
    m["w_pr"] = c(inputs["L1_attn_Wproj"].T)
    m["lutv"] = c(inputs["L1_ffn_lut"].reshape(1, 16))
    m["i_g"] = c(inputs["L1_ffn_gate_idx"].T)
    m["i_u"] = c(inputs["L1_ffn_up_idx"].T)
    m["i_d"] = c(inputs["L1_ffn_down_idx"].T)
    m["identf"] = np.eye(128, dtype=np.float32)
    sel = np.zeros((32, 32, 128), dtype=np.float32)
    for n in range(32):
        sel[n, n, :] = 1.0
    m["self32"] = c(sel.transpose(1, 0, 2)).reshape(32, 32 * 128)
    return m


def kernel(**inputs):
    S = 4096
    x = inputs["L0_attn_hidden_states"]
    NB = x.shape[0]
    shared = _host_prep(inputs)
    pr = Prog(S=S, TC=512)
    nc = pr.build()
    in_maps = []
    for b in range(NB):
        im = dict(shared)
        im["xin"] = np.ascontiguousarray(x[b].T).reshape(KD, 128, S)
        in_maps.append(im)
    res = run_bass_kernel_spmd(nc, in_maps, core_ids=list(range(NB)))
    outs = [np.asarray(r["out"]).reshape(D, S).T for r in res.results]
    return np.stack(outs).astype(np.float32)


# version 07
# speedup vs baseline: 1.1015x; 1.0265x over previous
import contextlib
import numpy as np
import concourse.bass as bass
import concourse.mybir as mybir
from concourse.bass_utils import run_bass_kernel_spmd

F32 = mybir.dt.float32
BF16 = mybir.dt.bfloat16
I32 = mybir.dt.int32
AF = mybir.ActivationFunctionType
ALU = mybir.AluOpType

D = 2048
KD = D // 128
NH = 2
HD = 2048
NST = 16
RNK = 128
IMIX = 4096
IFFN = 8192
IFF1 = 5504
KF1 = IFF1 // 128
NHEAD = 16
EPS = 1e-5
ISQ = float(1.0 / np.sqrt(128.0))
NEG = -1.0e30


class Prog:
    def __init__(self, S=4096, TC=512, debug_out=()):
        self.S, self.TC = S, TC
        self.NTC = S // TC
        self.debug_out = set(debug_out)
        self.nc = bass.Bass("TRN2", target_bir_lowering=False, detect_race_conditions=False)
        self.es = contextlib.ExitStack()
        self.es.__enter__()
        self.c = {k: 0 for k in ("LD0", "LD1", "LG0", "LG1", "LGW",
                                 "ST0", "ST1", "ST2", "ST3", "MM", "ES", "EV", "EG", "VS")}
        self.sem = {k: self.es.enter_context(self.nc.semaphore(k)) for k in self.c}
        self.uid = 0

    def nm(self, p):
        self.uid += 1
        return f"{p}{self.uid}"

    def inc(self, instr, name, k=1):
        instr.then_inc(self.sem[name], k)
        self.c[name] += k
        return self.c[name]

    def snap(self):
        return dict(self.c)

    def bwait(self, eng, snap, skip=()):
        for k, v in snap.items():
            if k not in skip and v > 0 and k != "VS":
                eng.wait_ge(self.sem[k], v)

    def vsync(self, eng, instr):
        self.inc(instr, "VS")
        eng.wait_ge(self.sem["VS"], self.c["VS"])

    def dram(self, name, shape, dt):
        kind = "ExternalOutput" if name in self.debug_out else "Internal"
        return self.nc.dram_tensor(name, shape, dt, kind=kind).ap()

    def sb(self, name, shape, dt):
        st = getattr(self, "ph_es", None) or self.es
        return st.enter_context(self.nc.sbuf_tensor(self.nm(name), shape, dt))

    def build(self):
        nc, es, S, TC, NTC = self.nc, self.es, self.S, self.TC, self.NTC
        NT128 = S // 128
        p = self

        EI = lambda n, sh, dt: nc.dram_tensor(n, sh, dt, kind="ExternalInput").ap()
        xin = EI("xin", [KD, 128, S], F32)
        w_inx = EI("w_inx", [D, IMIX], F32)
        w_ing = EI("w_ing", [D, IMIX], F32)
        convw = EI("convw", [128, 32 * 4], F32)
        convb = EI("convb", [128, 32], F32)
        w_xp = EI("w_xp", [128, NH * KD * 160], F32)
        w_dt = EI("w_dt", [128, NH * HD], F32)
        dtbi = EI("dtbi", [128, NH * 16], F32)
        alog = EI("alog", [128, NH * 16 * NST], F32)
        dpar = EI("dpar", [128, NH * 16], F32)
        w_out = EI("w_out", [IMIX, D], F32)
        w_fg = EI("w_fg", [D, IFFN], I32)
        w_fu = EI("w_fu", [D, IFFN], I32)
        w_fd = EI("w_fd", [IFFN, D], I32)
        sc_g = EI("sc_g", [128, 64], F32)
        sc_u = EI("sc_u", [128, 64], F32)
        sc_d = EI("sc_d", [128, 16], F32)
        w_qkv = EI("w_qkv", [D, 3 * D], F32)
        w_pr = EI("w_pr", [D, D], F32)
        lutv = EI("lutv", [1, 16], F32)
        i_g = EI("i_g", [D, IFF1], I32)
        i_u = EI("i_u", [D, IFF1], I32)
        i_d = EI("i_d", [IFF1, D], I32)
        identf = EI("identf", [128, 128], F32)
        mask128 = EI("mask128", [128, 128], F32)
        self32 = EI("self32", [32, 32 * 128], F32)
        out = nc.dram_tensor("out", [KD, 128, S], F32, kind="ExternalOutput").ap()

        xresA = p.dram("xresA", [KD, 128, S], BF16)
        xresB = p.dram("xresB", [KD, 128, S], BF16)
        lnx = p.dram("lnx", [KD, 128, S], BF16)
        xssm = p.dram("xssm", [32, 128, S], BF16)
        gsil = p.dram("gsil", [32, 128, S], BF16)
        ygat = p.dram("ygat", [32, 128, S], BF16)
        z32 = p.dram("z32", [KD, 128, S], F32)
        h1 = p.dram("h1", [64, 128, S], BF16)
        qf = p.dram("qf", [NHEAD, 128, S], BF16)
        kfd = p.dram("kfd", [NHEAD, 128, S], BF16)
        vhm = p.dram("vhm", [NHEAD, NT128, 128, 128], BF16)
        ao = p.dram("ao", [KD, 128, S], BF16)
        h2 = p.dram("h2", [KF1, 128, S], BF16)

        ident = p.sb("ident", [128, 128], BF16)
        maskt = p.sb("maskt", [128, 128], F32)
        ones1 = p.sb("ones1", [1, 128], BF16)
        onesc = p.sb("onesc", [128, 1], BF16)
        sel32 = p.sb("sel32", [32, 32 * 128], BF16)
        convw_s = p.sb("convw_s", [128, 32 * 4], F32)
        convb_s = p.sb("convb_s", [128, 32], F32)
        dtb_s = p.sb("dtb_s", [128, NH * 16], F32)
        a_s = p.sb("a_s", [128, NH * 16 * NST], F32)
        d_s = p.sb("d_s", [128, NH * 16], F32)
        scg_s = p.sb("scg_s", [128, 64], F32)
        scu_s = p.sb("scu_s", [128, 64], F32)
        scd_s = p.sb("scd_s", [128, 16], F32)
        lut_s = p.sb("lut_s", [1, 16], F32)
        lutbf = p.sb("lutbf", [1, 16], BF16)
        lutb_s = p.sb("lutb_s", [128, 16], F32)
        carry = p.sb("carry", [128, NH * 16 * NST], F32)
        w_xp_s = p.sb("w_xp_s", [128, NH * KD * 160], BF16)
        w_dt_s = p.sb("w_dt_s", [128, NH * 16 * 128], BF16)
        ps = [es.enter_context(nc.psum_tensor(f"ps{i}", [128, 512], F32)) for i in range(8)]

        block = es.enter_context(nc.Block())

        def p_init():
            s0 = p.snap()

            @block.gpsimd
            def _(g):
                p.inc(g.dma_start(out=ident[:], in_=identf), "LGW", 16)
                p.inc(g.dma_start(out=sel32[:], in_=self32), "LGW", 16)
                p.inc(g.dma_start(out=w_xp_s[:], in_=w_xp), "LGW", 16)
                p.inc(g.dma_start(out=w_dt_s[:], in_=w_dt), "LGW", 16)

            @block.sync
            def _(s):
                p.inc(s.dma_start(out=convw_s[:], in_=convw), "LD0", 16)
                p.inc(s.dma_start(out=convb_s[:], in_=convb), "LD0", 16)
                p.inc(s.dma_start(out=dtb_s[:], in_=dtbi), "LD0", 16)
                p.inc(s.dma_start(out=a_s[:], in_=alog), "LD0", 16)
                p.inc(s.dma_start(out=d_s[:], in_=dpar), "LD0", 16)
                p.inc(s.dma_start(out=scg_s[:], in_=sc_g), "LD0", 16)
                p.inc(s.dma_start(out=scu_s[:], in_=sc_u), "LD0", 16)
                p.inc(s.dma_start(out=scd_s[:], in_=sc_d), "LD0", 16)
                p.inc(s.dma_start(out=lut_s[:], in_=lutv), "LD0", 16)
                p.inc(s.dma_start(out=maskt[:], in_=mask128), "LD0", 16)

            ld1, lg1 = p.c["LD0"], p.c["LGW"]

            @block.scalar
            def _(sc):
                sc.wait_ge(p.sem["LD0"], ld1)
                sc.wait_ge(p.sem["LGW"], lg1)
                p.inc(sc.activation(out=a_s[:], in_=a_s[:], func=AF.Exp), "ES")

            es1 = p.c["ES"]

            @block.vector
            def _(v):
                v.wait_ge(p.sem["ES"], es1)
                v.wait_ge(p.sem["LD0"], ld1)
                v.tensor_scalar_mul(a_s[:], a_s[:], -1.0)
                v.memset(carry[:], 0.0)
                v.memset(ones1[:], 1.0)
                v.tensor_copy(out=lutbf[:], in_=lut_s[:])
                p.inc(v.memset(onesc[:], 1.0), "EV")

            ev1 = p.c["EV"]

            @block.tensor
            def _(t):
                t.wait_ge(p.sem["EV"], ev1)
                p.inc(t.matmul(ps[3][:, 0:16], ones1[:], lutbf[:], start=True, stop=True), "MM")

            mm1 = p.c["MM"]

            @block.vector
            def _(v):
                v.wait_ge(p.sem["MM"], mm1)
                p.inc(v.tensor_copy(out=lutb_s[:], in_=ps[3][:, 0:16]), "EV")

        def p_lnlike(src, src_f32, mode, xold=None, dst=None, write_xres=None):
            s0 = p.snap()
            KT = KD * TC
            final_f32 = (mode == "post" and dst is out)
            xb = [p.sb("lnxb", [128, KT], BF16) for _ in range(2)]
            xsq = p.sb("lnsq", [128, KT], BF16)
            onx = p.sb("lnonx", [128, KT], F32 if final_f32 else BF16)
            xob = [p.sb("lnxo", [128, KT], BF16) for _ in range(2)] if mode == "post" else None
            stat = p.sb("lnst", [1, 3 * TC], F32)
            coef = p.sb("lnco", [1, 2 * TC], BF16)
            abc = p.sb("lnab", [128, TC], BF16)
            bbc = p.sb("lnbb", [128, TC], BF16)

            mm0, es0, ev0 = s0["MM"], s0["ES"], s0["EV"]
            r_es, r_mm, r_ev = 3, 2, 3
            LQ = "LG" if src_f32 else "LD"
            ld_done = {}
            st_done = {}
            n_st = 2 if (mode == "pre" and write_xres is not None) else 1

            def loader(q, tc):
                if tc >= 2:
                    q.wait_ge(p.sem["EV"], ev0 + (tc - 2) * r_ev + r_ev)
                    pt = tc - 2
                    sm = "ST" + str(pt % 2)
                    q.wait_ge(p.sem[sm], s0[sm] + 16 * n_st * (pt // 2 + 1))
                sem = LQ + str(tc % 2)
                p.inc(q.dma_start(
                    out=xb[tc % 2][:].rearrange("p (k t) -> p k t", k=KD),
                    in_=src[:, :, tc * TC:(tc + 1) * TC].rearrange("k p t -> p k t")),
                    sem, 16)
                if mode == "post":
                    p.inc(q.dma_start(
                        out=xob[tc % 2][:].rearrange("p (k t) -> p k t", k=KD),
                        in_=xold[:, :, tc * TC:(tc + 1) * TC].rearrange("k p t -> p k t")),
                        sem, 16)
                ld_done[tc] = (sem, p.c[sem])

            if src_f32:
                @block.gpsimd
                def _(g):
                    p.bwait(g, s0, skip=("LG0", "LG1"))
                    for tc in range(NTC):
                        loader(g, tc)

            @block.sync
            def _(s):
                p.bwait(s, s0, skip=("LD0", "LD1", "ST0", "ST1"))
                for tc in range(NTC):
                    if not src_f32:
                        loader(s, tc)
                    s.wait_ge(p.sem["EV"], ev0 + tc * r_ev + 3)
                    sem = "ST" + str(tc % 2)
                    if mode == "pre":
                        p.inc(s.dma_start(
                            out=lnx[:, :, tc * TC:(tc + 1) * TC].rearrange("k p t -> p k t"),
                            in_=onx[:].rearrange("p (k t) -> p k t", k=KD)), sem, 16)
                        if write_xres is not None:
                            p.inc(s.dma_start(
                                out=write_xres[:, :, tc * TC:(tc + 1) * TC].rearrange("k p t -> p k t"),
                                in_=xb[tc % 2][:].rearrange("p (k t) -> p k t", k=KD)), sem, 16)
                    else:
                        p.inc(s.dma_start(
                            out=dst[:, :, tc * TC:(tc + 1) * TC].rearrange("k p t -> p k t"),
                            in_=onx[:].rearrange("p (k t) -> p k t", k=KD)), sem, 16)
                    st_done[tc] = (sem, p.c[sem])

            @block.scalar
            def _(sc):
                p.bwait(sc, s0, skip=("ES",))
                for tc in range(NTC):
                    sm, v = ld_done[tc]
                    sc.wait_ge(p.sem[sm], v)
                    p.inc(sc.activation(out=xsq[:], in_=xb[tc % 2][:], func=AF.Square), "ES")
                    sc.wait_ge(p.sem["EV"], ev0 + tc * r_ev + 1)
                    p.inc(sc.activation(out=stat[0:1, 2 * TC:3 * TC], in_=stat[0:1, TC:2 * TC],
                                        func=AF.Ln), "ES")
                    p.inc(sc.activation(out=stat[0:1, TC:2 * TC], in_=stat[0:1, 2 * TC:3 * TC],
                                        func=AF.Exp, scale=-0.5), "ES")

            @block.tensor
            def _(t):
                p.bwait(t, s0, skip=("MM",))
                for tc in range(NTC):
                    t.wait_ge(p.sem["ES"], es0 + tc * r_es + 1)
                    for k in range(KD):
                        t.matmul(ps[6][0:1, 0:TC], onesc[:], xb[tc % 2][:, k * TC:(k + 1) * TC],
                                 start=(k == 0), stop=(k == KD - 1))
                    for k in range(KD):
                        mi = t.matmul(ps[7][0:1, 0:TC], onesc[:], xsq[:, k * TC:(k + 1) * TC],
                                      start=(k == 0), stop=(k == KD - 1))
                    p.inc(mi, "MM")
                    t.wait_ge(p.sem["EV"], ev0 + tc * r_ev + 2)
                    t.matmul(ps[4][:, 0:TC], ones1[:], coef[0:1, 0:TC], start=True, stop=True)
                    mi = t.matmul(ps[5][:, 0:TC], ones1[:], coef[0:1, TC:2 * TC], start=True, stop=True)
                    p.inc(mi, "MM")

            @block.vector
            def _(v):
                p.bwait(v, s0, skip=("EV",))
                for tc in range(NTC):
                    v.wait_ge(p.sem["MM"], mm0 + tc * r_mm + 1)
                    v.tensor_scalar(out=stat[0:1, 0:TC], in0=ps[6][0:1, 0:TC],
                                    scalar1=1.0 / D, scalar2=None, op0=ALU.mult)
                    p.vsync(v, v.tensor_scalar(out=stat[0:1, TC:2 * TC], in0=ps[7][0:1, 0:TC],
                                               scalar1=1.0 / D, scalar2=float(EPS),
                                               op0=ALU.mult, op1=ALU.add))
                    v.tensor_tensor(out=stat[0:1, 2 * TC:3 * TC], in0=stat[0:1, 0:TC],
                                    in1=stat[0:1, 0:TC], op=ALU.mult)
                    p.inc(v.tensor_tensor(out=stat[0:1, TC:2 * TC], in0=stat[0:1, TC:2 * TC],
                                          in1=stat[0:1, 2 * TC:3 * TC], op=ALU.subtract), "EV")
                    v.wait_ge(p.sem["ES"], es0 + tc * r_es + 3)
                    v.tensor_copy(out=coef[0:1, 0:TC], in_=stat[0:1, TC:2 * TC])
                    v.tensor_tensor(out=stat[0:1, 2 * TC:3 * TC], in0=stat[0:1, 0:TC],
                                    in1=stat[0:1, TC:2 * TC], op=ALU.mult)
                    p.inc(v.tensor_scalar_mul(coef[0:1, TC:2 * TC], stat[0:1, 2 * TC:3 * TC], -1.0), "EV")
                    v.wait_ge(p.sem["MM"], mm0 + tc * r_mm + 2)
                    if tc >= 1:
                        _sm, _val = st_done[tc - 1]
                        v.wait_ge(p.sem[_sm], _val)
                    v.tensor_copy(out=abc[:], in_=ps[4][:, 0:TC])
                    p.vsync(v, v.tensor_copy(out=bbc[:], in_=ps[5][:, 0:TC]))
                    xv = xb[tc % 2][:].rearrange("p (k t) -> p k t", k=KD)
                    ov = onx[:].rearrange("p (k t) -> p k t", k=KD)
                    v.tensor_tensor(out=ov, in0=xv,
                                    in1=abc[:].unsqueeze(1).broadcast_to((128, KD, TC)), op=ALU.mult)
                    ii = v.tensor_tensor(out=ov, in0=ov,
                                         in1=bbc[:].unsqueeze(1).broadcast_to((128, KD, TC)), op=ALU.add)
                    if mode == "post":
                        ii = v.tensor_tensor(out=ov, in0=ov,
                                             in1=xob[tc % 2][:].rearrange("p (k t) -> p k t", k=KD),
                                             op=ALU.add)
                    p.inc(ii, "EV")

        def p_gemm(w, wkind, Kfeat, groups, rhs, evac, store_ap, out_f32=False, NPS=4, nrb=2):
            KC = Kfeat // 128
            s0 = p.snap()
            maxw = max(hi - lo for lo, hi in groups)
            NWB = 2 if wkind != "f32" else 1
            wgs = [p.sb("gmw", [128, KC * maxw * 128], BF16) for _ in range(NWB)]
            rb = [p.sb("gmr", [128, KC * TC], BF16) for _ in range(nrb)]
            ot = [p.sb("gmo", [128, TC], F32 if out_f32 else BF16) for _ in range(4)]
            mm0, es0, ev0, lgw0 = s0["MM"], s0["ES"], s0["EV"], s0["LGW"]
            NG = len(groups)
            units = []
            for g, (lo, hi) in enumerate(groups):
                for tc in range(NTC):
                    for j in range(hi - lo):
                        units.append((g, tc, j, lo + j))
            mm_of = {}
            last_of_g = {}
            n = mm0
            for g, (lo, hi) in enumerate(groups):
                for tc in range(NTC):
                    n += (hi - lo)
                    mm_of[(g, tc)] = n
                last_of_g[g] = n
            ld_rhs = {}
            st_done = {}

            nslab = {g: (groups[g][1] - groups[g][0]) for g in range(len(groups))}
            ibuf = [p.sb("gmib", [128, KC * 128], I32)] if wkind == "lut" else None
            acf = p.sb("gmac", [128, KC * 128], BF16) if wkind == "lut" else None
            tmf = p.sb("gmtm", [128, KC * 128], BF16) if wkind == "lut" else None
            slab_ld = {}
            nw_ev = {}
            slab_ev = {}
            if wkind == "lut":
                si = 0
                for g, (lo, hi) in enumerate(groups):
                    for j in range(hi - lo):
                        slab_ev[si] = ev0 + si + 1
                        si += 1

            @block.gpsimd
            def _(g_):
                p.bwait(g_, s0, skip=("LGW", "LG0", "LG1"))
                si = 0
                for g, (lo, hi) in enumerate(groups):
                    if g >= NWB:
                        g_.wait_ge(p.sem["MM"], last_of_g[g - NWB])
                    wg = wgs[g % NWB]
                    if wkind == "lut":
                        for j in range(hi - lo):
                            if si >= 1:
                                g_.wait_ge(p.sem["EV"], slab_ev[si - 1])
                            sem = "LG" + str(si % 2)
                            p.inc(g_.dma_start(
                                out=ibuf[0][:].rearrange("p (k m) -> p k m", k=KC),
                                in_=w.rearrange("(k q) m -> q k m", q=128)[:, :, (lo + j) * 128:(lo + j + 1) * 128]),
                                sem, 16)
                            slab_ld[si] = (sem, p.c[sem])
                            si += 1
                    else:
                        p.inc(g_.dma_start(
                            out=wg[:, 0:KC * (hi - lo) * 128].rearrange("p (k m) -> p k m", k=KC),
                            in_=w.rearrange("(k q) m -> q k m", q=128)[:, :, lo * 128:hi * 128]), "LGW", 16)

            if wkind == "i32":
                @block.vector
                def _(v):
                    p.bwait(v, s0, skip=("EV",))
                    for g, (lo, hi) in enumerate(groups):
                        wg = wgs[g % NWB]
                        v.wait_ge(p.sem["LGW"], lgw0 + 16 * (g + 1))
                        if g >= NWB:
                            v.wait_ge(p.sem["MM"], last_of_g[g - NWB])
                        p.inc(v.tensor_scalar_add(wg[:, 0:KC * (hi - lo) * 128],
                                                  wg[:, 0:KC * (hi - lo) * 128], -128.0), "EV")
                        nw_ev[g] = p.c["EV"]
            elif wkind == "lut":
                @block.vector
                def _(v):
                    p.bwait(v, s0, skip=("EV",))
                    si = 0
                    for g, (lo, hi) in enumerate(groups):
                        wg = wgs[g % NWB]
                        if g >= NWB:
                            v.wait_ge(p.sem["MM"], last_of_g[g - NWB])
                        for j in range(hi - lo):
                            sm, val = slab_ld[si]
                            v.wait_ge(p.sem[sm], val)
                            for vv in range(16):
                                dstv = acf[:] if vv == 0 else tmf[:]
                                v.tensor_scalar(out=dstv, in0=ibuf[0][:], scalar1=float(vv),
                                                scalar2=lutb_s[:, vv:vv + 1], op0=ALU.is_equal, op1=ALU.mult)
                                if vv > 0:
                                    v.tensor_tensor(out=acf[:], in0=acf[:], in1=tmf[:], op=ALU.add)
                            p.inc(v.tensor_copy(
                                out=wg[:, 0:KC * (hi - lo) * 128]
                                .rearrange("p (k m) -> p k m", k=KC)[:, :, j * 128:(j + 1) * 128],
                                in_=acf[:].rearrange("p (k m) -> p k m", k=KC)), "EV")
                            si += 1
                        nw_ev[g] = p.c["EV"]

            @block.sync
            def _(s):
                p.bwait(s, s0, skip=("LD0", "LD1", "ST0", "ST1", "ST2", "ST3"))
                i = 0
                for g, (lo, hi) in enumerate(groups):
                    for tc in range(NTC):
                        u = g * NTC + tc
                        if u >= nrb:
                            pg, ptc = divmod(u - nrb, NTC)
                            s.wait_ge(p.sem["MM"], mm_of[(pg, ptc)])
                        sem = "LD" + str(u % 2)
                        p.inc(s.dma_start(
                            out=rb[u % nrb][:].rearrange("p (k t) -> p k t", k=KC),
                            in_=rhs[:, :, tc * TC:(tc + 1) * TC].rearrange("k p t -> p k t")), sem, 16)
                        ld_rhs[(g, tc)] = (sem, p.c[sem])
                        for j in range(hi - lo):
                            s.wait_ge(p.sem["ES"], es0 + i + 1)
                            ssem = "ST" + str(i % 4)
                            p.inc(s.dma_start(out=store_ap(lo + j, tc), in_=ot[i % 4][:]), ssem, 16)
                            st_done[i] = (ssem, p.c[ssem])
                            i += 1

            @block.tensor
            def _(t):
                p.bwait(t, s0, skip=("MM",))
                i = 0
                for g, (lo, hi) in enumerate(groups):
                    wg = wgs[g % NWB]
                    if wkind == "f32":
                        t.wait_ge(p.sem["LGW"], lgw0 + 16 * (g + 1))
                    elif wkind == "i32":
                        t.wait_ge(p.sem["LGW"], lgw0 + 16 * (g + 1))
                        t.wait_ge(p.sem["EV"], nw_ev[g])
                    else:
                        t.wait_ge(p.sem["EV"], nw_ev[g])
                    for tc in range(NTC):
                        u = g * NTC + tc
                        sm, v = ld_rhs[(g, tc)]
                        t.wait_ge(p.sem[sm], v)
                        for j in range(hi - lo):
                            if i >= NPS:
                                t.wait_ge(p.sem["ES"], es0 + i - NPS + 1)
                            for k in range(KC):
                                mi = t.matmul(
                                    ps[i % NPS][:, 0:TC],
                                    wg[:, (k * (hi - lo) + j) * 128:(k * (hi - lo) + j + 1) * 128],
                                    rb[u % nrb][:, k * TC:(k + 1) * TC],
                                    start=(k == 0), stop=(k == KC - 1))
                            p.inc(mi, "MM")
                            i += 1

            @block.scalar
            def _(sc):
                p.bwait(sc, s0, skip=("ES",))
                for i, (g, tc, j, m) in enumerate(units):
                    sc.wait_ge(p.sem["MM"], mm0 + i + 1)
                    if i >= 4:
                        sm, v = st_done[i - 4]
                        sc.wait_ge(p.sem[sm], v)
                    p.inc(evac(m, ps[i % NPS][:, 0:TC], ot[i % 4][:], sc), "ES")

        def p_gemm_dual(wg_dram, wu_dram, wkind, Kfeat, groups, rhs, hdst, scg=None, scu=None):
            KC = Kfeat // 128
            s0 = p.snap()
            maxw = max(hi - lo for lo, hi in groups)
            wgas = [p.sb("duwa", [128, KC * maxw * 128], BF16) for _ in range(2)]
            wgbs = [p.sb("duwb", [128, KC * maxw * 128], BF16) for _ in range(2)]
            rb = [p.sb("durb", [128, KC * TC], BF16) for _ in range(2)]
            gt = [p.sb("dugt", [128, TC], BF16) for _ in range(2)]
            ut = p.sb("duut", [128, TC], BF16)
            ot = [p.sb("duot", [128, TC], BF16) for _ in range(4)]
            ibuf = [p.sb("duib", [128, KC * 128], I32) for _ in range(2)] if wkind == "lut" else None
            acf = p.sb("duac", [128, KC * 128], BF16) if wkind == "lut" else None
            tmf = p.sb("dutm", [128, KC * 128], BF16) if wkind == "lut" else None
            mm0, es0, ev0, lgw0 = s0["MM"], s0["ES"], s0["EV"], s0["LGW"]
            NG = len(groups)
            units = []
            for g, (lo, hi) in enumerate(groups):
                for tc in range(NTC):
                    for j in range(hi - lo):
                        units.append((g, tc, j, lo + j))
            mm_of, last_of_g = {}, {}
            n = mm0
            for g, (lo, hi) in enumerate(groups):
                for tc in range(NTC):
                    n += (hi - lo)
                    mm_of[(g, tc)] = n
                last_of_g[g] = n
            nw_ev = {}
            slab_ev = {}
            if wkind == "lut":
                si = 0
                for g, (lo, hi) in enumerate(groups):
                    si += 2 * (hi - lo)
                    nw_ev[g] = ev0 + si
            ev_unit = {}
            n = ev0
            si = 0
            for g, (lo, hi) in enumerate(groups):
                if wkind == "lut":
                    n += 2 * (hi - lo)
                else:
                    n += 2
                nw_ev[g] = n
                for tc in range(NTC):
                    for j in range(hi - lo):
                        n += 1
                        ev_unit[si] = n
                        si += 1
            ld_rhs, st_done, slab_ld = {}, {}, {}
            dq_ev = {}
            if wkind == "lut":
                n = ev0
                si = 0
                for g, (lo, hi) in enumerate(groups):
                    for j in range(2 * (hi - lo)):
                        n += 1
                        dq_ev[si] = n
                        si += 1
                    n += NTC * (hi - lo)

            @block.gpsimd
            def _(g_):
                p.bwait(g_, s0, skip=("LGW", "LG0", "LG1"))
                si = 0
                for g, (lo, hi) in enumerate(groups):
                    wga, wgb = wgas[g % 2], wgbs[g % 2]
                    if g >= 2:
                        g_.wait_ge(p.sem["MM"], last_of_g[g - 2])
                    if wkind == "lut":
                        for wdram, wbuf in ((wg_dram, wga), (wu_dram, wgb)):
                            for j in range(hi - lo):
                                if si >= 2:
                                    g_.wait_ge(p.sem["EV"], dq_ev[si - 2])
                                sem = "LG" + str(si % 2)
                                p.inc(g_.dma_start(
                                    out=ibuf[si % 2][:].rearrange("p (k m) -> p k m", k=KC),
                                    in_=wdram.rearrange("(k q) m -> q k m", q=128)
                                    [:, :, (lo + j) * 128:(lo + j + 1) * 128]), sem, 16)
                                slab_ld[si] = (sem, p.c[sem])
                                si += 1
                    else:
                        p.inc(g_.dma_start(
                            out=wga[:, 0:KC * (hi - lo) * 128].rearrange("p (k m) -> p k m", k=KC),
                            in_=wg_dram.rearrange("(k q) m -> q k m", q=128)[:, :, lo * 128:hi * 128]),
                            "LGW", 16)
                        p.inc(g_.dma_start(
                            out=wgb[:, 0:KC * (hi - lo) * 128].rearrange("p (k m) -> p k m", k=KC),
                            in_=wu_dram.rearrange("(k q) m -> q k m", q=128)[:, :, lo * 128:hi * 128]),
                            "LGW", 16)

            @block.sync
            def _(s):
                p.bwait(s, s0, skip=("LD0", "LD1", "ST0", "ST1", "ST2", "ST3"))
                i = 0
                for g, (lo, hi) in enumerate(groups):
                    for tc in range(NTC):
                        u = g * NTC + tc
                        if u >= 2:
                            pg, ptc = divmod(u - 2, NTC)
                            s.wait_ge(p.sem["MM"], mm_of[(pg, ptc)])
                        sem = "LD" + str(u % 2)
                        p.inc(s.dma_start(
                            out=rb[u % 2][:].rearrange("p (k t) -> p k t", k=KC),
                            in_=rhs[:, :, tc * TC:(tc + 1) * TC].rearrange("k p t -> p k t")), sem, 16)
                        ld_rhs[(g, tc)] = (sem, p.c[sem])
                        for j in range(hi - lo):
                            s.wait_ge(p.sem["EV"], ev_unit[i])
                            ssem = "ST" + str(i % 4)
                            p.inc(s.dma_start(out=hdst[lo + j, :, tc * TC:(tc + 1) * TC],
                                              in_=ot[i % 4][:]), ssem, 16)
                            st_done[i] = (ssem, p.c[ssem])
                            i += 1

            @block.vector
            def _(v):
                p.bwait(v, s0, skip=("EV",))
                si = 0
                ui = 0
                for g, (lo, hi) in enumerate(groups):
                    wga, wgb = wgas[g % 2], wgbs[g % 2]
                    if wkind == "lut":
                        if g >= 2:
                            v.wait_ge(p.sem["MM"], last_of_g[g - 2])
                        for wi, wbuf in enumerate((wga, wgb)):
                            for j in range(hi - lo):
                                sm, val = slab_ld[si]
                                v.wait_ge(p.sem[sm], val)
                                for vv in range(16):
                                    dstv = acf[:] if vv == 0 else tmf[:]
                                    v.tensor_scalar(out=dstv, in0=ibuf[si % 2][:], scalar1=float(vv),
                                                    scalar2=lutb_s[:, vv:vv + 1],
                                                    op0=ALU.is_equal, op1=ALU.mult)
                                    if vv > 0:
                                        v.tensor_tensor(out=acf[:], in0=acf[:], in1=tmf[:], op=ALU.add)
                                p.inc(v.tensor_copy(
                                    out=wbuf[:, 0:KC * (hi - lo) * 128]
                                    .rearrange("p (k m) -> p k m", k=KC)[:, :, j * 128:(j + 1) * 128],
                                    in_=acf[:].rearrange("p (k m) -> p k m", k=KC)), "EV")
                                si += 1
                    else:
                        v.wait_ge(p.sem["LGW"], lgw0 + 32 * (g + 1))
                        if g >= 2:
                            v.wait_ge(p.sem["MM"], last_of_g[g - 2])
                        p.inc(v.tensor_scalar_add(wga[:, 0:KC * (hi - lo) * 128],
                                                  wga[:, 0:KC * (hi - lo) * 128], -128.0), "EV")
                        p.inc(v.tensor_scalar_add(wgb[:, 0:KC * (hi - lo) * 128],
                                                  wgb[:, 0:KC * (hi - lo) * 128], -128.0), "EV")
                    for tc in range(NTC):
                        for j in range(hi - lo):
                            m = lo + j
                            v.wait_ge(p.sem["MM"], mm0 + ui + 1)
                            v.wait_ge(p.sem["ES"], es0 + ui + 1)
                            if scu is not None:
                                p.vsync(v, v.tensor_scalar_mul(ut[:], ps[(2 * ui + 1) % 4][:, 0:TC],
                                                               scu[:, m:m + 1]))
                            else:
                                p.vsync(v, v.tensor_copy(out=ut[:], in_=ps[(2 * ui + 1) % 4][:, 0:TC]))
                            if ui >= 4:
                                sm, val = st_done[ui - 4]
                                v.wait_ge(p.sem[sm], val)
                            p.inc(v.tensor_tensor(out=ot[ui % 4][:], in0=gt[ui % 2][:], in1=ut[:],
                                                  op=ALU.mult), "EV")
                            ui += 1

            @block.tensor
            def _(t):
                p.bwait(t, s0, skip=("MM",))
                i = 0
                for g, (lo, hi) in enumerate(groups):
                    wga, wgb = wgas[g % 2], wgbs[g % 2]
                    if wkind != "lut":
                        t.wait_ge(p.sem["LGW"], lgw0 + 32 * (g + 1))
                    t.wait_ge(p.sem["EV"], nw_ev[g])
                    for tc in range(NTC):
                        u = g * NTC + tc
                        sm, val = ld_rhs[(g, tc)]
                        t.wait_ge(p.sem[sm], val)
                        for j in range(hi - lo):
                            if i >= 2:
                                t.wait_ge(p.sem["ES"], es0 + (i - 2) + 1)
                                t.wait_ge(p.sem["EV"], ev_unit[i - 2])
                            for k in range(KC):
                                t.matmul(ps[(2 * i) % 4][:, 0:TC],
                                         wga[:, (k * (hi - lo) + j) * 128:(k * (hi - lo) + j + 1) * 128],
                                         rb[u % 2][:, k * TC:(k + 1) * TC],
                                         start=(k == 0), stop=(k == KC - 1))
                            for k in range(KC):
                                mi = t.matmul(ps[(2 * i + 1) % 4][:, 0:TC],
                                              wgb[:, (k * (hi - lo) + j) * 128:(k * (hi - lo) + j + 1) * 128],
                                              rb[u % 2][:, k * TC:(k + 1) * TC],
                                              start=(k == 0), stop=(k == KC - 1))
                            p.inc(mi, "MM")
                            i += 1

            @block.scalar
            def _(sc):
                p.bwait(sc, s0, skip=("ES",))
                for i, (g, tc, j, m) in enumerate(units):
                    sc.wait_ge(p.sem["MM"], mm0 + i + 1)
                    if i >= 2:
                        sc.wait_ge(p.sem["EV"], ev_unit[i - 2])
                    if scg is not None:
                        ii = sc.activation(out=gt[i % 2][:], in_=ps[(2 * i) % 4][:, 0:TC],
                                           func=AF.Silu, scale=scg[:, m:m + 1])
                    else:
                        ii = sc.activation(out=gt[i % 2][:], in_=ps[(2 * i) % 4][:, 0:TC], func=AF.Silu)
                    p.inc(ii, "ES")

        def p_vtm():
            s0 = p.snap()
            mm0, es0 = s0["MM"], s0["ES"]
            wv = p.sb("vtw", [128, KD * 2048], BF16)
            rb = [p.sb("vtr", [128, KD * TC], BF16) for _ in range(2)]
            ot = [p.sb("vto", [128, 512], BF16) for _ in range(4)]
            lgw0 = s0["LGW"]
            ld_rhs, st_done = {}, {}
            TSC = TC // 128
            units = []
            for tc in range(NTC):
                for tsc in range(TSC):
                    for nb in range(4):
                        units.append((tc, tsc, nb))

            @block.gpsimd
            def _(g_):
                p.bwait(g_, s0, skip=("LGW",))
                p.inc(g_.dma_start(
                    out=wv[:].rearrange("p (k m) -> p k m", k=KD),
                    in_=w_qkv.rearrange("(k q) m -> q k m", q=128)[:, :, 4096:6144]), "LGW", 16)

            @block.sync
            def _(s):
                p.bwait(s, s0, skip=("LD0", "LD1", "ST0", "ST1", "ST2", "ST3"))
                i = 0
                for tc in range(NTC):
                    if tc >= 2:
                        s.wait_ge(p.sem["MM"], mm0 + (tc - 1) * TSC * 4)
                    sem = "LD" + str(tc % 2)
                    p.inc(s.dma_start(
                        out=rb[tc % 2][:].rearrange("p (k t) -> p k t", k=KD),
                        in_=lnx[:, :, tc * TC:(tc + 1) * TC].rearrange("k p t -> p k t")), sem, 16)
                    ld_rhs[tc] = (sem, p.c[sem])
                    for tsc in range(TSC):
                        for nb in range(4):
                            s.wait_ge(p.sem["ES"], es0 + i + 1)
                            ssem = "ST" + str(i % 4)
                            kb = tc * TSC + tsc
                            for j in range(4):
                                head = nb * 4 + j
                                p.inc(s.dma_start(out=vhm[head, kb],
                                                  in_=ot[i % 4][:, j * 128:(j + 1) * 128]), ssem, 16)
                            st_done[i] = (ssem, p.c[ssem])
                            i += 1

            @block.tensor
            def _(t):
                p.bwait(t, s0, skip=("MM",))
                t.wait_ge(p.sem["LGW"], lgw0 + 16)
                i = 0
                for tc in range(NTC):
                    sm, val = ld_rhs[tc]
                    t.wait_ge(p.sem[sm], val)
                    for tsc in range(TSC):
                        for nb in range(4):
                            if i >= 4:
                                t.wait_ge(p.sem["ES"], es0 + (i - 4) + 1)
                            for k in range(KD):
                                mi = t.matmul(
                                    ps[i % 4][:, 0:512],
                                    rb[tc % 2][:, k * TC + tsc * 128:k * TC + tsc * 128 + 128],
                                    wv[:, k * 2048 + nb * 512:k * 2048 + (nb + 1) * 512],
                                    start=(k == 0), stop=(k == KD - 1))
                            p.inc(mi, "MM")
                            i += 1

            @block.scalar
            def _(sc):
                p.bwait(sc, s0, skip=("ES",))
                for i in range(len(units)):
                    sc.wait_ge(p.sem["MM"], mm0 + i + 1)
                    if i >= 4:
                        sm, val = st_done[i - 4]
                        sc.wait_ge(p.sem[sm], val)
                    p.inc(sc.copy(out=ot[i % 4][:], in_=ps[i % 4][:, 0:512]), "ES")

        def p_attn():
            s0 = p.snap()
            NQB = S // 512
            adbg = False
            qh = p.sb("atq", [128, S], BF16)
            kh = p.sb("atk", [128, S], BF16)
            vh = p.sb("atv", [128, (S // 128) * 128], BF16)
            Ssb2 = [p.sb("atS", [128, S], F32) for _ in range(2)]
            Psb2 = [p.sb("atP", [128, S], BF16) for _ in range(2)]
            Pt = p.sb("atPt", [128, (S // 128) * 512], BF16)
            mx = p.sb("atmx", [128, 1], F32)
            nb2 = [p.sb("atnb", [128, 1], F32) for _ in range(2)]
            lsum = [p.sb("atl", [128, 1], F32) for _ in range(2)]
            lsum2 = [p.sb("atl2", [128, 1], F32) for _ in range(2)]
            linv = p.sb("atli", [128, 1], F32)
            ot = [p.sb("atot", [128, 512], BF16) for _ in range(4)]

            marks = {}
            cn = {"MM": s0["MM"], "ES": s0["ES"], "EV": s0["EV"], "EG": s0["EG"]}

            def walk(record):
                for h in range(NHEAD):
                    record("loads", h)
                    for qb in range(NQB):
                        record("ptzero", (h, qb))
                        for qc4 in range(4):
                            qci = qb * 4 + qc4
                            for kb5 in range(qb + 1):
                                record("score", (h, qci, kb5))
                                record("scev", (h, qci, kb5))
                            record("mask", (h, qci))
                            record("rmax", (h, qci))
                            record("exp", (h, qci))
                            record("recip", (h, qci))
                            record("pscale", (h, qci))
                            ntr = qci + 1
                            for kg in range((ntr + 3) // 4):
                                record("trans", (h, qci, kg))
                                record("trev", (h, qci, kg))
                        record("av", (h, qb))
                        record("avev", (h, qb))
                        record("store", (h, qb))

            incs = {"loads": None, "store": None, "score": "MM", "scev": "ES", "mask": "EV",
                    "rmax": "EV", "exp": "ES", "recip": "EV", "pscale": "EV", "trans": "MM",
                    "trev": "ES", "av": "MM", "avev": "ES", "ptzero": "EV"}

            def rec(kind, key):
                sm = incs[kind]
                if sm is not None:
                    cn[sm] += 1
                marks[(kind, key)] = dict(cn)
            walk(rec)
            tot = dict(cn)
            ld_done, st_done, dbg_st = {}, {}, {}

            @block.sync
            def _(s):
                p.bwait(s, s0, skip=("LD0", "LD1", "ST0", "ST1", "ST2", "ST3"))
                for h in range(NHEAD):
                    if h >= 1:
                        s.wait_ge(p.sem["MM"], marks[("av", (h - 1, NQB - 1))]["MM"])
                    sem = "LD" + str(h % 2)
                    p.inc(s.dma_start(out=qh[:], in_=qf[h]), sem, 16)
                    p.inc(s.dma_start(out=kh[:], in_=kfd[h]), sem, 16)
                    p.inc(s.dma_start(out=vh[:].rearrange("p (kb d) -> p kb d", d=128),
                                      in_=vhm[h].rearrange("kb p d -> p kb d")), sem, 16)
                    ld_done[h] = (sem, p.c[sem])
                    for qb in range(NQB):
                        s.wait_ge(p.sem["ES"], marks[("avev", (h, qb))]["ES"])
                        ssem = "ST" + str((h * NQB + qb) % 4)
                        p.inc(s.dma_start(out=ao[h, :, qb * 512:(qb + 1) * 512],
                                          in_=ot[(h * NQB + qb) % 4][:]), ssem, 16)
                        st_done[(h, qb)] = (ssem, p.c[ssem])

            @block.tensor
            def _(t):
                p.bwait(t, s0, skip=("MM",))
                for h in range(NHEAD):
                    sm, val = ld_done[h]
                    t.wait_ge(p.sem[sm], val)
                    for qb in range(NQB):
                        for qc4 in range(4):
                            qci = qb * 4 + qc4
                            for kb5 in range(qb + 1):
                                nw = 512 if kb5 < qb else (qc4 + 1) * 128
                                if (h, qci, kb5) != (0, 0, 0):
                                    t.wait_ge(p.sem["ES"], max(s0["ES"],
                                              marks[("scev", (h, qci, kb5))]["ES"] - 2))
                                p.inc(t.matmul(ps[(kb5 + qci) % 2][:, 0:nw],
                                               qh[:, qci * 128:(qci + 1) * 128],
                                               kh[:, kb5 * 512:kb5 * 512 + nw],
                                               start=True, stop=True), "MM")
                            t.wait_ge(p.sem["EV"], marks[("pscale", (h, qci))]["EV"])
                            ntr = qci + 1
                            for kg in range((ntr + 3) // 4):
                                wdt = min(4, ntr - kg * 4)
                                t.wait_ge(p.sem["ES"], max(s0["ES"],
                                          marks[("trev", (h, qci, kg))]["ES"] - 2))
                                for jj in range(wdt):
                                    kb = kg * 4 + jj
                                    mi = t.matmul(ps[4 + kg % 2][:, jj * 128:(jj + 1) * 128],
                                                  Psb2[qci % 2][:, kb * 128:(kb + 1) * 128], ident[:],
                                                  start=True, stop=True)
                                p.inc(mi, "MM")
                        t.wait_ge(p.sem["ES"], marks[("trev", (h, qb * 4 + 3,
                                  (qb * 4 + 3) // 4))]["ES"])
                        nkb = 4 * qb + 4
                        fqb = h * NQB + qb
                        if fqb >= 2:
                            pk = fqb - 2
                            t.wait_ge(p.sem["ES"], marks[("avev", (pk // NQB, pk % NQB))]["ES"])
                        for kb in range(nkb):
                            mi = t.matmul(ps[6 + fqb % 2][:, 0:512],
                                          vh[:, kb * 128:(kb + 1) * 128],
                                          Pt[:, kb * 512:(kb + 1) * 512],
                                          start=(kb == 0), stop=(kb == nkb - 1))
                        p.inc(mi, "MM")

            @block.scalar
            def _(sc):
                p.bwait(sc, s0, skip=("ES",))
                for h in range(NHEAD):
                    for qb in range(NQB):
                        for qc4 in range(4):
                            qci = qb * 4 + qc4
                            Ssb = Ssb2[qci % 2]
                            Psb = Psb2[qci % 2]
                            for kb5 in range(qb + 1):
                                nw = 512 if kb5 < qb else (qc4 + 1) * 128
                                sc.wait_ge(p.sem["MM"], marks[("score", (h, qci, kb5))]["MM"])
                                p.inc(sc.copy(out=Ssb[:, kb5 * 512:kb5 * 512 + nw],
                                              in_=ps[(kb5 + qci) % 2][:, 0:nw]), "ES")
                            sc.wait_ge(p.sem["EV"], marks[("rmax", (h, qci))]["EV"])
                            gq = h * 4 * NQB + qci
                            if gq >= 2:
                                pq = divmod(gq - 2, 4 * NQB)
                                pntr = pq[1] + 1
                                sc.wait_ge(p.sem["MM"],
                                           marks[("trans", (pq[0], pq[1], (pntr - 1) // 4))]["MM"])
                            ln_ = (qci + 1) * 128
                            sc.activation(out=Psb[:, 0:ln_], in_=Ssb[:, 0:ln_], func=AF.Exp,
                                          scale=1.0, bias=nb2[qci % 2][:, 0:1],
                                          accum_out=lsum[qci % 2][:, 0:1])
                            p.inc(sc.copy(out=lsum2[qci % 2][:, 0:1], in_=lsum[qci % 2][:, 0:1]), "ES")
                            ntr = qci + 1
                            for kg in range((ntr + 3) // 4):
                                wdt = min(4, ntr - kg * 4)
                                sc.wait_ge(p.sem["MM"], marks[("trans", (h, qci, kg))]["MM"])
                                p.inc(sc.copy(
                                    out=Pt[:, 0:(S // 128) * 512]
                                    .rearrange("p (kb q) -> p kb q", q=512)
                                    [:, kg * 4:kg * 4 + wdt, qc4 * 128:(qc4 + 1) * 128],
                                    in_=ps[4 + kg % 2][:, 0:wdt * 128]
                                    .rearrange("p (w q) -> p w q", q=128)), "ES")
                        sc.wait_ge(p.sem["MM"], marks[("av", (h, qb))]["MM"])
                        fqb = h * NQB + qb
                        i4 = fqb % 4
                        if fqb >= 4:
                            pk = fqb - 4
                            sm, val = st_done[(pk // NQB, pk % NQB)]
                            sc.wait_ge(p.sem[sm], val)
                        p.inc(sc.copy(out=ot[i4][:], in_=ps[6 + fqb % 2][:, 0:512]), "ES")

            @block.vector
            def _(v):
                p.bwait(v, s0, skip=("EV",))
                for h in range(NHEAD):
                    for qb in range(NQB):
                        if (h, qb) != (0, 0):
                            pqb = qb - 1 if qb >= 1 else NQB - 1
                            ph = h if qb >= 1 else h - 1
                            v.wait_ge(p.sem["MM"], marks[("av", (ph, pqb))]["MM"])
                        p.inc(v.memset(Pt[:, qb * 4 * 512:(qb + 1) * 4 * 512], 0.0), "EV")
                        for qc4 in range(4):
                            qci = qb * 4 + qc4
                            Ssb = Ssb2[qci % 2]
                            Psb = Psb2[qci % 2]
                            v.wait_ge(p.sem["ES"], marks[("scev", (h, qci, qb))]["ES"])
                            p.inc(v.tensor_tensor(out=Ssb[:, qci * 128:(qci + 1) * 128],
                                                  in0=Ssb[:, qci * 128:(qci + 1) * 128],
                                                  in1=maskt[:], op=ALU.add), "EV")
                            ln_ = (qci + 1) * 128
                            p.inc(v.reduce_max(out=nb2[qci % 2][:, 0:1], in_=Ssb[:, 0:ln_],
                                               axis=mybir.AxisListType.X, negate=True), "EV")
                            v.wait_ge(p.sem["ES"], marks[("exp", (h, qci))]["ES"])
                            p.inc(v.reciprocal(out=linv[:, 0:1], in_=lsum2[qci % 2][:, 0:1]), "EV")
                            v.wait_ge(p.sem["EV"], marks[("recip", (h, qci))]["EV"])
                            p.inc(v.tensor_scalar_mul(Psb[:, 0:ln_], Psb[:, 0:ln_],
                                                      linv[:, 0:1]), "EV")

        def p_conv(xraw):
            s0 = p.snap()
            es0, ev0 = s0["ES"], s0["EV"]
            cv = [p.sb("cvb", [128, TC + 3], BF16) for _ in range(2)]
            t1 = [p.sb("cvt", [128, TC], F32) for _ in range(2)]
            ot = [p.sb("cvo", [128, TC], BF16) for _ in range(4)]
            ld_done, st_done = {}, {}
            NU = 32 * NTC

            @block.sync
            def _(s):
                p.bwait(s, s0, skip=("LD0", "LD1", "ST0", "ST1", "ST2", "ST3"))
                for i in range(NU):
                    cb, tc = divmod(i, NTC)
                    if i >= 2:
                        s.wait_ge(p.sem["ES"], es0 + (i - 2) + 1)
                    sem = "LD" + str(i % 2)
                    if tc == 0:
                        p.inc(s.dma_start(out=cv[i % 2][:, 3:3 + TC],
                                          in_=xraw[cb, :, 0:TC]), sem, 16)
                    else:
                        p.inc(s.dma_start(out=cv[i % 2][:, 0:TC + 3],
                                          in_=xraw[cb, :, tc * TC - 3:(tc + 1) * TC]), sem, 16)
                    ld_done[i] = (sem, p.c[sem])
                    s.wait_ge(p.sem["ES"], es0 + i + 1)
                    ssem = "ST" + str(i % 4)
                    p.inc(s.dma_start(out=xssm[cb, :, tc * TC:(tc + 1) * TC], in_=ot[i % 4][:]), ssem, 16)
                    st_done[i] = (ssem, p.c[ssem])

            @block.vector
            def _(v):
                p.bwait(v, s0, skip=("EV",))
                for i in range(NU):
                    cb, tc = divmod(i, NTC)
                    if tc == 0:
                        if i >= 2:
                            v.wait_ge(p.sem["ES"], es0 + (i - 2) + 1)
                        v.memset(cv[i % 2][:, 0:3], 0.0)
                    sm, val = ld_done[i]
                    v.wait_ge(p.sem[sm], val)
                    c = cv[i % 2]
                    v.tensor_scalar_mul(t1[i % 2][:], c[:, 3:3 + TC], convw_s[:, cb * 4 + 3:cb * 4 + 4])
                    for j in (2, 1, 0):
                        ii = v.scalar_tensor_tensor(out=t1[i % 2][:], in0=c[:, j:j + TC],
                                                    scalar=convw_s[:, cb * 4 + j:cb * 4 + j + 1],
                                                    in1=t1[i % 2][:], op0=ALU.mult, op1=ALU.add)
                    p.inc(ii, "EV")

            @block.scalar
            def _(sc):
                p.bwait(sc, s0, skip=("ES",))
                for i in range(NU):
                    cb, tc = divmod(i, NTC)
                    sc.wait_ge(p.sem["EV"], ev0 + i + 1)
                    if i >= 4:
                        sm, val = st_done[i - 4]
                        sc.wait_ge(p.sem[sm], val)
                    p.inc(sc.activation(out=ot[i % 4][:], in_=t1[i % 2][:], func=AF.Silu,
                                        bias=convb_s[:, cb:cb + 1]), "ES")

        def p_scan():
            s0 = p.snap()
            es0, ev0, mm0, eg0 = s0["ES"], s0["EV"], s0["MM"], s0["EG"]
            xsb = p.sb("scxs", [128, KD * TC], BF16)
            gsb = p.sb("scgs", [128, KD * TC], BF16)
            dtin = p.sb("scdt", [128, TC], BF16)
            bcb = p.sb("scbc", [32, TC], BF16)
            bcbuf = p.sb("scbb", [128, 32 * TC], BF16)
            dtf = [p.sb("scdf", [128, TC], F32) for _ in range(2)]
            spr = p.sb("scsp", [128, TC], F32)
            spt = p.sb("scspt", [128, TC], F32)
            dab = [p.sb("scda", [128, (NST // 2) * TC], F32) for _ in range(2)]
            uf = p.sb("scu", [128, TC], F32)
            uf2 = p.sb("scu2", [128, TC], F32)
            dbu = p.sb("scdb", [128, NST * TC], BF16)
            stb = [p.sb("scst", [128, NST * TC], BF16) for _ in range(2)]
            ytm = p.sb("scyt", [128, TC], BF16)
            gty = p.sb("scgy", [128, TC], BF16)
            yacv = [p.sb("scya", [128, TC], F32) for _ in range(2)]
            yag = p.sb("scyag", [128, TC], F32)
            ygb = [p.sb("scyg", [128, TC], BF16) for _ in range(2)]
            VSET = list(range(6))
            GSET = [n for n in range(NST) if n not in VSET]
            ld_done, st_done = {}, {}
            NU = NTC * NH
            r_mm = 2 + 32 + 16
            r_es = 2 + 32 + 48
            r_ev = 48

            def mm_at(i, stage, n=0):
                base = mm0 + i * r_mm
                if stage == "xp":
                    return base + 2
                if stage == "bc":
                    return base + 2 + n + 1
                return base + 2 + 32 + n + 1

            def es_at(i, stage, n=0):
                base = es0 + i * r_es
                if stage == "io":
                    return base + 2
                if stage == "bc":
                    return base + 2 + n + 1
                if stage == "db_a":
                    return base + 2 + 32 + 3 * n + 1
                if stage == "db_h0":
                    return base + 2 + 32 + 3 * n + 2
                return base + 2 + 32 + 3 * n + 3

            def ev_at(i, db, stage="end"):
                base = ev0 + i * r_ev + 3 * db
                return base + {"dtf": 1, "h0": 2, "end": 3}[stage]

            def eg_at(i, db):
                return eg0 + i * 16 + db + 1

            @block.sync
            def _(s):
                p.bwait(s, s0, skip=("LD0", "LD1", "ST0", "ST1", "ST2", "ST3"))
                for i in range(NU):
                    tc, h = divmod(i, NH)
                    if i >= 1:
                        s.wait_ge(p.sem["EG"], eg_at(i - 1, 15))
                    sem = "LD" + str(i % 2)
                    p.inc(s.dma_start(out=xsb[:].rearrange("p (c t) -> p c t", c=KD),
                                      in_=xssm[h * KD:(h + 1) * KD, :, tc * TC:(tc + 1) * TC]
                                      .rearrange("c p t -> p c t")), sem, 16)
                    p.inc(s.dma_start(out=gsb[:].rearrange("p (c t) -> p c t", c=KD),
                                      in_=gsil[h * KD:(h + 1) * KD, :, tc * TC:(tc + 1) * TC]
                                      .rearrange("c p t -> p c t")), sem, 16)
                    ld_done[i] = (sem, p.c[sem])
                    for db in range(16):
                        s.wait_ge(p.sem["EG"], eg_at(i, db))
                        ssem = "ST" + str(db % 2)
                        p.inc(s.dma_start(out=ygat[h * 16 + db, :, tc * TC:(tc + 1) * TC],
                                          in_=ygb[db % 2][:]), ssem, 16)
                        st_done[(i, db)] = (ssem, p.c[ssem])

            @block.tensor
            def _(t):
                p.bwait(t, s0, skip=("MM",))
                for i in range(NU):
                    tc, h = divmod(i, NH)
                    sm, val = ld_done[i]
                    t.wait_ge(p.sem[sm], val)
                    if i >= 1:
                        t.wait_ge(p.sem["ES"], es_at(i - 1, "db_a", 15))
                    for k in range(KD):
                        mi = t.matmul(ps[2][:, 0:TC],
                                      w_xp_s[:, ((h * KD + k) * 160):((h * KD + k) * 160 + 128)],
                                      xsb[:, k * TC:(k + 1) * TC], start=(k == 0), stop=(k == KD - 1))
                    p.inc(mi, "MM")
                    for k in range(KD):
                        mi = t.matmul(ps[3][0:32, 0:TC],
                                      w_xp_s[:, ((h * KD + k) * 160 + 128):((h * KD + k) * 160 + 160)],
                                      xsb[:, k * TC:(k + 1) * TC], start=(k == 0), stop=(k == KD - 1))
                    p.inc(mi, "MM")
                    t.wait_ge(p.sem["ES"], es_at(i, "io"))
                    for n in range(32):
                        if n >= 2:
                            t.wait_ge(p.sem["ES"], es_at(i, "bc", n - 2))
                        p.inc(t.matmul(ps[4 + n % 2][:, 0:TC], sel32[:, n * 128:(n + 1) * 128],
                                       bcb[:], start=True, stop=True), "MM")
                    for db in range(16):
                        if db >= 2:
                            t.wait_ge(p.sem["ES"], es_at(i, "db_a", db - 2))
                        p.inc(t.matmul(ps[db % 2][:, 0:TC],
                                       w_dt_s[:, (h * 16 + db) * 128:(h * 16 + db + 1) * 128],
                                       dtin[:], start=True, stop=True), "MM")

            @block.scalar
            def _(sc):
                p.bwait(sc, s0, skip=("ES",))
                for i in range(NU):
                    tc, h = divmod(i, NH)
                    sc.wait_ge(p.sem["MM"], mm_at(i, "xp"))
                    p.inc(sc.copy(out=dtin[:], in_=ps[2][:, 0:TC]), "ES")
                    p.inc(sc.copy(out=bcb[:], in_=ps[3][0:32, 0:TC]), "ES")
                    for n in range(32):
                        sc.wait_ge(p.sem["MM"], mm_at(i, "bc", n))
                        if i >= 1 and n == 0:
                            sc.wait_ge(p.sem["EG"], eg_at(i - 1, 15))
                        p.inc(sc.copy(out=bcbuf[:, n * TC:(n + 1) * TC], in_=ps[4 + n % 2][:, 0:TC]), "ES")
                    for db in range(16):
                        sc.wait_ge(p.sem["MM"], mm_at(i, "dt", db))
                        bb = dtb_s[:, h * 16 + db:h * 16 + db + 1]
                        sc.activation(out=spr[:], in_=ps[db % 2][:, 0:TC], func=AF.Relu, bias=bb)
                        sc.activation(out=spt[:], in_=ps[db % 2][:, 0:TC], func=AF.Abs, bias=bb)
                        sc.activation(out=spt[:], in_=spt[:], func=AF.Exp, scale=-1.0)
                        p.inc(sc.activation(out=spt[:], in_=spt[:], func=AF.Ln, bias=1.0), "ES")
                        sc.wait_ge(p.sem["EV"], ev_at(i, db, "dtf"))
                        for half in range(2):
                            hidx = (i * 16 + db) * 2 + half
                            if hidx >= 2:
                                ph = hidx - 2
                                pi2, r2 = divmod(ph, 32)
                                pdb, phalf = divmod(r2, 2)
                                sc.wait_ge(p.sem["EV"],
                                           ev_at(pi2, pdb, "h0" if phalf == 0 else "end"))
                            for nn in range(NST // 2):
                                n = half * (NST // 2) + nn
                                ii = sc.activation(
                                    out=dab[hidx % 2][:, nn * TC:(nn + 1) * TC], in_=dtf[db % 2][:],
                                    func=AF.Exp,
                                    scale=a_s[:, (h * 16 + db) * NST + n:(h * 16 + db) * NST + n + 1])
                            p.inc(ii, "ES")

            @block.vector
            def _(v):
                p.bwait(v, s0, skip=("EV",))
                for i in range(NU):
                    tc, h = divmod(i, NH)
                    for db in range(16):
                        v.wait_ge(p.sem["ES"], es_at(i, "db_a", db))
                        p.inc(v.tensor_tensor(out=dtf[db % 2][:], in0=spr[:], in1=spt[:], op=ALU.add), "EV")
                        gi = i * 16 + db - 2
                        if gi >= 0:
                            v.wait_ge(p.sem["EG"], eg0 + gi + 1)
                        v.tensor_tensor(out=uf[:], in0=dtf[db % 2][:], in1=xsb[:, db * TC:(db + 1) * TC],
                                        op=ALU.mult)
                        v.tensor_tensor(out=dbu[:].rearrange("p (n t) -> p n t", n=NST),
                                        in0=uf[:].unsqueeze(1).broadcast_to((128, NST, TC)),
                                        in1=bcbuf[:, 0:NST * TC].rearrange("p (n t) -> p n t", n=NST),
                                        op=ALU.mult)
                        cbase = (h * 16 + db) * NST
                        hidx = (i * 16 + db) * 2
                        v.wait_ge(p.sem["ES"], es_at(i, "db_h0", db))
                        for n in range(NST // 2):
                            v.tensor_tensor_scan(
                                out=stb[db % 2][:, n * TC:(n + 1) * TC],
                                data0=dab[hidx % 2][:, n * TC:(n + 1) * TC],
                                data1=dbu[:, n * TC:(n + 1) * TC],
                                initial=carry[:, cbase + n:cbase + n + 1],
                                op0=ALU.mult, op1=ALU.add)
                        p.inc(v.tensor_copy(out=ytm[:, 0:1], in_=stb[db % 2][:, TC - 1:TC]), "EV")
                        v.wait_ge(p.sem["ES"], es_at(i, "db_h1", db))
                        for nn in range(NST // 2):
                            n = NST // 2 + nn
                            v.tensor_tensor_scan(
                                out=stb[db % 2][:, n * TC:(n + 1) * TC],
                                data0=dab[(hidx + 1) % 2][:, nn * TC:(nn + 1) * TC],
                                data1=dbu[:, n * TC:(n + 1) * TC],
                                initial=carry[:, cbase + n:cbase + n + 1],
                                op0=ALU.mult, op1=ALU.add)
                        p.vsync(v, v.tensor_copy(
                            out=carry[:, cbase:cbase + NST],
                            in_=stb[db % 2][:].rearrange("p (n t) -> p n t", n=NST)[:, :, TC - 1:TC]
                            .rearrange("p n t -> p (n t)")))
                        first_v = True
                        for n in VSET:
                            v.tensor_tensor(out=ytm[:], in0=stb[db % 2][:, n * TC:(n + 1) * TC],
                                            in1=bcbuf[:, (16 + n) * TC:(17 + n) * TC], op=ALU.mult)
                            if first_v:
                                v.tensor_copy(out=yacv[db % 2][:], in_=ytm[:])
                                first_v = False
                            else:
                                v.tensor_tensor(out=yacv[db % 2][:], in0=yacv[db % 2][:],
                                                in1=ytm[:], op=ALU.add)
                        p.inc(v.tensor_copy(out=uf2[:, 0:1], in_=yacv[db % 2][:, 0:1]), "EV")

            @block.gpsimd
            def _(g_):
                p.bwait(g_, s0, skip=("EG",))
                for i in range(NU):
                    tc, h = divmod(i, NH)
                    for db in range(16):
                        g_.wait_ge(p.sem["EV"], ev_at(i, db))
                        gdb = i * 16 + db
                        if gdb >= 2:
                            smv, val = st_done[((gdb - 2) // 16, (gdb - 2) % 16)]
                            g_.wait_ge(p.sem[smv], val)
                        first_g = True
                        for n in GSET:
                            g_.tensor_tensor(out=gty[:], in0=stb[db % 2][:, n * TC:(n + 1) * TC],
                                             in1=bcbuf[:, (16 + n) * TC:(17 + n) * TC], op=ALU.mult)
                            if first_g:
                                g_.tensor_copy(out=yag[:], in_=gty[:])
                                first_g = False
                            else:
                                g_.tensor_tensor(out=yag[:], in0=yag[:], in1=gty[:], op=ALU.add)
                        g_.tensor_tensor(out=yag[:], in0=yag[:], in1=yacv[db % 2][:], op=ALU.add)
                        g_.tensor_scalar(out=uf2[:],
                                         in0=xsb[:, db * TC:(db + 1) * TC],
                                         scalar1=d_s[:, h * 16 + db:h * 16 + db + 1],
                                         scalar2=0.0, op0=ALU.mult, op1=ALU.add)
                        g_.tensor_tensor(out=yag[:], in0=yag[:], in1=uf2[:], op=ALU.add)
                        p.inc(g_.tensor_tensor(out=ygb[db % 2][:], in0=yag[:],
                                               in1=gsb[:, db * TC:(db + 1) * TC], op=ALU.mult), "EG")

        xraw = p.dram("xraw", [32, 128, S], BF16)

        def ev_copy(m, pa, oa, sc):
            return sc.copy(out=oa, in_=pa)

        def ev_silu(m, pa, oa, sc):
            return sc.activation(out=oa, in_=pa, func=AF.Silu)

        def ev_scale_d(m, pa, oa, sc):
            return sc.activation(out=oa, in_=pa, func=AF.Identity, scale=scd_s[:, m:m + 1])

        def ev_scale_isq(m, pa, oa, sc):
            return sc.activation(out=oa, in_=pa, func=AF.Identity, scale=ISQ)

        g8 = [(k * 6, min((k + 1) * 6, 64)) for k in range(11)]
        g43 = [(k * 6, min((k + 1) * 6, 43)) for k in range(8)]
        gd16 = [(k * 3, min((k + 1) * 3, 16)) for k in range(6)]

        seq = [
            ("init", p_init),
            ("ln1", lambda: p_lnlike(xin, True, "pre", write_xres=xresA)),
            ("inpx", lambda: p_gemm(w_inx, "f32", D, [(0, 16), (16, 32)], lnx, ev_copy,
                                    lambda m, tc: xraw[m, :, tc * TC:(tc + 1) * TC])),
            ("inpg", lambda: p_gemm(w_ing, "f32", D, [(0, 16), (16, 32)], lnx, ev_silu,
                                    lambda m, tc: gsil[m, :, tc * TC:(tc + 1) * TC])),
            ("conv", lambda: p_conv(xraw)),
            ("scan", p_scan),
            ("outp", lambda: p_gemm(w_out, "f32", IMIX, [(0, 8), (8, 16)], ygat, ev_copy,
                                    lambda m, tc: z32[m, :, tc * TC:(tc + 1) * TC], out_f32=True)),
            ("pln1", lambda: p_lnlike(z32, True, "post", xold=xresA, dst=xresB)),
            ("ln2", lambda: p_lnlike(xresB, False, "pre")),
            ("ffgu", lambda: p_gemm_dual(w_fg, w_fu, "i32", D, g8, lnx, h1,
                                         scg=scg_s, scu=scu_s)),
            ("ffdn", lambda: p_gemm(w_fd, "i32", IFFN, gd16,
                                    h1, ev_scale_d,
                                    lambda m, tc: z32[m, :, tc * TC:(tc + 1) * TC],
                                    out_f32=True, nrb=1)),
            ("pln2", lambda: p_lnlike(z32, True, "post", xold=xresB, dst=xresA)),
            ("ln3", lambda: p_lnlike(xresA, False, "pre")),
            ("qp", lambda: p_gemm(w_qkv, "f32", D, [(0, 16)], lnx, ev_scale_isq,
                                  lambda m, tc: qf[m, :, tc * TC:(tc + 1) * TC])),
            ("kp", lambda: p_gemm(w_qkv, "f32", D, [(16, 32)], lnx, ev_copy,
                                  lambda m, tc: kfd[m - 16, :, tc * TC:(tc + 1) * TC])),
            ("vp", p_vtm),
            ("attn", p_attn),
            ("proj", lambda: p_gemm(w_pr, "f32", D, [(0, 8), (8, 16)], ao, ev_copy,
                                    lambda m, tc: z32[m, :, tc * TC:(tc + 1) * TC], out_f32=True)),
            ("pln3", lambda: p_lnlike(z32, True, "post", xold=xresA, dst=xresB)),
            ("ln4", lambda: p_lnlike(xresB, False, "pre")),
            ("f1gu", lambda: p_gemm_dual(i_g, i_u, "lut", D, g43, lnx, h2)),
            ("f1dn", lambda: p_gemm(i_d, "lut", IFF1, gd16,
                                    h2, ev_copy,
                                    lambda m, tc: z32[m, :, tc * TC:(tc + 1) * TC],
                                    out_f32=True, nrb=1)),
            ("pln4", lambda: p_lnlike(z32, True, "post", xold=xresB, dst=out)),
        ]
        stop = getattr(self, "stop_after", None)
        self.phase_counts = {}
        for n, fn in seq:
            with contextlib.ExitStack() as ph_es:
                self.ph_es = ph_es
                fn()
                self.ph_es = None
            self.phase_counts[n] = dict(self.c)
            if n == stop:
                break

        fin = self.snap()

        @block.sync
        def _(s):
            self.bwait(s, fin)

        @block.gpsimd
        def _(g):
            self.bwait(g, fin)
        self.tensors = dict(
            xin=xin, lnx=lnx, xresA=xresA, xresB=xresB, out=out, z32=z32,
            xssm=xssm, gsil=gsil, ygat=ygat, h1=h1, qf=qf, kfd=kfd, vhm=vhm,
            ao=ao, h2=h2)
        return nc


def _host_prep(inputs):
    c = np.ascontiguousarray
    ipw = inputs["L0_attn_in_proj_w"]
    m = {}
    m["w_inx"] = c(ipw[0::2].T)
    m["w_ing"] = c(ipw[1::2].T)
    m["convw"] = c(inputs["L0_attn_conv_w"].reshape(32, 128, 4).transpose(1, 0, 2)).reshape(128, 128)
    m["convb"] = c(inputs["L0_attn_conv_b"].reshape(32, 128).T)
    m["w_xp"] = c(inputs["L0_attn_x_proj_weight"].transpose(0, 2, 1)
                  .reshape(2, 16, 128, 160).transpose(2, 0, 1, 3)).reshape(128, 2 * 16 * 160)
    m["w_dt"] = c(inputs["L0_attn_dt_proj_weight"].transpose(0, 2, 1)
                  .transpose(1, 0, 2)).reshape(128, 2 * 2048)
    m["dtbi"] = c(inputs["L0_attn_dt_proj_bias"].reshape(2, 16, 128).transpose(2, 0, 1)).reshape(128, 32)
    m["alog"] = c(inputs["L0_attn_A_log"].reshape(2, 16, 128, 16).transpose(2, 0, 1, 3)).reshape(128, 512)
    m["dpar"] = c(inputs["L0_attn_D"].reshape(2, 16, 128).transpose(2, 0, 1)).reshape(128, 32)
    m["w_out"] = c(inputs["L0_attn_out_proj_w"].T)
    m["w_fg"] = c(inputs["L0_ffn_gate_w"].T)
    m["w_fu"] = c(inputs["L0_ffn_up_w"].T)
    m["w_fd"] = c(inputs["L0_ffn_down_w"].T)
    m["sc_g"] = c(inputs["L0_ffn_gate_scale"].reshape(64, 128).T)
    m["sc_u"] = c(inputs["L0_ffn_up_scale"].reshape(64, 128).T)
    m["sc_d"] = c(inputs["L0_ffn_down_scale"].reshape(16, 128).T)
    m["w_qkv"] = c(inputs["L1_attn_Wqkv"].T)
    m["w_pr"] = c(inputs["L1_attn_Wproj"].T)
    m["lutv"] = c(inputs["L1_ffn_lut"].reshape(1, 16))
    m["i_g"] = c(inputs["L1_ffn_gate_idx"].T)
    m["i_u"] = c(inputs["L1_ffn_up_idx"].T)
    m["i_d"] = c(inputs["L1_ffn_down_idx"].T)
    m["identf"] = np.eye(128, dtype=np.float32)
    m["mask128"] = np.where(np.arange(128)[:, None] >= np.arange(128)[None, :],
                            0.0, -1.0e30).astype(np.float32)
    sel = np.zeros((32, 32, 128), dtype=np.float32)
    for n in range(32):
        sel[n, n, :] = 1.0
    m["self32"] = c(sel.transpose(1, 0, 2)).reshape(32, 32 * 128)
    return m


def kernel(**inputs):
    S = 4096
    x = inputs["L0_attn_hidden_states"]
    NB = x.shape[0]
    shared = _host_prep(inputs)
    pr = Prog(S=S, TC=512)
    nc = pr.build()
    in_maps = []
    for b in range(NB):
        im = dict(shared)
        im["xin"] = np.ascontiguousarray(x[b].T).reshape(KD, 128, S)
        in_maps.append(im)
    res = run_bass_kernel_spmd(nc, in_maps, core_ids=list(range(NB)))
    outs = [np.asarray(r["out"]).reshape(D, S).T for r in res.results]
    return np.stack(outs).astype(np.float32)
